# version 13
# baseline (speedup 1.0000x reference)
import numpy as np
import ml_dtypes
from contextlib import ExitStack
import concourse.bass as bass
import concourse.mybir as mybir
from concourse.bass_utils import run_bass_kernel_spmd

F32 = mybir.dt.float32
BF16 = mybir.dt.bfloat16
I32 = mybir.dt.int32
AF = mybir.ActivationFunctionType
ALU = mybir.AluOpType
AX = mybir.AxisListType

NCORES = 8
D = 1024
H = 16
HD = 64
C = 128
FF = 4096
EPS = 1e-5
ENGS = ("sync", "gpsimd", "tensor", "vector", "scalar")
ESEM = {"sync": ("ld", 16), "gpsimd": ("gq", 16), "tensor": ("te", 1),
        "vector": ("ve", 1), "scalar": ("ae", 1)}
ASYNC = ("ld", "xa", "xb", "wl", "ca", "cb", "sa", "sb", "out", "gq", "cc")
ALLSEMS = ("ld", "xa", "xb", "wl", "ca", "cb", "sa", "sb", "out", "gq", "cc", "te", "ve", "ae")


class Buf:
    __slots__ = ("w", "r", "name")

    def __init__(self, name=""):
        self.name = name
        self.w = None
        self.r = []


class Prog:
    def __init__(self, nc):
        self.nc = nc
        self.ops = {e: [] for e in ENGS}
        self.cnt = {s: 0 for s in ALLSEMS}
        self.wm = {e: {} for e in ENGS}
        self.bar = {s: 0 for s in ASYNC}
        self.handles = {}

    def _wait(self, eng, sem, val):
        if val <= 0:
            return
        if self.wm[eng].get(sem, 0) >= val:
            return
        self.wm[eng][sem] = val
        self.ops[eng].append(("wait", sem, val))

    def op(self, eng, fn, reads=(), writes=(), sem=None, step=None,
           no_self_wait=False, pbase=None):
        if sem is None:
            sem, step = ESEM[eng]
        if pbase is not None:
            if getattr(self, "_last_pbase", None) not in (None, pbase):
                self._wait(eng, "te", self.cnt["te"])
            self._last_pbase = pbase
        waits = {}
        for b in reads:
            if b.w is not None:
                s, v = b.w
                if waits.get(s, 0) < v:
                    waits[s] = v
        for b in writes:
            if b.w is not None:
                s, v = b.w
                if waits.get(s, 0) < v:
                    waits[s] = v
            for s, v in b.r:
                if waits.get(s, 0) < v:
                    waits[s] = v
        own = ESEM[eng][0]
        for s, v in sorted(waits.items()):
            if no_self_wait and s == own:
                continue
            if s in self.bar:
                v2 = self.cnt[s]
                if s != sem:
                    self.bar[s] = max(self.bar[s], v2)
                self._wait(eng, s, v2)
            else:
                self._wait(eng, s, v)
        if sem in self.bar and self.bar[sem] > 0:
            self._wait(eng, sem, self.bar[sem])
        self.cnt[sem] += step
        val = self.cnt[sem]
        self.ops[eng].append(("op", fn, sem, step))
        for b in writes:
            b.w = (sem, val)
            b.r = []
        for b in reads:
            b.r.append((sem, val))
        return (sem, val)

    def raw_wait(self, eng, sem, val):
        self._wait(eng, sem, val)

    def build(self, sems, block):
        self.handles = sems

        def body(e, eng_name):
            for item in self.ops[eng_name]:
                if item[0] == "wait":
                    e.wait_ge(self.handles[item[1]], item[2])
                else:
                    _, fn, sem, step = item
                    ins = fn(e)
                    ins.then_inc(self.handles[sem], step)

        block.sync(lambda e: body(e, "sync"))
        block.gpsimd(lambda e: body(e, "gpsimd"))
        block.tensor(lambda e: body(e, "tensor"))
        block.vector(lambda e: body(e, "vector"))
        block.scalar(lambda e: body(e, "scalar"))


def ceil_div(a, b):
    return (a + b - 1) // b


def build_program(nchunk=16, dbg=False, stop_after=None):
    TPC = nchunk * C
    nc = bass.Bass("TRN2", target_bir_lowering=False)
    P = Prog(nc)

    def din(name, shape, dt=F32):
        return nc.dram_tensor(name, shape, dt, kind="ExternalInput").ap()

    x_in = din("x", [TPC, D])
    Wq_in = din("wq", [D, D]); bq_in = din("bq", [D])
    Wk_in = din("wk", [D, D]); bk_in = din("bk", [D])
    Wv_in = din("wv", [D, D]); bv_in = din("bv", [D])
    Wb_in = din("wb", [D, D]); bb_in = din("bb", [D])
    Wa_in = din("wa", [D, D]); ba_in = din("ba", [D])
    w1s_in = din("w1s", [FF // NCORES, D])
    b1_in = din("b1", [FF])
    w2s_in = din("w2s", [D // NCORES, FF])
    b2_in = din("b2", [D])
    triu_in = din("triu", [C, C], BF16)
    maskbig_in = din("maskbig", [C, C])
    idt_in = din("idt", [C, C], BF16)
    ones32_in = din("ones32", [C, C], BF16)
    onesf_in = din("onesf", [1, C])
    idh_in = din("idh", [C, H])
    masklt_in = din("masklt", [H, NCORES])
    cst_in = din("cst", [C, 8])

    out_d = nc.dram_tensor("out", [TPC, D], F32, kind="ExternalOutput").ap()
    dbg_d = {}
    if dbg:
        dbg_d["dbg_m"] = nc.dram_tensor("dbg_m", [C, H * nchunk], F32, kind="ExternalOutput").ap()
        dbg_d["dbg_y"] = nc.dram_tensor("dbg_y", [TPC, D], F32, kind="ExternalOutput").ap()
        dbg_d["dbg_x2"] = nc.dram_tensor("dbg_x2", [TPC, D], F32, kind="ExternalOutput").ap()
        dbg_d["dbg_h"] = nc.dram_tensor("dbg_h", [C, FF], F32, kind="ExternalOutput").ap()
        dbg_d["dbg_xq1"] = nc.dram_tensor("dbg_xq1", [TPC, D], F32, kind="ExternalOutput").ap()
        dbg_d["dbg_state"] = nc.dram_tensor("dbg_state", [C, 512], F32, kind="ExternalOutput").ap()
        dbg_d["dbg_sin"] = nc.dram_tensor("dbg_sin", [C, 512], F32, kind="ExternalOutput").ap()

    def dint(name, shape, dt=F32):
        return nc.dram_tensor(name, shape, dt).ap()

    w_bf = {n: dint(f"{n}_bf", [D, D], BF16) for n in ("wq", "wk", "wv", "wb", "wa")}
    x1_h = dint("x1_h", [TPC, D], BF16)
    kfm_h = dint("kfm_h", [D, TPC], BF16)
    y_h = dint("y_h", [TPC, D], F32)
    x2_h = dint("x2_h", [TPC, D], F32)
    t1_sl = dint("t1_sl", [FF // NCORES, D], BF16)
    t1_f = dint("t1_f", [FF, D], BF16)
    t2_sl = dint("t2_sl", [D // NCORES, FF], BF16)
    t2_f = dint("t2_f", [D, FF], BF16)
    ws_in_d = dint("ws_in_d", [1, 2])
    ws_out_d = dint("ws_out_d", [1, 2])
    lag_in = dint("lag_in", [H, HD * HD], BF16)
    lag_out = dint("lag_out", [NCORES * H, HD * HD], BF16)
    laa_in = dint("laa_in", [1, H])
    laa_out = dint("laa_out", [NCORES, H])
    sin_d = dint("sin_d", [H, HD * HD], BF16)

    ctx = ExitStack()
    E = ctx.enter_context

    AR_W = 53100
    AR = E(nc.sbuf_tensor("AR", [128, AR_W], F32))

    off = [0]

    def alloc(words):
        o = off[0]
        off[0] += words
        assert off[0] <= AR_W, f"arena overflow {off[0]}"
        return o

    def f32(o, w, p0=0, pn=128):
        return AR[p0:p0 + pn, o:o + w]

    def bf16(o, w, p0=0, pn=128):
        return AR.bitcast(BF16)[p0:p0 + pn, 2 * o:2 * o + 2 * w]

    def i32v(o, w, p0=0, pn=128):
        return AR.bitcast(I32)[p0:p0 + pn, o:o + w]

    o_triu = alloc(64)
    o_maskbig = alloc(128)
    o_idt = alloc(64)
    o_ones32 = alloc(64)
    o_onesf = alloc(128)
    o_idh = alloc(16)
    o_masklt = alloc(8)
    o_cst = alloc(8)
    o_bvrow_bf = alloc(512)
    o_bbrow_bf = alloc(512)
    o_barow_bf = alloc(512)
    o_b1row_bf = alloc(2048)
    o_b2row_bf = alloc(512)
    o_bqcol = alloc(8)
    o_bkcol = alloc(8)
    o_m_all = alloc(H * nchunk)
    o_curve = alloc(H * nchunk)
    o_uhl = alloc(nchunk * C)
    o_m127 = alloc(H * nchunk)
    o_logg = alloc(H * nchunk)
    o_acol = alloc(H * nchunk)
    o_scr = alloc(160)
    o_state = alloc(512)
    o_statebf = alloc(256)

    R3_W = 10240
    o_R3 = alloc(R3_W)
    o_WS = alloc(16384)
    w_fm = 8 * TPC // 2
    o_x1fm = alloc(w_fm)
    o_qfm = alloc(w_fm)
    if 2 * w_fm < 16384:
        alloc(16384 - 2 * w_fm)
    if nchunk * C >= 2048:
        o_b1t = o_uhl
        o_b2t = o_uhl + 2048
        B1T_DED = False
    else:
        o_b1t = alloc(2048)
        o_b2t = alloc(512)
        B1T_DED = True
    o_fx2b = alloc(1024)
    assert off[0] <= AR_W, f"arena {off[0]}"

    scs = 1
    assert nchunk % scs == 0
    r = o_R3
    o_xt = r; r += 1024
    o_wcl = r; r += 1024
    o_rl = r; r += 1024
    o_az = r; r += 1024
    o_vt = r; r += 1024
    o_bt = r; r += 1024
    o_vb = r; r += 512 * scs
    o_vB = r; r += 512 * scs
    o_kst = r; r += 512
    o_kfmch = r; r += 512
    o_ktm = r; r += 512
    o_dse = r; r += 128
    o_et = r; r += 128
    o_smt = r; r += 64
    o_yt = r; r += 1024
    assert r <= o_R3 + R3_W, f"attn R3 overflow {r - o_R3}"
    o_x1bf = o_kst
    o_lgat = o_wcl
    o_sinh = o_az
    o_sinst = o_bt
    o_wslice = o_xt
    o_wge = o_WS

    o_fx2 = o_R3
    o_fxr = o_fx2 + 1024
    o_fh = o_fxr + 1024
    o_fge = o_fh + 4096
    o_fxq1 = o_fge + 2048
    o_fxq1fm = o_fxq1 + 512
    o_ffout = o_fxq1fm + 512
    assert o_ffout + 1024 <= o_R3 + R3_W

    PS = [E(nc.psum_tensor(f"ps{i}", [128, 512], F32)) for i in range(8)]

    sems = {}
    for s in ALLSEMS:
        sems[s] = E(nc.semaphore(s))

    B = {}

    def buf(name):
        if name not in B:
            B[name] = Buf(name)
        return B[name]

    def dma(eng, out, in_, reads=(), writes=(), transpose=False, sem="ld"):
        if eng == "gpsimd":
            sem = "gq"
        if transpose:
            return P.op(eng, lambda e, o=out, i=in_: e.dma_start(out=o, in_=i, transpose=True),
                        reads=reads, writes=writes, sem=sem, step=16)
        return P.op(eng, lambda e, o=out, i=in_: e.dma_start(
                        out=o, in_=i, allow_slow_non_contiguous=True),
                    reads=reads, writes=writes, sem=sem, step=16)

    def mm(out, lhsT, rhs, start, stop, reads=(), writes=(), pbase=0):
        return P.op("tensor",
                    lambda e, o=out, l=lhsT, r=rhs, s0=start, s1=stop:
                    e.matmul(o, l, r, start=s0, stop=s1),
                    reads=reads, writes=writes, no_self_wait=True, pbase=pbase)

    def transp(out, in_, reads=(), writes=()):
        ident = bf16(o_idt, 64)
        return P.op("tensor",
                    lambda e, o=out, i=in_, d=ident: e.transpose(o, i, d),
                    reads=reads, writes=writes, no_self_wait=True, pbase=0)

    def act(out, in_, func, bias=0.0, scale=1.0, reads=(), writes=()):
        return P.op("scalar",
                    lambda e, o=out, i=in_, f=func, b=bias, s=scale:
                    e.activation(o, i, f, bias=b, scale=s),
                    reads=reads, writes=writes)

    def vop(name, *args, reads=(), writes=(), **kw):
        return P.op("vector",
                    lambda e, a=args, k=kw, n=name: getattr(e, n)(*a, **k),
                    reads=reads, writes=writes)

    def coll(kind, alu, ins, outs, reads=(), writes=()):
        return P.op("gpsimd", lambda e, k=kind, a=alu, i=ins, o=outs:
                    e.collective_compute(k, a, replica_groups=[list(range(NCORES))],
                                         ins=i, outs=o),
                    reads=reads, writes=writes, sem="cc", step=1)

    def cstc(i, pn=128, p0=0):
        return f32(o_cst, 8, p0, pn)[:, i:i + 1]

    def finish_early():
        b_o = buf("outst")
        for ch_ in range(nchunk):
            dma("sync", out_d[ch_ * C:(ch_ + 1) * C, :], f32(o_xt, D),
                reads=[buf("xt")], writes=[b_o], sem="out")
        s_, v_ = b_o.w
        P.raw_wait("gpsimd", s_, v_)
        P.raw_wait("scalar", s_, v_)
        blk = E(nc.Block())
        P.build(sems, blk)
        ctx.close()
        return nc

    b_const = buf("const")
    dma("sync", bf16(o_triu, 64), triu_in, writes=[b_const])
    dma("sync", f32(o_maskbig, 128), maskbig_in, writes=[b_const])
    dma("sync", bf16(o_idt, 64), idt_in, writes=[b_const])
    dma("sync", bf16(o_ones32, 64), ones32_in, writes=[b_const])
    dma("sync", f32(o_onesf, 128, 0, 1), onesf_in, writes=[b_const])
    dma("sync", f32(o_idh, 16), idh_in, writes=[b_const])
    dma("sync", f32(o_masklt, 8, 0, H), masklt_in, writes=[b_const])
    dma("sync", f32(o_cst, 8), cst_in, writes=[b_const])
    b_rowsbf = buf("rowsbf")
    dma("gpsimd", bf16(o_bvrow_bf, 512, 0, 1), bv_in[None, :], writes=[b_rowsbf])
    dma("gpsimd", bf16(o_bbrow_bf, 512, 0, 1), bb_in[None, :], writes=[b_rowsbf])
    dma("gpsimd", bf16(o_barow_bf, 512, 0, 1), ba_in[None, :], writes=[b_rowsbf])
    dma("gpsimd", bf16(o_b1row_bf, 2048, 0, 1), b1_in[None, :], writes=[b_rowsbf])
    dma("gpsimd", bf16(o_b2row_bf, 512, 0, 1), b2_in[None, :], writes=[b_rowsbf])
    b_bcol = buf("bcol")
    dma("sync", f32(o_bqcol, 8), bq_in.rearrange("(a p) -> p a", p=128), writes=[b_bcol])
    dma("sync", f32(o_bkcol, 8), bk_in.rearrange("(a p) -> p a", p=128), writes=[b_bcol])
    vop("tensor_scalar", f32(o_bqcol, 8), f32(o_bqcol, 8), 0.125, None,
        op0=ALU.mult, reads=[b_bcol], writes=[b_bcol])
    vop("tensor_scalar", f32(o_bkcol, 8), f32(o_bkcol, 8), 0.125, None,
        op0=ALU.mult, reads=[b_bcol], writes=[b_bcol])

    b_wbf = {n: buf(f"wbf_{n}") for n in w_bf}
    for n, w_in in (("wa", Wa_in), ("wq", Wq_in), ("wk", Wk_in), ("wv", Wv_in), ("wb", Wb_in)):
        dma("gpsimd", w_bf[n], w_in, writes=[b_wbf[n]])

    b_xt = buf("xt"); b_x1bf = buf("x1bf"); b_wcl = buf("wcl")
    b_rl = buf("rl"); b_az = buf("az")
    RWsl = [b_xt, b_wcl, b_rl, b_az]
    b_WS0 = buf("WS0")
    b_scr = buf("scr")
    b_wsum = buf("wsum")
    b_wsrow = buf("wsrow")
    dma("sync", f32(o_wslice, FF, 0, 128).rearrange("p (a q) -> p a q", a=4),
        w1s_in.rearrange("(a p) q -> p a q", p=128), writes=RWsl)
    vop("reduce_sum", f32(o_scr, 1), f32(o_wslice, FF), axis=AX.X,
        apply_absolute_value=True, reads=RWsl, writes=[b_wsum])
    dma("sync", f32(o_scr + 8, 128, 0, 1), f32(o_scr, 1), reads=[b_wsum], writes=[b_wsrow])
    vop("reduce_sum", f32(o_scr + 4, 1, 0, 1), f32(o_scr + 8, 128, 0, 1), axis=AX.X,
        reads=[b_wsrow], writes=[b_wsrow])
    vop("tensor_copy", f32(o_scr + 6, 1, 0, 1), f32(o_scr + 4, 1, 0, 1),
        reads=[b_wsrow], writes=[b_wsum])
    b_wsl2 = buf("wslice2")
    dma("sync", f32(o_wslice, FF, 0, 128), w2s_in, writes=RWsl + [b_wsl2])
    vop("reduce_sum", f32(o_scr, 1), f32(o_wslice, FF), axis=AX.X,
        apply_absolute_value=True, reads=RWsl, writes=[b_wsum])
    dma("sync", f32(o_scr + 8, 128, 0, 1), f32(o_scr, 1), reads=[b_wsum], writes=[b_wsrow])
    vop("reduce_sum", f32(o_scr + 7, 1, 0, 1), f32(o_scr + 8, 128, 0, 1), axis=AX.X,
        reads=[b_wsrow], writes=[b_wsum])
    dma("gpsimd", ws_in_d, f32(o_scr + 6, 2, 0, 1), reads=[b_wsum], writes=[buf("wsd")])
    coll("AllReduce", ALU.add, [ws_in_d], [ws_out_d],
         reads=[buf("wsd")], writes=[buf("wsd_o")])
    dma("gpsimd", f32(o_scr + 10, 2, 0, 1), ws_out_d, reads=[buf("wsd_o")], writes=[b_wsum])
    b_mcol = buf("mcol")
    vop("tensor_scalar", f32(o_scr + 12, 2, 0, 1), f32(o_scr + 10, 2, 0, 1),
        1.0 / (FF * D), 1e-5, op0=ALU.mult, op1=ALU.max, reads=[b_wsum], writes=[b_wsum])
    b_ps = [buf(f"ps{i}") for i in range(8)]
    mm(PS[0][:, 0:2], f32(o_onesf, 128, 0, 1), f32(o_scr + 12, 2, 0, 1),
       True, True, reads=[b_const, b_wsum], writes=[b_ps[0]])
    vop("tensor_copy", f32(o_scr + 14, 2), PS[0][:, 0:2], reads=[b_ps[0]], writes=[b_mcol])
    m1col = f32(o_scr + 14, 2)[:, 0:1]
    m2col = f32(o_scr + 14, 2)[:, 1:2]
    vop("reciprocal", f32(o_scr + 16, 2), f32(o_scr + 14, 2),
        reads=[b_mcol], writes=[b_mcol])
    s1wcol = f32(o_scr + 16, 2)[:, 0:1]
    s2wcol = f32(o_scr + 16, 2)[:, 1:2]

    def quant_slice(scol, src_bufs, dst_dram, t_buf, src_view=None):
        vop("tensor_scalar", f32(o_wslice, FF), f32(o_wslice, FF), scol, None,
            op0=ALU.mult, reads=src_bufs + [b_mcol], writes=src_bufs)
        vop("tensor_copy", i32v(o_wge, FF), f32(o_wslice, FF),
            reads=src_bufs, writes=[b_WS0])
        vop("tensor_copy", f32(o_wge + FF, FF), i32v(o_wge, FF),
            reads=[b_WS0], writes=[b_WS0])
        vop("tensor_tensor", f32(o_wge, FF), f32(o_wslice, FF), f32(o_wge + FF, FF),
            op=ALU.subtract, reads=[b_WS0] + src_bufs, writes=[b_WS0])
        vop("scalar_tensor_tensor", f32(o_wge + FF, FF), f32(o_wge, FF), 0.5,
            f32(o_wge + FF, FF), op0=ALU.is_gt, op1=ALU.add,
            reads=[b_WS0], writes=[b_WS0])
        vop("tensor_scalar", f32(o_wge, FF), f32(o_wge, FF), -0.5, None,
            op0=ALU.is_lt, reads=[b_WS0], writes=[b_WS0])
        vop("tensor_tensor", f32(o_wge + FF, FF), f32(o_wge + FF, FF),
            f32(o_wge, FF), op=ALU.subtract, reads=[b_WS0], writes=[b_WS0])
        vop("tensor_scalar", bf16(o_wslice, FF // 2), f32(o_wge + FF, FF), 1.0, -1.0,
            op0=ALU.min, op1=ALU.max, reads=[b_WS0], writes=src_bufs)
        sv = bf16(o_wslice, FF // 2) if src_view is None else src_view
        dma("gpsimd", dst_dram, sv, reads=src_bufs, writes=[t_buf])

    b_t1sl = buf("t1sl"); b_t2sl = buf("t2sl")
    quant_slice(s2wcol, RWsl, t2_sl, b_t2sl)
    dma("sync", f32(o_wslice, FF, 0, 128).rearrange("p (a q) -> p a q", a=4),
        w1s_in.rearrange("(a p) q -> p a q", p=128), writes=RWsl)
    quant_slice(s1wcol, RWsl, t1_sl.rearrange("(a p) q -> p a q", p=128), b_t1sl,
                src_view=bf16(o_wslice, FF // 2).rearrange("p (a q) -> p a q", a=4))
    b_t1f = buf("t1f"); b_t2f = buf("t2f")
    coll("AllGather", ALU.bypass, [t2_sl], [t2_f], reads=[b_t2sl], writes=[b_t2f])
    coll("AllGather", ALU.bypass, [t1_sl], [t1_f], reads=[b_t1sl], writes=[b_t1f])
    if stop_after == "setup":
        return finish_early()

    b_x1h = buf("x1h")
    mu = f32(o_scr + 20, 2)
    bnt = f32(o_scr + 24, 12)
    rs_col = f32(o_scr + 36, 1)

    b_scr2 = buf("scr2")
    mu_b = f32(o_scr + 136, 2)
    bnt_b = f32(o_scr + 138, 12)
    rs_b = f32(o_scr + 150, 1)
    for n in range(nchunk):
        par = n % 2
        o_x = o_xt if par == 0 else o_wcl
        b_x = b_xt if par == 0 else b_wcl
        o_xb = o_x1bf if par == 0 else o_rl
        b_xb = b_x1bf if par == 0 else b_rl
        muP, bntP, rsP = (mu, bnt, rs_col) if par == 0 else (mu_b, bnt_b, rs_b)
        b_s = b_scr if par == 0 else b_scr2
        xt = f32(o_x, D)
        dma("sync", xt, x_in[n * C:(n + 1) * C, :], writes=[b_x],
            sem="xa" if par == 0 else "xb")
        vop("bn_stats", bntP[:, 0:6], xt[:, 0:512], reads=[b_x], writes=[b_s])
        vop("bn_stats", bntP[:, 6:12], xt[:, 512:1024], reads=[b_x], writes=[b_s])
        vop("bn_aggr", muP, bntP, reads=[b_s], writes=[b_s])
        act(rsP, muP[:, 1:2], AF.Sqrt, bias=cstc(3), scale=1.0,
            reads=[b_s, b_const], writes=[b_s])
        vop("reciprocal", rsP, rsP, reads=[b_s], writes=[b_s])
        vop("tensor_scalar", bf16(o_xb, 512), xt, muP[:, 0:1], rsP,
            op0=ALU.subtract, op1=ALU.mult, reads=[b_x, b_s], writes=[b_xb])
        dma("sync", x1_h[n * C:(n + 1) * C, :], bf16(o_xb, 512),
            reads=[b_xb], writes=[b_x1h], sem="sa")

    b_x1fm = buf("x1fm")
    for fi in range(8):
        dma("sync", bf16(o_x1fm + fi * (TPC // 2), TPC // 2),
            x1_h[:, fi * 128:(fi + 1) * 128], reads=[b_x1h], writes=[b_x1fm],
            transpose=True, sem="xa")

    def wt_slot(slot, fi):
        return bf16(o_WS + slot * 4096 + fi * 512, 512)

    b_wsl3 = buf("wslot3")
    for fi in range(8):
        dma("sync", wt_slot(3, fi), w_bf["wa"][:, fi * 128:(fi + 1) * 128],
            reads=[b_wbf["wa"]], writes=[b_wsl3, b_WS0], transpose=True, sem="wl")

    b_m = buf("m_all"); b_uhl = buf("uhl"); b_m127 = buf("m127")
    b_curve = buf("curve"); b_acol = buf("acol")
    la_bf = bf16(o_scr + 40, 8)
    la_f = f32(o_scr + 48, 16)
    uh_f = f32(o_scr + 64, 16)
    uh_i = i32v(o_scr + 80, 16)
    uh_bf = bf16(o_scr + 82, 8)
    ul_bf = bf16(o_scr + 90, 8)

    for n in range(nchunk):
        for fi in range(8):
            lhsT = bf16(o_x1fm, w_fm)[:, fi * TPC + n * C: fi * TPC + (n + 1) * C]
            for hf in range(2):
                mm(PS[hf][:, 0:512], lhsT,
                   wt_slot(3, fi)[:, hf * 512:(hf + 1) * 512],
                   start=(fi == 0), stop=False,
                   reads=[b_x1fm, b_wsl3], writes=[b_ps[hf]])
        ones_row = bf16(o_ones32, 64)[0:1, 0:128]
        for hf in range(2):
            mm(PS[hf][:, 0:512], ones_row,
               bf16(o_barow_bf, 512, 0, 1)[0:1, hf * 512:(hf + 1) * 512],
               start=False, stop=True,
               reads=[b_const, b_rowsbf], writes=[b_ps[hf]])
        for hf in range(2):
            sl = slice(hf * 512, (hf + 1) * 512)
            vop("tensor_scalar", f32(o_wcl, D)[:, sl], PS[hf][:, 0:512], 10.0, -10.0,
                op0=ALU.min, op1=ALU.max, reads=[b_ps[hf]], writes=[b_wcl])
            act(f32(o_rl, D)[:, sl], f32(o_wcl, D)[:, sl], AF.Relu, scale=-1.0,
                reads=[b_wcl], writes=[b_rl])
            act(f32(o_az, D)[:, sl], f32(o_wcl, D)[:, sl], AF.Abs,
                reads=[b_wcl], writes=[b_az])
            act(f32(o_az, D)[:, sl], f32(o_az, D)[:, sl], AF.Exp, scale=-1.0,
                reads=[b_az], writes=[b_az])
            act(f32(o_az, D)[:, sl], f32(o_az, D)[:, sl], AF.Ln, bias=cstc(0),
                reads=[b_az, b_const], writes=[b_az])
        vop("tensor_tensor", f32(o_rl, D), f32(o_rl, D), f32(o_az, D), op=ALU.add,
            reads=[b_rl, b_az], writes=[b_rl])
        vop("reduce_sum", la_f.rearrange("p (h o) -> p h o", o=1),
            f32(o_rl, D).rearrange("p (h d) -> p h d", h=H), axis=AX.X,
            reads=[b_rl], writes=[b_scr])
        vop("tensor_scalar", la_bf, la_f, -1.0 / 64, None, op0=ALU.mult,
            reads=[b_scr], writes=[b_scr])
        mm(PS[2][:, 0:16], bf16(o_triu, 64), la_bf, True, True,
           reads=[b_const, b_scr], writes=[b_ps[2]])
        mslice = f32(o_m_all, H * nchunk)[:, n * H:(n + 1) * H]
        vop("tensor_copy", mslice, PS[2][:, 0:16], reads=[b_ps[2]], writes=[b_m])
        vop("tensor_scalar", uh_i, mslice, 0.125, None, op0=ALU.mult,
            reads=[b_m], writes=[b_scr])
        vop("tensor_scalar", uh_f, uh_i, 8.0, None, op0=ALU.mult,
            reads=[b_scr], writes=[b_scr])
        vop("tensor_copy", uh_bf, uh_f, reads=[b_scr], writes=[b_scr])
        vop("tensor_tensor", ul_bf, mslice, uh_f, op=ALU.subtract,
            reads=[b_m, b_scr], writes=[b_scr])
        uhl = bf16(o_uhl, nchunk * C, 0, H)
        transp(PS[3].bitcast(BF16)[0:16, 0:128], uh_bf,
               reads=[b_scr, b_const], writes=[b_ps[3]])
        vop("tensor_copy", uhl[:, n * 256:n * 256 + 128],
            PS[3].bitcast(BF16)[0:16, 0:128], reads=[b_ps[3]], writes=[b_uhl])
        transp(PS[3].bitcast(BF16)[0:16, 128:256], ul_bf,
               reads=[b_scr, b_const], writes=[b_ps[3]])
        vop("tensor_copy", uhl[:, n * 256 + 128:n * 256 + 256],
            PS[3].bitcast(BF16)[0:16, 128:256], reads=[b_ps[3]], writes=[b_uhl])
        act(f32(o_curve, H * nchunk)[:, n * H:(n + 1) * H], mslice, AF.Exp,
            reads=[b_m], writes=[b_curve])
        dma("sync", f32(o_m127, H * nchunk, 0, 1)[:, n * H:(n + 1) * H],
            mslice[127:128, :], reads=[b_m], writes=[b_m127], sem="ca")
        mm(PS[2][:, 16:32], f32(o_onesf, 128, 0, 1),
           f32(o_m127, H * nchunk, 0, 1)[:, n * H:(n + 1) * H],
           True, True, reads=[b_const, b_m127], writes=[b_ps[2]])
        act(f32(o_acol, H * nchunk)[:, n * H:(n + 1) * H],
            PS[2][:, 16:32], AF.Exp, reads=[b_ps[2]], writes=[b_acol])

    if dbg:
        dma("sync", dbg_d["dbg_m"], f32(o_m_all, H * nchunk),
            reads=[b_m], writes=[buf("dbgst")], sem="out")
    if stop_after == "A":
        return finish_early()

    b_wslot = [buf(f"wslot{i}") for i in range(4)]
    for slot, wn in ((0, "wq"), (1, "wk")):
        for fi in range(8):
            dma("sync", wt_slot(slot, fi), w_bf[wn][:, fi * 128:(fi + 1) * 128],
                reads=[b_wbf[wn]], writes=[b_wslot[slot], b_WS0], transpose=True,
                sem="wl")

    b_qfm = buf("qfm"); b_kst = buf("kst"); b_kst2 = buf("kst2")
    b_kfmh = buf("kfmh"); b_kfmh2 = buf("kfmh2")
    B_KFMH = [b_kfmh, b_kfmh2]
    groups = [(g * 512, min(512, TPC - g * 512)) for g in range(ceil_div(TPC, 512))]
    ng = len(groups)
    assert ng <= 4
    for wi, (slot, bcol_off) in enumerate(((0, o_bqcol), (1, o_bkcol))):
        for f in range(8):
            for fi in range(8):
                lhsT = wt_slot(slot, fi)[:, f * 128:(f + 1) * 128]
                for g, (gs, gw) in enumerate(groups):
                    mm(PS[g][:, 0:gw], lhsT,
                       bf16(o_x1fm, w_fm)[:, fi * TPC + gs: fi * TPC + gs + gw],
                       start=(fi == 0), stop=(fi == 7),
                       reads=[b_wslot[slot], b_x1fm], writes=[b_ps[g]])
            for g, (gs, gw) in enumerate(groups):
                bcol = f32(bcol_off, 8)[:, f:f + 1]
                if wi == 0:
                    act(bf16(o_qfm, w_fm)[:, f * TPC + gs: f * TPC + gs + gw],
                        PS[g][:, 0:gw], AF.Identity, bias=bcol, scale=0.125,
                        reads=[b_ps[g], b_bcol], writes=[b_qfm])
                else:
                    kp = (f * ng + g) % 2
                    o_ks = o_kst if kp == 0 else o_yt
                    bks = b_kst if kp == 0 else b_kst2
                    extra = [b_x1bf] if f == 0 and g == 0 else (
                        [buf("yt")] if f == 0 and g == 1 else [])
                    act(bf16(o_ks, 512)[:, 0:gw],
                        PS[g][:, 0:gw], AF.Identity, bias=bcol, scale=0.125,
                        reads=[b_ps[g], b_bcol], writes=[bks] + extra)
                    dma("sync", kfm_h[f * 128:(f + 1) * 128, gs:gs + gw],
                        bf16(o_ks, 512)[:, 0:gw], reads=[bks],
                        writes=[B_KFMH[kp]], sem="sa" if kp == 0 else "sb")

    for slot, wn in ((2, "wv"), (3, "wb")):
        for fi in range(8):
            dma("sync", wt_slot(slot, fi), w_bf[wn][:, fi * 128:(fi + 1) * 128],
                reads=[b_wbf[wn]],
                writes=[b_wslot[slot]] + ([b_wsl3] if slot == 3 else []),
                transpose=True, sem="wl")

    if stop_after == "B0":
        return finish_early()
    b_state = buf("state"); b_statebf = buf("statebf")
    vop("memset", f32(o_state, 512), 0.0, writes=[b_state])
    vop("memset", bf16(o_statebf, 256), 0.0, writes=[b_statebf])

    b_vt = buf("vt"); b_bt = buf("bt"); b_vb = buf("vb"); b_vB = buf("vB")
    b_kfmch = buf("kfmch"); b_ktm = buf("ktm")
    b_dse = buf("dse"); b_et = buf("et"); b_smt = buf("smt"); b_yt = buf("yt")
    b_yh = buf("y_h")
    bm16 = f32(o_scr + 48, 16)

    nsc = nchunk // scs
    for sc in range(nsc):
        for n4 in range(scs):
            ch = sc * scs + n4
            for fi in range(8):
                lhsT = bf16(o_x1fm, w_fm)[:, fi * TPC + ch * C: fi * TPC + (ch + 1) * C]
                for hf in range(2):
                    mm(PS[4 + hf][:, 0:512], lhsT,
                       wt_slot(2, fi)[:, hf * 512:(hf + 1) * 512],
                       start=(fi == 0), stop=False,
                       reads=[b_x1fm, b_wslot[2]], writes=[b_ps[4 + hf]])
                    mm(PS[6 + hf][:, 0:512], lhsT,
                       wt_slot(3, fi)[:, hf * 512:(hf + 1) * 512],
                       start=(fi == 0), stop=False,
                       reads=[b_x1fm, b_wslot[3]], writes=[b_ps[6 + hf]])
            ones_row = bf16(o_ones32, 64)[0:1, 0:128]
            for hf in range(2):
                mm(PS[4 + hf][:, 0:512], ones_row,
                   bf16(o_bvrow_bf, 512, 0, 1)[0:1, hf * 512:(hf + 1) * 512],
                   start=False, stop=True, reads=[b_const, b_rowsbf],
                   writes=[b_ps[4 + hf]])
                mm(PS[6 + hf][:, 0:512], ones_row,
                   bf16(o_bbrow_bf, 512, 0, 1)[0:1, hf * 512:(hf + 1) * 512],
                   start=False, stop=True, reads=[b_const, b_rowsbf],
                   writes=[b_ps[6 + hf]])
            for hf in range(2):
                sl = slice(hf * 512, (hf + 1) * 512)
                vop("tensor_copy", f32(o_vt, D)[:, sl], PS[4 + hf][:, 0:512],
                    reads=[b_ps[4 + hf]], writes=[b_vt])
                act(f32(o_rl, D)[:, sl], PS[6 + hf][:, 0:512], AF.Relu,
                    reads=[b_ps[6 + hf]], writes=[b_rl])
                act(f32(o_az, D)[:, sl], PS[6 + hf][:, 0:512], AF.Abs,
                    reads=[b_ps[6 + hf]], writes=[b_az])
                act(f32(o_az, D)[:, sl], f32(o_az, D)[:, sl], AF.Exp, scale=-1.0,
                    reads=[b_az], writes=[b_az])
                act(f32(o_az, D)[:, sl], f32(o_az, D)[:, sl], AF.Ln, bias=cstc(0),
                    reads=[b_az, b_const], writes=[b_az])
            vop("tensor_tensor", f32(o_bt, D), f32(o_rl, D), f32(o_az, D),
                op=ALU.add, reads=[b_rl, b_az], writes=[b_bt])
            vop("tensor_tensor", bf16(o_vb + n4 * 512, 512), f32(o_vt, D),
                f32(o_bt, D), op=ALU.mult, reads=[b_vt, b_bt], writes=[b_vb])
            vop("reduce_sum", bm16.rearrange("p (h o) -> p h o", o=1),
                f32(o_bt, D).rearrange("p (h d) -> p h d", h=H), axis=AX.X,
                reads=[b_bt], writes=[b_scr])
            vop("tensor_scalar", bm16, bm16, 1.0 / 64, None, op0=ALU.mult,
                reads=[b_scr], writes=[b_scr])
            for h in range(H):
                vop("tensor_scalar",
                    bf16(o_vB + n4 * 512, 512)[:, h * 64:(h + 1) * 64],
                    f32(o_vt, D)[:, h * 64:(h + 1) * 64], bm16[:, h:h + 1], None,
                    op0=ALU.mult, reads=[b_vt, b_scr], writes=[b_vB])

        for n4 in range(scs):
            ch = sc * scs + n4
            csem = "ca" if ch % 2 == 0 else "cb"
            dma("sync", bf16(o_kfmch, 512).rearrange("p (a j) -> p a j", a=8),
                kfm_h.rearrange("(a p) t -> p a t", p=128)[:, :, ch * C:(ch + 1) * C],
                reads=B_KFMH, writes=[b_kfmch], sem=csem)
            dma("sync", bf16(o_ktm, 512), kfm_h[:, ch * C:(ch + 1) * C],
                reads=B_KFMH, writes=[b_ktm], transpose=True, sem=csem)
            uhl = bf16(o_uhl, nchunk * C, 0, H)
            dma("sync", bf16(o_xt, 1024, 0, 1), uhl[0:8, ch * 256:(ch + 1) * 256],
                reads=[b_uhl], writes=[b_xt], sem=csem)
            dma("sync", bf16(o_xt, 1024, 64, 1), uhl[8:16, ch * 256:(ch + 1) * 256],
                reads=[b_uhl], writes=[b_xt], sem=csem)
            for h in range(H):
                base = (h % 2) * 64
                f = h // 2
                kfm_sl = bf16(o_kfmch, 512, base, 64)[:, f * 128:(f + 1) * 128]
                qfm_sl = bf16(o_qfm, w_fm, base, 64)[:, f * TPC + ch * C: f * TPC + (ch + 1) * C]
                pd = PS[6] if base == 0 else PS[7]
                bpd = b_ps[6] if base == 0 else b_ps[7]
                mm(pd[base:base + 64, f * 64:(f + 1) * 64],
                   bf16(o_ktm, 512)[:, h * 64:(h + 1) * 64],
                   bf16(o_vB + n4 * 512, 512)[:, h * 64:(h + 1) * 64],
                   True, True, reads=[b_ktm, b_vB], writes=[bpd])
                half = 0 if h < 8 else 64
                hb = h % 8
                ones_h = bf16(o_ones32, 64)[half:half + 1, 0:128]
                u_row = bf16(o_xt, 1024, half, 1)
                mm(PS[1][:, 0:128], ones_h, u_row[:, hb * 256:hb * 256 + 128],
                   True, False, reads=[b_const, b_xt], writes=[b_ps[1]], pbase=half)
                mm(PS[1][:, 0:128], ones_h, u_row[:, hb * 256 + 128:hb * 256 + 256],
                   False, True, reads=[b_const, b_xt], writes=[b_ps[1]], pbase=half)
                mm(PS[0][:, 0:128], kfm_sl, qfm_sl, True, True,
                   reads=[b_kfmch, b_qfm], writes=[b_ps[0]], pbase=base)
                pm = PS[4] if h < 8 else PS[5]
                bpm = b_ps[4] if h < 8 else b_ps[5]
                mm(pm[:, (h % 8) * 64:(h % 8) * 64 + 64], qfm_sl,
                   bf16(o_statebf, 256, base, 64)[:, f * 64:(f + 1) * 64],
                   True, True, reads=[b_qfm, b_statebf], writes=[bpm], pbase=base)
                vop("scalar_tensor_tensor", f32(o_dse, 128), PS[1][:, 0:128],
                    f32(o_m_all, H * nchunk)[:, ch * H + h:ch * H + h + 1],
                    f32(o_maskbig, 128), op0=ALU.subtract, op1=ALU.add,
                    reads=[b_ps[1], b_m, b_const], writes=[b_dse])
                act(f32(o_et, 128), f32(o_dse, 128), AF.Exp,
                    reads=[b_dse], writes=[b_et])
                vop("tensor_tensor", bf16(o_smt, 64), PS[0][:, 0:128],
                    f32(o_et, 128), op=ALU.mult,
                    reads=[b_ps[0], b_et], writes=[b_smt])
                py = PS[2] if h < 8 else PS[3]
                bpy = b_ps[2] if h < 8 else b_ps[3]
                mm(py[:, (h % 8) * 64:(h % 8) * 64 + 64], bf16(o_smt, 64),
                   bf16(o_vb + n4 * 512, 512)[:, h * 64:(h + 1) * 64],
                   True, True, reads=[b_smt, b_vb], writes=[bpy])
            for h in range(H):
                pm = PS[4] if h < 8 else PS[5]
                bpm = b_ps[4] if h < 8 else b_ps[5]
                sl = slice((h % 8) * 64, (h % 8) * 64 + 64)
                vop("tensor_scalar", f32(o_yt, D)[:, h * 64:(h + 1) * 64],
                    pm[:, sl],
                    f32(o_curve, H * nchunk)[:, ch * H + h:ch * H + h + 1], None,
                    op0=ALU.mult, reads=[bpm, b_curve], writes=[b_yt])
            for h in range(H):
                py = PS[2] if h < 8 else PS[3]
                bpy = b_ps[2] if h < 8 else b_ps[3]
                sl = slice((h % 8) * 64, (h % 8) * 64 + 64)
                ysl = f32(o_yt, D)[:, h * 64:(h + 1) * 64]
                vop("tensor_tensor", ysl, py[:, sl], ysl, op=ALU.add,
                    reads=[bpy], writes=[b_yt])
            dma("sync", y_h[ch * C:(ch + 1) * C, :], f32(o_yt, D),
                reads=[b_yt], writes=[b_yh], sem="sa")
            for h in range(H):
                base = (h % 2) * 64
                f = h // 2
                pd = PS[6] if base == 0 else PS[7]
                bpd = b_ps[6] if base == 0 else b_ps[7]
                sl = slice(f * 64, (f + 1) * 64)
                st_sl = f32(o_state, 512, base, 64)[:, sl]
                vop("scalar_tensor_tensor", st_sl, st_sl,
                    f32(o_acol, H * nchunk, base, 64)[:, ch * H + h:ch * H + h + 1],
                    pd[base:base + 64, sl], op0=ALU.mult, op1=ALU.add,
                    reads=[b_state, b_acol, bpd], writes=[b_state])
            vop("tensor_copy", bf16(o_statebf, 256), f32(o_state, 512),
                reads=[b_state], writes=[b_statebf])

    b_w1qT = buf("w1qT")
    for fi in range(8):
        dma("sync", bf16(o_WS + fi * 2048, 2048), t1_f[:, fi * 128:(fi + 1) * 128],
            reads=[b_t1f],
            writes=[b_w1qT, b_wsl3, b_WS0] + b_wslot, transpose=True, sem="wl")
    if stop_after == "B":
        return finish_early()

    if dbg:
        dma("sync", dbg_d["dbg_state"], f32(o_state, 512),
            reads=[b_state], writes=[buf("dbgst")], sem="out")
    b_logg = buf("logg")
    lg = f32(o_logg, H * nchunk, 0, 1)
    m127r = f32(o_m127, H * nchunk, 0, 1)
    vop("memset", lg[:, 0:H], 0.0, writes=[b_logg])
    for n in range(1, nchunk):
        vop("tensor_tensor", lg[:, n * H:(n + 1) * H],
            lg[:, (n - 1) * H:n * H], m127r[:, (n - 1) * H:n * H], op=ALU.add,
            reads=[b_logg, b_m127], writes=[b_logg])
    b_laa = buf("laa")
    vop("tensor_tensor", f32(o_scr + 90, H, 0, 1), lg[:, (nchunk - 1) * H:],
        m127r[:, (nchunk - 1) * H:], op=ALU.add, reads=[b_logg, b_m127],
        writes=[b_laa])
    dma("gpsimd", laa_in, f32(o_scr + 90, H, 0, 1), reads=[b_laa], writes=[buf("laad")])
    lag4 = lag_in.rearrange("(f h2) (p d) -> h2 p f d", h2=2, p=64)
    for h2 in range(2):
        dma("gpsimd", lag4[h2],
            f32(o_state, 512, h2 * 64, 64).rearrange("p (f d) -> p f d", d=64),
            reads=[b_state], writes=[buf("lagd")])
    coll("AllGather", ALU.bypass, [laa_in], [laa_out],
         reads=[buf("laad")], writes=[buf("laao")])
    coll("AllGather", ALU.bypass, [lag_in], [lag_out],
         reads=[buf("lagd")], writes=[buf("lago")])
    if stop_after == "C1":
        return finish_early()
    b_lgat = buf("lgat")
    lag3 = lag_out.rearrange("(b h) d -> h b d", h=H)
    for h in range(H):
        dma("sync", bf16(o_lgat, 2048, h * 8, 8), lag3[h],
            reads=[buf("lago")],
            writes=[b_lgat] + ([b_wcl, b_rl] if h == 0 else []), sem="ca")
    b_laat = buf("laat")
    laat = f32(o_scr + 108, 8, 0, H)
    dma("sync", laat, laa_out.rearrange("b h -> h b"),
        reads=[buf("laao")], writes=[b_laat], sem="ca")
    if stop_after == "C2":
        return finish_early()
    wlog = f32(o_scr + 64, 8, 0, H)
    vop("tensor_tensor", laat, laat, f32(o_masklt, 8, 0, H), op=ALU.mult,
        reads=[b_laat, b_const], writes=[b_laat])
    vop("memset", wlog[:, 7:8], 0.0, writes=[b_scr])
    for b in range(6, -1, -1):
        vop("tensor_tensor", wlog[:, b:b + 1], wlog[:, b + 1:b + 2],
            laat[:, b + 1:b + 2], op=ALU.add, reads=[b_scr, b_laat], writes=[b_scr])
    act(wlog, wlog, AF.Exp, reads=[b_scr], writes=[b_scr])
    vop("tensor_tensor", wlog, wlog, f32(o_masklt, 8, 0, H), op=ALU.mult,
        reads=[b_scr, b_const], writes=[b_scr])
    b_wcol = buf("wcol")
    dma("sync", f32(o_scr + 120, 1), wlog, reads=[b_scr], writes=[b_wcol], sem="ca")
    vop("tensor_tensor", bf16(o_scr + 122, 8), f32(o_idh, 16),
        f32(o_scr + 120, 1)[:, 0:1].broadcast_to([128, H]), op=ALU.mult,
        reads=[b_wcol, b_const], writes=[b_wcol])
    b_sinh = buf("sinh")
    for o in range(8):
        mm(PS[0][0:16, 0:512], bf16(o_scr + 122, 8),
           bf16(o_lgat, 2048)[:, o * 512:(o + 1) * 512],
           True, True, reads=[b_wcol, b_lgat], writes=[b_ps[0]])
        vop("tensor_copy", bf16(o_sinh, 2048, 0, H)[:, o * 512:(o + 1) * 512],
            PS[0][0:16, 0:512], reads=[b_ps[0]],
            writes=[b_sinh] + ([b_az, b_vt] if o == 0 else []))
    dma("gpsimd", sin_d, bf16(o_sinh, 2048, 0, H), reads=[b_sinh], writes=[buf("sind")])
    b_sinst = buf("sinst")
    sin4 = sin_d.rearrange("(f h2) (p d) -> h2 p f d", h2=2, p=64)
    for h2 in range(2):
        dma("sync",
            bf16(o_sinst, 256, h2 * 64, 64).rearrange("p (f d) -> p f d", d=64),
            sin4[h2], reads=[buf("sind")],
            writes=[b_sinst] + ([b_bt] if h2 == 0 else []), sem="ca")
    if dbg:
        b_dbgsin = buf("dbgsin")
        vop("tensor_copy", f32(o_yt, 512), bf16(o_sinst, 256),
            reads=[b_sinst], writes=[b_dbgsin, b_yt])
        dma("sync", dbg_d["dbg_sin"], f32(o_yt, 512),
            reads=[b_dbgsin, b_yt], writes=[buf("dbgst")], sem="out")
    if stop_after == "C3":
        return finish_early()

    b_cexp = buf("cexp"); b_x2 = buf("x2h")
    mu2 = f32(o_scr + 20, 2)
    for ch in range(nchunk):
        par = ch % 2
        if par == 0:
            o_y, yb = o_yt, [b_yt]
            o_d, o_e, eb = o_dse, o_et, [b_dse, b_et]
            o_x, xb_ = o_xt, [b_xt]
            muP, bntP, rsP, nbP, b_s = mu2, bnt, rs_col, f32(o_scr + 37, 1), b_scr
            b_ce = b_cexp
        else:
            o_y, yb = o_vb, [b_vb, b_vB]
            o_d, o_e, eb = o_kst, o_kst + 128, [b_kst]
            o_x, xb_ = o_kfmch, [b_kfmch, b_ktm]
            muP, bntP, rsP, nbP, b_s = mu_b, bnt_b, rs_b, f32(o_scr + 151, 1), b_scr2
            b_ce = buf("cexp2")
        mm(PS[1][:, 0:16], f32(o_onesf, 128, 0, 1), lg[:, ch * H:(ch + 1) * H],
           True, True, reads=[b_const, b_logg], writes=[b_ps[1]])
        vop("tensor_tensor", f32(o_d, 128)[:, 0:16], PS[1][:, 0:16],
            f32(o_m_all, H * nchunk)[:, ch * H:(ch + 1) * H], op=ALU.add,
            reads=[b_ps[1], b_m], writes=eb)
        act(f32(o_e, 128)[:, 0:16], f32(o_d, 128)[:, 0:16], AF.Exp,
            reads=eb, writes=[b_ce] + eb)
        dma("sync", f32(o_y, D), y_h[ch * C:(ch + 1) * C, :],
            reads=[b_yh], writes=yb, sem="ca" if par == 0 else "cb")
        for h in range(H):
            base = (h % 2) * 64
            f = h // 2
            qfm_sl = bf16(o_qfm, w_fm, base, 64)[:, f * TPC + ch * C: f * TPC + (ch + 1) * C]
            pc = PS[2] if h < 8 else PS[3]
            bpc = b_ps[2] if h < 8 else b_ps[3]
            mm(pc[:, (h % 8) * 64:(h % 8) * 64 + 64], qfm_sl,
               bf16(o_sinst, 256, base, 64)[:, f * 64:(f + 1) * 64],
               True, True, reads=[b_qfm, b_sinst], writes=[bpc], pbase=base)
        if stop_after == "C6" and ch == 0:
            return finish_early()
        for h in range(H):
            pc = PS[2] if h < 8 else PS[3]
            bpc = b_ps[2] if h < 8 else b_ps[3]
            ysl = f32(o_y, D)[:, h * 64:(h + 1) * 64]
            vop("scalar_tensor_tensor", ysl, pc[:, (h % 8) * 64:(h % 8) * 64 + 64],
                f32(o_e, 128)[:, h:h + 1], ysl, op0=ALU.mult, op1=ALU.add,
                reads=[bpc, b_ce] + yb[1:], writes=[yb[0]])
        if dbg:
            dma("sync", dbg_d["dbg_y"][ch * C:(ch + 1) * C, :], f32(o_y, D),
                reads=yb, writes=[buf("dbgst")], sem="out")
        vop("bn_stats", bntP[:, 0:6], f32(o_y, D)[:, 0:512], reads=yb, writes=[b_s])
        vop("bn_stats", bntP[:, 6:12], f32(o_y, D)[:, 512:1024], reads=yb, writes=[b_s])
        vop("bn_aggr", muP, bntP, reads=[b_s], writes=[b_s])
        act(rsP, muP[:, 1:2], AF.Sqrt, bias=cstc(3), scale=1.0,
            reads=[b_s, b_const], writes=[b_s])
        vop("reciprocal", rsP, rsP, reads=[b_s], writes=[b_s])
        vop("tensor_scalar", nbP, muP[:, 0:1], rsP, -1.0, op0=ALU.mult,
            op1=ALU.mult, reads=[b_s], writes=[b_s])
        act(f32(o_y, D), f32(o_y, D), AF.Identity, bias=nbP, scale=rsP,
            reads=yb + [b_s], writes=[yb[0]])
        dma("sync", f32(o_x, D), x_in[ch * C:(ch + 1) * C, :],
            writes=xb_, sem="ca" if par == 0 else "cb")
        vop("tensor_tensor", f32(o_x, D), f32(o_x, D), f32(o_y, D), op=ALU.add,
            reads=yb, writes=[xb_[0]])
        dma("sync", x2_h[ch * C:(ch + 1) * C, :], f32(o_x, D),
            reads=xb_, writes=[b_x2], sem="sa")
        if dbg:
            dma("sync", dbg_d["dbg_x2"][ch * C:(ch + 1) * C, :], f32(o_x, D),
                reads=xb_, writes=[buf("dbgst")], sem="out")

    if stop_after == "C":
        return finish_early()

    b_w2qT = buf("w2qT")
    for fi in range(32):
        dma("sync", bf16(o_x1fm + fi * 512, 512), t2_f[:, fi * 128:(fi + 1) * 128],
            reads=[b_t2f], writes=[b_w2qT, b_x1fm, b_qfm], transpose=True, sem="wl")

    b_b1t = buf("b1t"); b_b2t = buf("b2t")
    ones_row = bf16(o_ones32, 64)[0:1, 0:128]
    b1t_regions = [] if B1T_DED else [b_uhl]
    b2t_regions = [] if B1T_DED else [b_m127, b_logg]
    for o in range(8):
        mm(PS[4][:, 0:512], ones_row, bf16(o_b1row_bf, 2048, 0, 1)[0:1, o * 512:(o + 1) * 512],
           True, True, reads=[b_const, b_rowsbf], writes=[b_ps[4]])
        vop("tensor_copy", bf16(o_b1t, 2048)[:, o * 512:(o + 1) * 512], PS[4][:, 0:512],
            reads=[b_ps[4]], writes=[b_b1t] + (b1t_regions if o == 0 else []))
    for o in range(2):
        mm(PS[4][:, 0:512], ones_row, bf16(o_b2row_bf, 512, 0, 1)[0:1, o * 512:(o + 1) * 512],
           True, True, reads=[b_const, b_rowsbf], writes=[b_ps[4]])
        vop("tensor_copy", bf16(o_b2t, 512)[:, o * 512:(o + 1) * 512], PS[4][:, 0:512],
            reads=[b_ps[4]], writes=[b_b2t] + (b2t_regions if o == 0 else []))

    b_fx2 = buf("fx2"); b_fxr = buf("fxr"); b_fh = buf("fh")
    b_fge = buf("fge"); b_fxq1 = buf("fxq1"); b_fxq1fm = buf("fxq1fm")
    b_ffout = buf("ffout")
    b_outst = buf("outst")
    mu3 = f32(o_scr + 20, 2)
    bnt8 = f32(o_scr + 24, 48)
    b_fx2b = buf("fx2b")
    for ch in range(nchunk):
        par = ch % 2
        fx2 = f32(o_fx2 if par == 0 else o_fx2b, D)
        b_f2 = b_fx2 if par == 0 else b_fx2b
        muD, bntD, rsD, b_sD = (mu3, bnt, rs_col, b_scr) if par == 0 else \
            (mu_b, bnt_b, rs_b, b_scr2)
        dma("sync", fx2, x2_h[ch * C:(ch + 1) * C, :],
            reads=[b_x2], writes=[b_f2] + ([b_xt] if ch == 0 else []),
            sem="ca" if par == 0 else "cb")
        vop("bn_stats", bntD[:, 0:6], fx2[:, 0:512], reads=[b_f2], writes=[b_sD])
        vop("bn_stats", bntD[:, 6:12], fx2[:, 512:1024], reads=[b_f2], writes=[b_sD])
        vop("bn_aggr", muD, bntD, reads=[b_sD], writes=[b_sD])
        act(rsD, muD[:, 1:2], AF.Sqrt, bias=cstc(3), scale=1.0,
            reads=[b_sD, b_const], writes=[b_sD])
        vop("reciprocal", rsD, rsD, reads=[b_sD], writes=[b_sD])
        pe = f32(o_scr + (38 if par == 0 else 152), 1)
        vop("tensor_scalar", pe, muD[:, 1:2], EPS, None, op0=ALU.add,
            reads=[b_sD], writes=[b_sD])
        pet = f32(o_scr + (52 if par == 0 else 153), 1)
        vop("reciprocal", pet, pe, reads=[b_sD], writes=[b_sD])
        vop("tensor_tensor", pe, muD[:, 1:2], pet, op=ALU.mult,
            reads=[b_sD], writes=[b_sD])
        rho = f32(o_scr + (39 if par == 0 else 154), 1)
        act(rho, pe, AF.Sqrt, bias=cstc(3), scale=1.0, reads=[b_sD, b_const],
            writes=[b_sD])
        vop("reciprocal", rho, rho, reads=[b_sD], writes=[b_sD])
        rr = f32(o_scr + (40 if par == 0 else 155), 1)
        vop("tensor_tensor", rr, rsD, rho, op=ALU.mult, reads=[b_sD], writes=[b_sD])
        fxr = f32(o_fxr, D)
        vop("tensor_scalar", fxr, fx2, muD[:, 0:1], rr, op0=ALU.subtract,
            op1=ALU.mult, reads=[b_f2, b_sD],
            writes=[b_fxr] + ([b_wcl, b_lgat] if ch == 0 else []))
        mx = f32(o_scr + 41, 1)
        vop("reduce_max", mx, fxr, axis=AX.X, apply_absolute_value=True,
            reads=[b_fxr], writes=[b_scr])
        vop("tensor_scalar", mx, mx, 1e-5, None, op0=ALU.max, reads=[b_scr], writes=[b_scr])
        scl = f32(o_scr + 42, 1)
        vop("reciprocal", scl, mx, reads=[b_scr], writes=[b_scr])
        vop("tensor_scalar", scl, scl, 127.0, None, op0=ALU.mult,
            reads=[b_scr], writes=[b_scr])
        stok = f32(o_scr + 43, 1)
        vop("tensor_scalar", stok, mx, 1.0 / 127, None, op0=ALU.mult,
            reads=[b_scr], writes=[b_scr])
        vop("tensor_scalar", fxr, fxr, scl, None, op0=ALU.mult,
            reads=[b_scr], writes=[b_fxr])
        fI = f32(o_fge, D)
        vop("tensor_copy", i32v(o_ffout, D), fxr, reads=[b_fxr],
            writes=[b_ffout] + ([b_yt] if ch == 0 else []))
        vop("tensor_copy", fI, i32v(o_ffout, D), reads=[b_ffout],
            writes=[b_fge] + ([b_vb, b_vB, b_kst, b_x1bf, b_kfmch] if ch == 0 else []))
        vop("tensor_tensor", f32(o_ffout, D), fxr, fI, op=ALU.subtract,
            reads=[b_fxr, b_fge], writes=[b_ffout])
        vop("scalar_tensor_tensor", fI, f32(o_ffout, D), 0.5, fI,
            op0=ALU.is_gt, op1=ALU.add, reads=[b_ffout], writes=[b_fge])
        vop("tensor_scalar", f32(o_ffout, D), f32(o_ffout, D), -0.5, None,
            op0=ALU.is_lt, reads=[b_ffout], writes=[b_ffout])
        vop("tensor_tensor", fI, fI, f32(o_ffout, D), op=ALU.subtract,
            reads=[b_ffout], writes=[b_fge])
        vop("tensor_copy", bf16(o_fxq1, 512), fI,
            reads=[b_fge], writes=[b_fxq1] + ([b_ktm] if ch == 0 else []))
        if dbg:
            vop("tensor_scalar", f32(o_ffout, D), bf16(o_fxq1, 512), stok, None,
                op0=ALU.mult, reads=[b_fxq1, b_scr], writes=[b_ffout])
            dma("sync", dbg_d["dbg_xq1"][ch * C:(ch + 1) * C, :], f32(o_ffout, D),
                reads=[b_ffout], writes=[buf("dbgst")], sem="out")
        for fi in range(8):
            transp(PS[fi].bitcast(BF16)[:, 0:128],
                   bf16(o_fxq1, 512)[:, fi * 128:(fi + 1) * 128],
                   reads=[b_fxq1, b_const], writes=[b_ps[fi]])
        for fi in range(8):
            vop("tensor_copy", bf16(o_fxq1fm + fi * 64, 64),
                PS[fi].bitcast(BF16)[:, 0:128], reads=[b_ps[fi]],
                writes=[b_fxq1fm] + ([b_dse, b_et, b_smt] if ch == 0 and fi == 0 else []))
        for fi in range(8):
            lhsT = bf16(o_fxq1fm + fi * 64, 64)
            for b in range(8):
                mm(PS[b][:, 0:512], lhsT,
                   bf16(o_WS + fi * 2048, 2048)[:, b * 512:(b + 1) * 512],
                   start=(fi == 0), stop=(fi == 7),
                   reads=[b_fxq1fm, b_w1qT], writes=[b_ps[b]])
        sc1 = f32(o_scr + 44, 1)
        vop("tensor_tensor", sc1, stok, m1col, op=ALU.mult,
            reads=[b_scr, b_mcol], writes=[b_scr])
        fh = f32(o_fh, FF)
        for b in range(8):
            vop("scalar_tensor_tensor", fh[:, b * 512:(b + 1) * 512], PS[b][:, 0:512],
                sc1, bf16(o_b1t, 2048)[:, b * 512:(b + 1) * 512],
                op0=ALU.mult, op1=ALU.add, reads=[b_ps[b], b_scr, b_b1t],
                writes=[b_fh] + ([b_rl, b_az, b_vt, b_bt, b_lgat, b_sinh, b_sinst]
                                 if ch == 0 and b == 0 else []))
        for qq in range(4):
            act(fh[:, qq * 1024:(qq + 1) * 1024], fh[:, qq * 1024:(qq + 1) * 1024],
                AF.Gelu, reads=[b_fh], writes=[b_fh])
        if dbg and ch == 0:
            dma("sync", dbg_d["dbg_h"], fh, reads=[b_fh], writes=[buf("dbgst")], sem="out")
        for q8 in range(8):
            vop("bn_stats", bnt8[:, q8 * 6:q8 * 6 + 6], fh[:, q8 * 512:(q8 + 1) * 512],
                reads=[b_fh], writes=[b_scr])
        vop("bn_aggr", mu3, bnt8, reads=[b_scr], writes=[b_scr])
        msq = f32(o_scr + 45, 1)
        vop("tensor_scalar", msq, mu3[:, 0:1], mu3[:, 0:1], None, op0=ALU.mult,
            reads=[b_scr], writes=[b_scr])
        vop("tensor_tensor", msq, msq, mu3[:, 1:2], op=ALU.add,
            reads=[b_scr], writes=[b_scr])
        rhoh = f32(o_scr + 46, 1)
        act(rhoh, msq, AF.Sqrt, bias=cstc(3), scale=1.0, reads=[b_scr, b_const],
            writes=[b_scr])
        vop("reciprocal", rhoh, rhoh, reads=[b_scr], writes=[b_scr])
        mxh = f32(o_scr + 47, 1)
        vop("reduce_max", mxh, fh, axis=AX.X, apply_absolute_value=True,
            reads=[b_fh], writes=[b_scr])
        mxu = f32(o_scr + 41, 1)
        vop("tensor_tensor", mxu, mxh, rhoh, op=ALU.mult, reads=[b_scr], writes=[b_scr])
        vop("tensor_scalar", mxu, mxu, 1e-5, None, op0=ALU.max, reads=[b_scr], writes=[b_scr])
        sclh = f32(o_scr + 42, 1)
        vop("reciprocal", sclh, mxu, reads=[b_scr], writes=[b_scr])
        vop("tensor_scalar", sclh, sclh, 127.0, None, op0=ALU.mult,
            reads=[b_scr], writes=[b_scr])
        vop("tensor_tensor", sclh, sclh, rhoh, op=ALU.mult, reads=[b_scr], writes=[b_scr])
        stok2 = f32(o_scr + 43, 1)
        vop("tensor_scalar", stok2, mxu, 1.0 / 127, None, op0=ALU.mult,
            reads=[b_scr], writes=[b_scr])
        vop("tensor_scalar", fh, fh, sclh, None, op0=ALU.mult,
            reads=[b_scr], writes=[b_fh])
        SCR2 = [b_fxq1, b_fxq1fm, b_ffout]
        for hh in range(2):
            fhH = fh[:, hh * 2048:(hh + 1) * 2048]
            dF = f32(o_fge, 2048)
            fS = f32(o_fxq1, 2048)
            vop("tensor_copy", i32v(o_fge, 2048), fhH, reads=[b_fh], writes=[b_fge])
            vop("tensor_copy", fS, i32v(o_fge, 2048), reads=[b_fge], writes=SCR2)
            vop("tensor_tensor", dF, fhH, fS, op=ALU.subtract,
                reads=[b_fh] + SCR2, writes=[b_fge])
            vop("scalar_tensor_tensor", fS, dF, 0.5, fS, op0=ALU.is_gt,
                op1=ALU.add, reads=[b_fge], writes=SCR2)
            vop("tensor_scalar", dF, dF, -0.5, None, op0=ALU.is_lt,
                reads=[b_fge], writes=[b_fge])
            vop("tensor_tensor", fS, fS, dF, op=ALU.subtract,
                reads=[b_fge], writes=SCR2)
            vop("tensor_copy", bf16(o_fh + hh * 1024, 1024), fS,
                reads=SCR2, writes=[b_fh])
        q2bf = bf16(o_fh, 2048)
        for g in range(4):
            for j in range(8):
                fi = g * 8 + j
                transp(PS[j].bitcast(BF16)[:, 0:128],
                       q2bf[:, fi * 128:(fi + 1) * 128],
                       reads=[b_fh, b_const], writes=[b_ps[j]])
            for j in range(8):
                fi = g * 8 + j
                vop("tensor_copy", bf16(o_fge + fi * 64, 64),
                    PS[j].bitcast(BF16)[:, 0:128], reads=[b_ps[j]],
                    writes=[b_fge])
        for fi in range(32):
            lhsT = bf16(o_fge + fi * 64, 64)
            mm(PS[0][:, 0:512], lhsT, bf16(o_x1fm + fi * 512, 512)[:, 0:512],
               start=(fi == 0), stop=(fi == 31), reads=[b_fge, b_w2qT],
               writes=[b_ps[0]])
            mm(PS[1][:, 0:512], lhsT, bf16(o_x1fm + fi * 512, 512)[:, 512:1024],
               start=(fi == 0), stop=(fi == 31), reads=[b_fge, b_w2qT],
               writes=[b_ps[1]])
        sc2 = f32(o_scr + 44, 1)
        vop("tensor_tensor", sc2, stok2, m2col, op=ALU.mult,
            reads=[b_scr, b_mcol], writes=[b_scr])
        ffo = f32(o_ffout, D)
        for b in range(2):
            vop("scalar_tensor_tensor", ffo[:, b * 512:(b + 1) * 512], PS[b][:, 0:512],
                sc2, bf16(o_b2t, 512)[:, b * 512:(b + 1) * 512],
                op0=ALU.mult, op1=ALU.add, reads=[b_ps[b], b_scr, b_b2t],
                writes=[b_ffout])
        vop("bn_stats", bnt[:, 0:6], ffo[:, 0:512], reads=[b_ffout], writes=[b_scr])
        vop("bn_stats", bnt[:, 6:12], ffo[:, 512:1024], reads=[b_ffout], writes=[b_scr])
        vop("bn_aggr", mu3, bnt, reads=[b_scr], writes=[b_scr])
        act(rs_col, mu3[:, 1:2], AF.Sqrt, bias=cstc(3), scale=1.0,
            reads=[b_scr, b_const], writes=[b_scr])
        vop("reciprocal", rs_col, rs_col, reads=[b_scr], writes=[b_scr])
        nb = f32(o_scr + 37, 1)
        vop("tensor_scalar", nb, mu3[:, 0:1], rs_col, -1.0, op0=ALU.mult,
            op1=ALU.mult, reads=[b_scr], writes=[b_scr])
        act(ffo, ffo, AF.Identity, bias=nb, scale=rs_col,
            reads=[b_scr], writes=[b_ffout])
        vop("tensor_tensor", fx2, fx2, ffo, op=ALU.add,
            reads=[b_ffout], writes=[b_f2])
        dma("sync", out_d[ch * C:(ch + 1) * C, :], fx2, reads=[b_f2],
            writes=[b_outst], sem="out")

    st_sem, st_val = b_outst.w
    if dbg:
        d_sem, d_val = buf("dbgst").w
        st_val = max(st_val, d_val)
    P.raw_wait("gpsimd", st_sem, st_val)
    P.raw_wait("scalar", st_sem, st_val)

    block = E(nc.Block())
    P.build(sems, block)
    ctx.close()
    return nc


def make_consts(core_id):
    triu = np.triu(np.ones((C, C))).astype(ml_dtypes.bfloat16)
    maskbig = np.where(np.triu(np.ones((C, C))) > 0, 0.0, -30000.0).astype(np.float32)
    idt = np.eye(C, dtype=ml_dtypes.bfloat16)
    ones32 = np.ones((C, C), dtype=ml_dtypes.bfloat16)
    onesf = np.ones((1, C), dtype=np.float32)
    idh = np.zeros((C, H), dtype=np.float32)
    for h in range(H):
        for b in range(NCORES):
            idh[h * 8 + b, h] = 1.0
    masklt = np.zeros((H, NCORES), dtype=np.float32)
    masklt[:, :core_id] = 1.0
    cst = np.zeros((C, 8), dtype=np.float32)
    cst[:, 0] = 1.0
    cst[:, 1] = 127.0
    cst[:, 2] = 1.0 / 127.0
    cst[:, 3] = EPS
    cst[:, 4] = 0.125
    cst[:, 5] = 1.0 / 64.0
    return dict(triu=triu, maskbig=maskbig, idt=idt, ones32=ones32, onesf=onesf,
                idh=idh, masklt=masklt, cst=cst)


_NC_CACHE = {}


def kernel(L0_attn_x, L0_attn_Wq, L0_attn_bq, L0_attn_Wk, L0_attn_bk,
           L0_attn_Wv, L0_attn_bv, L0_attn_Wbeta, L0_attn_bbeta,
           L0_attn_Walpha, L0_attn_balpha, L0_ffn_w1, L0_ffn_b1,
           L0_ffn_w2, L0_ffn_b2):
    x = np.ascontiguousarray(np.asarray(L0_attn_x, dtype=np.float32)[0])
    T = x.shape[0]
    nchunk = T // (NCORES * C)
    key = nchunk
    if key not in _NC_CACHE:
        _NC_CACHE[key] = build_program(nchunk=nchunk)
    nc = _NC_CACHE[key]
    TPC = nchunk * C
    com = dict(
        wq=np.asarray(L0_attn_Wq, np.float32), bq=np.asarray(L0_attn_bq, np.float32),
        wk=np.asarray(L0_attn_Wk, np.float32), bk=np.asarray(L0_attn_bk, np.float32),
        wv=np.asarray(L0_attn_Wv, np.float32), bv=np.asarray(L0_attn_bv, np.float32),
        wb=np.asarray(L0_attn_Wbeta, np.float32), bb=np.asarray(L0_attn_bbeta, np.float32),
        wa=np.asarray(L0_attn_Walpha, np.float32), ba=np.asarray(L0_attn_balpha, np.float32),
        b1=np.asarray(L0_ffn_b1, np.float32), b2=np.asarray(L0_ffn_b2, np.float32),
    )
    w1 = np.asarray(L0_ffn_w1, np.float32)
    w2 = np.asarray(L0_ffn_w2, np.float32)
    in_maps = []
    for cid in range(NCORES):
        m = dict(com)
        m["x"] = np.ascontiguousarray(x[cid * TPC:(cid + 1) * TPC])
        m["w1s"] = np.ascontiguousarray(w1[cid * (FF // NCORES):(cid + 1) * (FF // NCORES)])
        m["w2s"] = np.ascontiguousarray(w2[cid * (D // NCORES):(cid + 1) * (D // NCORES)])
        m.update(make_consts(cid))
        in_maps.append(m)
    res = run_bass_kernel_spmd(nc, in_maps, core_ids=list(range(NCORES)))
    out = np.concatenate([np.asarray(r["out"], np.float32) for r in res.results], axis=0)
    return out.reshape(1, T, D)


# version 14
# speedup vs baseline: 1.0028x; 1.0028x over previous
import numpy as np
import ml_dtypes
from contextlib import ExitStack
import concourse.bass as bass
import concourse.mybir as mybir
from concourse.bass_utils import run_bass_kernel_spmd

F32 = mybir.dt.float32
BF16 = mybir.dt.bfloat16
I32 = mybir.dt.int32
AF = mybir.ActivationFunctionType
ALU = mybir.AluOpType
AX = mybir.AxisListType

NCORES = 8
D = 1024
H = 16
HD = 64
C = 128
FF = 4096
EPS = 1e-5
ENGS = ("sync", "gpsimd", "tensor", "vector", "scalar")
ESEM = {"sync": ("ld", 16), "gpsimd": ("gq", 16), "tensor": ("te", 1),
        "vector": ("ve", 1), "scalar": ("ae", 1)}
ASYNC = ("ld", "xa", "xb", "wl", "ca", "cb", "sa", "sb", "out", "gq", "cc")
ALLSEMS = ("ld", "xa", "xb", "wl", "ca", "cb", "sa", "sb", "out", "gq", "cc", "te", "ve", "ae")


class Buf:
    __slots__ = ("w", "r", "name")

    def __init__(self, name=""):
        self.name = name
        self.w = None
        self.r = []


class Prog:
    def __init__(self, nc):
        self.nc = nc
        self.ops = {e: [] for e in ENGS}
        self.cnt = {s: 0 for s in ALLSEMS}
        self.wm = {e: {} for e in ENGS}
        self.bar = {s: 0 for s in ASYNC}
        self.handles = {}

    def _wait(self, eng, sem, val):
        if val <= 0:
            return
        if self.wm[eng].get(sem, 0) >= val:
            return
        self.wm[eng][sem] = val
        self.ops[eng].append(("wait", sem, val))

    def op(self, eng, fn, reads=(), writes=(), sem=None, step=None,
           no_self_wait=False, pbase=None):
        if sem is None:
            sem, step = ESEM[eng]
        if pbase is not None:
            if getattr(self, "_last_pbase", None) not in (None, pbase):
                self._wait(eng, "te", self.cnt["te"])
            self._last_pbase = pbase
        waits = {}
        for b in reads:
            if b.w is not None:
                s, v = b.w
                if waits.get(s, 0) < v:
                    waits[s] = v
        for b in writes:
            if b.w is not None:
                s, v = b.w
                if waits.get(s, 0) < v:
                    waits[s] = v
            for s, v in b.r:
                if waits.get(s, 0) < v:
                    waits[s] = v
        own = ESEM[eng][0]
        for s, v in sorted(waits.items()):
            if no_self_wait and s == own:
                continue
            if s in self.bar:
                v2 = self.cnt[s]
                if s != sem:
                    self.bar[s] = max(self.bar[s], v2)
                self._wait(eng, s, v2)
            else:
                self._wait(eng, s, v)
        if sem in self.bar and self.bar[sem] > 0:
            self._wait(eng, sem, self.bar[sem])
        self.cnt[sem] += step
        val = self.cnt[sem]
        self.ops[eng].append(("op", fn, sem, step))
        for b in writes:
            b.w = (sem, val)
            b.r = []
        for b in reads:
            b.r.append((sem, val))
        return (sem, val)

    def raw_wait(self, eng, sem, val):
        self._wait(eng, sem, val)

    def build(self, sems, block):
        self.handles = sems

        def body(e, eng_name):
            for item in self.ops[eng_name]:
                if item[0] == "wait":
                    e.wait_ge(self.handles[item[1]], item[2])
                else:
                    _, fn, sem, step = item
                    ins = fn(e)
                    ins.then_inc(self.handles[sem], step)

        block.sync(lambda e: body(e, "sync"))
        block.gpsimd(lambda e: body(e, "gpsimd"))
        block.tensor(lambda e: body(e, "tensor"))
        block.vector(lambda e: body(e, "vector"))
        block.scalar(lambda e: body(e, "scalar"))


def ceil_div(a, b):
    return (a + b - 1) // b


def build_program(nchunk=16, dbg=False, stop_after=None):
    TPC = nchunk * C
    nc = bass.Bass("TRN2", target_bir_lowering=False)
    P = Prog(nc)

    def din(name, shape, dt=F32):
        return nc.dram_tensor(name, shape, dt, kind="ExternalInput").ap()

    x_in = din("x", [TPC, D])
    Wq_in = din("wq", [D, D]); bq_in = din("bq", [D])
    Wk_in = din("wk", [D, D]); bk_in = din("bk", [D])
    Wv_in = din("wv", [D, D]); bv_in = din("bv", [D])
    Wb_in = din("wb", [D, D]); bb_in = din("bb", [D])
    Wa_in = din("wa", [D, D]); ba_in = din("ba", [D])
    w1s_in = din("w1s", [FF // NCORES, D])
    b1_in = din("b1", [FF])
    w2s_in = din("w2s", [D // NCORES, FF])
    b2_in = din("b2", [D])
    triu_in = din("triu", [C, C], BF16)
    maskbig_in = din("maskbig", [C, C])
    idt_in = din("idt", [C, C], BF16)
    ones32_in = din("ones32", [C, C], BF16)
    onesf_in = din("onesf", [1, C])
    idh_in = din("idh", [C, H])
    masklt_in = din("masklt", [H, NCORES])
    cst_in = din("cst", [C, 8])

    out_d = nc.dram_tensor("out", [TPC, D], F32, kind="ExternalOutput").ap()
    dbg_d = {}
    if dbg:
        dbg_d["dbg_m"] = nc.dram_tensor("dbg_m", [C, H * nchunk], F32, kind="ExternalOutput").ap()
        dbg_d["dbg_y"] = nc.dram_tensor("dbg_y", [TPC, D], F32, kind="ExternalOutput").ap()
        dbg_d["dbg_x2"] = nc.dram_tensor("dbg_x2", [TPC, D], F32, kind="ExternalOutput").ap()
        dbg_d["dbg_h"] = nc.dram_tensor("dbg_h", [C, FF], F32, kind="ExternalOutput").ap()
        dbg_d["dbg_xq1"] = nc.dram_tensor("dbg_xq1", [TPC, D], F32, kind="ExternalOutput").ap()
        dbg_d["dbg_state"] = nc.dram_tensor("dbg_state", [C, 512], F32, kind="ExternalOutput").ap()
        dbg_d["dbg_sin"] = nc.dram_tensor("dbg_sin", [C, 512], F32, kind="ExternalOutput").ap()

    def dint(name, shape, dt=F32):
        return nc.dram_tensor(name, shape, dt).ap()

    w_bf = {n: dint(f"{n}_bf", [D, D], BF16) for n in ("wq", "wk", "wv", "wb", "wa")}
    x1_h = dint("x1_h", [TPC, D], BF16)
    kfm_h = dint("kfm_h", [D, TPC], BF16)
    y_h = dint("y_h", [TPC, D], F32)
    x2_h = dint("x2_h", [TPC, D], F32)
    t1_sl = dint("t1_sl", [FF // NCORES, D], BF16)
    t1_f = dint("t1_f", [FF, D], BF16)
    t2_sl = dint("t2_sl", [D // NCORES, FF], BF16)
    t2_f = dint("t2_f", [D, FF], BF16)
    ws_in_d = dint("ws_in_d", [1, 2])
    ws_out_d = dint("ws_out_d", [1, 2])
    lag_in = dint("lag_in", [H, HD * HD], BF16)
    lag_out = dint("lag_out", [NCORES * H, HD * HD], BF16)
    laa_in = dint("laa_in", [1, H])
    laa_out = dint("laa_out", [NCORES, H])
    sin_d = dint("sin_d", [H, HD * HD], BF16)

    ctx = ExitStack()
    E = ctx.enter_context

    AR_W = 53100
    AR = E(nc.sbuf_tensor("AR", [128, AR_W], F32))

    off = [0]

    def alloc(words):
        o = off[0]
        off[0] += words
        assert off[0] <= AR_W, f"arena overflow {off[0]}"
        return o

    def f32(o, w, p0=0, pn=128):
        return AR[p0:p0 + pn, o:o + w]

    def bf16(o, w, p0=0, pn=128):
        return AR.bitcast(BF16)[p0:p0 + pn, 2 * o:2 * o + 2 * w]

    def i32v(o, w, p0=0, pn=128):
        return AR.bitcast(I32)[p0:p0 + pn, o:o + w]

    o_triu = alloc(64)
    o_maskbig = alloc(128)
    o_idt = alloc(64)
    o_ones32 = alloc(64)
    o_onesf = alloc(128)
    o_idh = alloc(16)
    o_masklt = alloc(8)
    o_cst = alloc(8)
    o_bvrow_bf = alloc(512)
    o_bbrow_bf = alloc(512)
    o_barow_bf = alloc(512)
    o_b1row_bf = alloc(2048)
    o_b2row_bf = alloc(512)
    o_bqcol = alloc(8)
    o_bkcol = alloc(8)
    o_m_all = alloc(H * nchunk)
    o_curve = alloc(H * nchunk)
    o_uhl = alloc(nchunk * C)
    o_m127 = alloc(H * nchunk)
    o_logg = alloc(H * nchunk)
    o_acol = alloc(H * nchunk)
    o_scr = alloc(160)
    o_state = alloc(512)
    o_statebf = alloc(256)

    R3_W = 10240
    o_R3 = alloc(R3_W)
    o_WS = alloc(16384)
    w_fm = 8 * TPC // 2
    o_x1fm = alloc(w_fm)
    o_qfm = alloc(w_fm)
    if 2 * w_fm < 16384:
        alloc(16384 - 2 * w_fm)
    if nchunk * C >= 2048:
        o_b1t = o_uhl
        o_b2t = o_uhl + 2048
        B1T_DED = False
    else:
        o_b1t = alloc(2048)
        o_b2t = alloc(512)
        B1T_DED = True
    assert off[0] <= AR_W, f"arena {off[0]}"

    scs = 1
    assert nchunk % scs == 0
    r = o_R3
    o_xt = r; r += 1024
    o_wcl = r; r += 1024
    o_rl = r; r += 1024
    o_az = r; r += 1024
    o_vt = r; r += 1024
    o_bt = r; r += 1024
    o_vb = r; r += 512 * scs
    o_vB = r; r += 512 * scs
    o_kst = r; r += 512
    o_kfmch = r; r += 512
    o_ktm = r; r += 512
    o_dse = r; r += 128
    o_et = r; r += 128
    o_smt = r; r += 64
    o_yt = r; r += 1024
    assert r <= o_R3 + R3_W, f"attn R3 overflow {r - o_R3}"
    o_x1bf = o_kst
    o_lgat = o_wcl
    o_sinh = o_az
    o_sinst = o_bt
    o_wslice = o_xt
    o_wge = o_WS

    o_fx2 = o_R3
    o_fxr = o_fx2 + 1024
    o_fh = o_fxr + 1024
    o_fge = o_fh + 4096
    o_fxq1 = o_fge + 2048
    o_fxq1fm = o_fxq1 + 512
    o_ffout = o_fxq1fm + 512
    assert o_ffout + 1024 <= o_R3 + R3_W

    PS = [E(nc.psum_tensor(f"ps{i}", [128, 512], F32)) for i in range(8)]

    sems = {}
    for s in ALLSEMS:
        sems[s] = E(nc.semaphore(s))

    B = {}

    def buf(name):
        if name not in B:
            B[name] = Buf(name)
        return B[name]

    def dma(eng, out, in_, reads=(), writes=(), transpose=False, sem="ld"):
        if eng == "gpsimd":
            sem = "gq"
        if transpose:
            return P.op(eng, lambda e, o=out, i=in_: e.dma_start(out=o, in_=i, transpose=True),
                        reads=reads, writes=writes, sem=sem, step=16)
        return P.op(eng, lambda e, o=out, i=in_: e.dma_start(
                        out=o, in_=i, allow_slow_non_contiguous=True),
                    reads=reads, writes=writes, sem=sem, step=16)

    def mm(out, lhsT, rhs, start, stop, reads=(), writes=(), pbase=0):
        return P.op("tensor",
                    lambda e, o=out, l=lhsT, r=rhs, s0=start, s1=stop:
                    e.matmul(o, l, r, start=s0, stop=s1),
                    reads=reads, writes=writes, no_self_wait=True, pbase=pbase)

    def transp(out, in_, reads=(), writes=()):
        ident = bf16(o_idt, 64)
        return P.op("tensor",
                    lambda e, o=out, i=in_, d=ident: e.transpose(o, i, d),
                    reads=reads, writes=writes, no_self_wait=True, pbase=0)

    def act(out, in_, func, bias=0.0, scale=1.0, reads=(), writes=()):
        return P.op("scalar",
                    lambda e, o=out, i=in_, f=func, b=bias, s=scale:
                    e.activation(o, i, f, bias=b, scale=s),
                    reads=reads, writes=writes)

    def vop(name, *args, reads=(), writes=(), **kw):
        return P.op("vector",
                    lambda e, a=args, k=kw, n=name: getattr(e, n)(*a, **k),
                    reads=reads, writes=writes)

    def coll(kind, alu, ins, outs, reads=(), writes=()):
        return P.op("gpsimd", lambda e, k=kind, a=alu, i=ins, o=outs:
                    e.collective_compute(k, a, replica_groups=[list(range(NCORES))],
                                         ins=i, outs=o),
                    reads=reads, writes=writes, sem="cc", step=1)

    def cstc(i, pn=128, p0=0):
        return f32(o_cst, 8, p0, pn)[:, i:i + 1]

    def finish_early():
        b_o = buf("outst")
        for ch_ in range(nchunk):
            dma("sync", out_d[ch_ * C:(ch_ + 1) * C, :], f32(o_xt, D),
                reads=[buf("xt")], writes=[b_o], sem="out")
        s_, v_ = b_o.w
        P.raw_wait("gpsimd", s_, v_)
        P.raw_wait("scalar", s_, v_)
        blk = E(nc.Block())
        P.build(sems, blk)
        ctx.close()
        return nc

    b_const = buf("const")
    dma("sync", bf16(o_triu, 64), triu_in, writes=[b_const])
    dma("sync", f32(o_maskbig, 128), maskbig_in, writes=[b_const])
    dma("sync", bf16(o_idt, 64), idt_in, writes=[b_const])
    dma("sync", bf16(o_ones32, 64), ones32_in, writes=[b_const])
    dma("sync", f32(o_onesf, 128, 0, 1), onesf_in, writes=[b_const])
    dma("sync", f32(o_idh, 16), idh_in, writes=[b_const])
    dma("sync", f32(o_masklt, 8, 0, H), masklt_in, writes=[b_const])
    dma("sync", f32(o_cst, 8), cst_in, writes=[b_const])
    b_rowsbf = buf("rowsbf")
    dma("gpsimd", bf16(o_bvrow_bf, 512, 0, 1), bv_in[None, :], writes=[b_rowsbf])
    dma("gpsimd", bf16(o_bbrow_bf, 512, 0, 1), bb_in[None, :], writes=[b_rowsbf])
    dma("gpsimd", bf16(o_barow_bf, 512, 0, 1), ba_in[None, :], writes=[b_rowsbf])
    dma("gpsimd", bf16(o_b1row_bf, 2048, 0, 1), b1_in[None, :], writes=[b_rowsbf])
    dma("gpsimd", bf16(o_b2row_bf, 512, 0, 1), b2_in[None, :], writes=[b_rowsbf])
    b_bcol = buf("bcol")
    dma("sync", f32(o_bqcol, 8), bq_in.rearrange("(a p) -> p a", p=128), writes=[b_bcol])
    dma("sync", f32(o_bkcol, 8), bk_in.rearrange("(a p) -> p a", p=128), writes=[b_bcol])
    vop("tensor_scalar", f32(o_bqcol, 8), f32(o_bqcol, 8), 0.125, None,
        op0=ALU.mult, reads=[b_bcol], writes=[b_bcol])
    vop("tensor_scalar", f32(o_bkcol, 8), f32(o_bkcol, 8), 0.125, None,
        op0=ALU.mult, reads=[b_bcol], writes=[b_bcol])

    b_wbf = {n: buf(f"wbf_{n}") for n in w_bf}
    for n, w_in in (("wa", Wa_in), ("wq", Wq_in), ("wk", Wk_in), ("wv", Wv_in), ("wb", Wb_in)):
        dma("gpsimd", w_bf[n], w_in, writes=[b_wbf[n]])

    b_xt = buf("xt"); b_x1bf = buf("x1bf"); b_wcl = buf("wcl")
    b_rl = buf("rl"); b_az = buf("az")
    RWsl = [b_xt, b_wcl, b_rl, b_az]
    b_WS0 = buf("WS0")
    b_scr = buf("scr")
    b_wsum = buf("wsum")
    b_wsrow = buf("wsrow")
    dma("sync", f32(o_wslice, FF, 0, 128).rearrange("p (a q) -> p a q", a=4),
        w1s_in.rearrange("(a p) q -> p a q", p=128), writes=RWsl)
    vop("reduce_sum", f32(o_scr, 1), f32(o_wslice, FF), axis=AX.X,
        apply_absolute_value=True, reads=RWsl, writes=[b_wsum])
    dma("sync", f32(o_scr + 8, 128, 0, 1), f32(o_scr, 1), reads=[b_wsum], writes=[b_wsrow])
    vop("reduce_sum", f32(o_scr + 4, 1, 0, 1), f32(o_scr + 8, 128, 0, 1), axis=AX.X,
        reads=[b_wsrow], writes=[b_wsrow])
    vop("tensor_copy", f32(o_scr + 6, 1, 0, 1), f32(o_scr + 4, 1, 0, 1),
        reads=[b_wsrow], writes=[b_wsum])
    b_wsl2 = buf("wslice2")
    dma("sync", f32(o_wslice, FF, 0, 128), w2s_in, writes=RWsl + [b_wsl2])
    vop("reduce_sum", f32(o_scr, 1), f32(o_wslice, FF), axis=AX.X,
        apply_absolute_value=True, reads=RWsl, writes=[b_wsum])
    dma("sync", f32(o_scr + 8, 128, 0, 1), f32(o_scr, 1), reads=[b_wsum], writes=[b_wsrow])
    vop("reduce_sum", f32(o_scr + 7, 1, 0, 1), f32(o_scr + 8, 128, 0, 1), axis=AX.X,
        reads=[b_wsrow], writes=[b_wsum])
    dma("gpsimd", ws_in_d, f32(o_scr + 6, 2, 0, 1), reads=[b_wsum], writes=[buf("wsd")])
    coll("AllReduce", ALU.add, [ws_in_d], [ws_out_d],
         reads=[buf("wsd")], writes=[buf("wsd_o")])
    dma("gpsimd", f32(o_scr + 10, 2, 0, 1), ws_out_d, reads=[buf("wsd_o")], writes=[b_wsum])
    b_mcol = buf("mcol")
    vop("tensor_scalar", f32(o_scr + 12, 2, 0, 1), f32(o_scr + 10, 2, 0, 1),
        1.0 / (FF * D), 1e-5, op0=ALU.mult, op1=ALU.max, reads=[b_wsum], writes=[b_wsum])
    b_ps = [buf(f"ps{i}") for i in range(8)]
    mm(PS[0][:, 0:2], f32(o_onesf, 128, 0, 1), f32(o_scr + 12, 2, 0, 1),
       True, True, reads=[b_const, b_wsum], writes=[b_ps[0]])
    vop("tensor_copy", f32(o_scr + 14, 2), PS[0][:, 0:2], reads=[b_ps[0]], writes=[b_mcol])
    m1col = f32(o_scr + 14, 2)[:, 0:1]
    m2col = f32(o_scr + 14, 2)[:, 1:2]
    vop("reciprocal", f32(o_scr + 16, 2), f32(o_scr + 14, 2),
        reads=[b_mcol], writes=[b_mcol])
    s1wcol = f32(o_scr + 16, 2)[:, 0:1]
    s2wcol = f32(o_scr + 16, 2)[:, 1:2]

    def quant_slice(scol, src_bufs, dst_dram, t_buf, src_view=None):
        vop("tensor_scalar", f32(o_wslice, FF), f32(o_wslice, FF), scol, None,
            op0=ALU.mult, reads=src_bufs + [b_mcol], writes=src_bufs)
        vop("tensor_copy", i32v(o_wge, FF), f32(o_wslice, FF),
            reads=src_bufs, writes=[b_WS0])
        vop("tensor_copy", f32(o_wge + FF, FF), i32v(o_wge, FF),
            reads=[b_WS0], writes=[b_WS0])
        vop("tensor_tensor", f32(o_wge, FF), f32(o_wslice, FF), f32(o_wge + FF, FF),
            op=ALU.subtract, reads=[b_WS0] + src_bufs, writes=[b_WS0])
        vop("scalar_tensor_tensor", f32(o_wge + FF, FF), f32(o_wge, FF), 0.5,
            f32(o_wge + FF, FF), op0=ALU.is_gt, op1=ALU.add,
            reads=[b_WS0], writes=[b_WS0])
        vop("tensor_scalar", f32(o_wge, FF), f32(o_wge, FF), -0.5, None,
            op0=ALU.is_lt, reads=[b_WS0], writes=[b_WS0])
        vop("tensor_tensor", f32(o_wge + FF, FF), f32(o_wge + FF, FF),
            f32(o_wge, FF), op=ALU.subtract, reads=[b_WS0], writes=[b_WS0])
        vop("tensor_scalar", bf16(o_wslice, FF // 2), f32(o_wge + FF, FF), 1.0, -1.0,
            op0=ALU.min, op1=ALU.max, reads=[b_WS0], writes=src_bufs)
        sv = bf16(o_wslice, FF // 2) if src_view is None else src_view
        dma("gpsimd", dst_dram, sv, reads=src_bufs, writes=[t_buf])

    b_t1sl = buf("t1sl"); b_t2sl = buf("t2sl")
    quant_slice(s2wcol, RWsl, t2_sl, b_t2sl)
    dma("sync", f32(o_wslice, FF, 0, 128).rearrange("p (a q) -> p a q", a=4),
        w1s_in.rearrange("(a p) q -> p a q", p=128), writes=RWsl)
    quant_slice(s1wcol, RWsl, t1_sl.rearrange("(a p) q -> p a q", p=128), b_t1sl,
                src_view=bf16(o_wslice, FF // 2).rearrange("p (a q) -> p a q", a=4))
    b_t1f = buf("t1f"); b_t2f = buf("t2f")
    coll("AllGather", ALU.bypass, [t2_sl], [t2_f], reads=[b_t2sl], writes=[b_t2f])
    coll("AllGather", ALU.bypass, [t1_sl], [t1_f], reads=[b_t1sl], writes=[b_t1f])
    if stop_after == "setup":
        return finish_early()

    b_x1h = buf("x1h")
    mu = f32(o_scr + 20, 2)
    bnt = f32(o_scr + 24, 12)
    rs_col = f32(o_scr + 36, 1)

    b_scr2 = buf("scr2")
    mu_b = f32(o_scr + 136, 2)
    bnt_b = f32(o_scr + 138, 12)
    rs_b = f32(o_scr + 150, 1)
    for n in range(nchunk):
        par = n % 2
        o_x = o_xt if par == 0 else o_wcl
        b_x = b_xt if par == 0 else b_wcl
        o_xb = o_x1bf if par == 0 else o_rl
        b_xb = b_x1bf if par == 0 else b_rl
        muP, bntP, rsP = (mu, bnt, rs_col) if par == 0 else (mu_b, bnt_b, rs_b)
        b_s = b_scr if par == 0 else b_scr2
        xt = f32(o_x, D)
        dma("sync", xt, x_in[n * C:(n + 1) * C, :], writes=[b_x],
            sem="xa" if par == 0 else "xb")
        vop("bn_stats", bntP[:, 0:6], xt[:, 0:512], reads=[b_x], writes=[b_s])
        vop("bn_stats", bntP[:, 6:12], xt[:, 512:1024], reads=[b_x], writes=[b_s])
        vop("bn_aggr", muP, bntP, reads=[b_s], writes=[b_s])
        act(rsP, muP[:, 1:2], AF.Sqrt, bias=cstc(3), scale=1.0,
            reads=[b_s, b_const], writes=[b_s])
        vop("reciprocal", rsP, rsP, reads=[b_s], writes=[b_s])
        vop("tensor_scalar", bf16(o_xb, 512), xt, muP[:, 0:1], rsP,
            op0=ALU.subtract, op1=ALU.mult, reads=[b_x, b_s], writes=[b_xb])
        dma("sync", x1_h[n * C:(n + 1) * C, :], bf16(o_xb, 512),
            reads=[b_xb], writes=[b_x1h], sem="sa")

    b_x1fm = buf("x1fm")
    for fi in range(8):
        dma("sync", bf16(o_x1fm + fi * (TPC // 2), TPC // 2),
            x1_h[:, fi * 128:(fi + 1) * 128], reads=[b_x1h], writes=[b_x1fm],
            transpose=True, sem="xa")

    def wt_slot(slot, fi):
        return bf16(o_WS + slot * 4096 + fi * 512, 512)

    b_wsl3 = buf("wslot3")
    for fi in range(8):
        dma("sync", wt_slot(3, fi), w_bf["wa"][:, fi * 128:(fi + 1) * 128],
            reads=[b_wbf["wa"]], writes=[b_wsl3, b_WS0], transpose=True, sem="wl")

    b_m = buf("m_all"); b_uhl = buf("uhl"); b_m127 = buf("m127")
    b_curve = buf("curve"); b_acol = buf("acol")
    la_bf = bf16(o_scr + 40, 8)
    la_f = f32(o_scr + 48, 16)
    uh_f = f32(o_scr + 64, 16)
    uh_i = i32v(o_scr + 80, 16)
    uh_bf = bf16(o_scr + 82, 8)
    ul_bf = bf16(o_scr + 90, 8)

    for n in range(nchunk):
        for fi in range(8):
            lhsT = bf16(o_x1fm, w_fm)[:, fi * TPC + n * C: fi * TPC + (n + 1) * C]
            for hf in range(2):
                mm(PS[hf][:, 0:512], lhsT,
                   wt_slot(3, fi)[:, hf * 512:(hf + 1) * 512],
                   start=(fi == 0), stop=False,
                   reads=[b_x1fm, b_wsl3], writes=[b_ps[hf]])
        ones_row = bf16(o_ones32, 64)[0:1, 0:128]
        for hf in range(2):
            mm(PS[hf][:, 0:512], ones_row,
               bf16(o_barow_bf, 512, 0, 1)[0:1, hf * 512:(hf + 1) * 512],
               start=False, stop=True,
               reads=[b_const, b_rowsbf], writes=[b_ps[hf]])
        for hf in range(2):
            sl = slice(hf * 512, (hf + 1) * 512)
            vop("tensor_scalar", f32(o_wcl, D)[:, sl], PS[hf][:, 0:512], 10.0, -10.0,
                op0=ALU.min, op1=ALU.max, reads=[b_ps[hf]], writes=[b_wcl])
            act(f32(o_rl, D)[:, sl], f32(o_wcl, D)[:, sl], AF.Relu, scale=-1.0,
                reads=[b_wcl], writes=[b_rl])
            act(f32(o_az, D)[:, sl], f32(o_wcl, D)[:, sl], AF.Abs,
                reads=[b_wcl], writes=[b_az])
            act(f32(o_az, D)[:, sl], f32(o_az, D)[:, sl], AF.Exp, scale=-1.0,
                reads=[b_az], writes=[b_az])
            act(f32(o_az, D)[:, sl], f32(o_az, D)[:, sl], AF.Ln, bias=cstc(0),
                reads=[b_az, b_const], writes=[b_az])
        vop("tensor_tensor", f32(o_rl, D), f32(o_rl, D), f32(o_az, D), op=ALU.add,
            reads=[b_rl, b_az], writes=[b_rl])
        vop("reduce_sum", la_f.rearrange("p (h o) -> p h o", o=1),
            f32(o_rl, D).rearrange("p (h d) -> p h d", h=H), axis=AX.X,
            reads=[b_rl], writes=[b_scr])
        vop("tensor_scalar", la_bf, la_f, -1.0 / 64, None, op0=ALU.mult,
            reads=[b_scr], writes=[b_scr])
        mm(PS[2][:, 0:16], bf16(o_triu, 64), la_bf, True, True,
           reads=[b_const, b_scr], writes=[b_ps[2]])
        mslice = f32(o_m_all, H * nchunk)[:, n * H:(n + 1) * H]
        vop("tensor_copy", mslice, PS[2][:, 0:16], reads=[b_ps[2]], writes=[b_m])
        vop("tensor_scalar", uh_i, mslice, 0.125, None, op0=ALU.mult,
            reads=[b_m], writes=[b_scr])
        vop("tensor_scalar", uh_f, uh_i, 8.0, None, op0=ALU.mult,
            reads=[b_scr], writes=[b_scr])
        vop("tensor_copy", uh_bf, uh_f, reads=[b_scr], writes=[b_scr])
        vop("tensor_tensor", ul_bf, mslice, uh_f, op=ALU.subtract,
            reads=[b_m, b_scr], writes=[b_scr])
        uhl = bf16(o_uhl, nchunk * C, 0, H)
        transp(PS[3].bitcast(BF16)[0:16, 0:128], uh_bf,
               reads=[b_scr, b_const], writes=[b_ps[3]])
        vop("tensor_copy", uhl[:, n * 256:n * 256 + 128],
            PS[3].bitcast(BF16)[0:16, 0:128], reads=[b_ps[3]], writes=[b_uhl])
        transp(PS[3].bitcast(BF16)[0:16, 128:256], ul_bf,
               reads=[b_scr, b_const], writes=[b_ps[3]])
        vop("tensor_copy", uhl[:, n * 256 + 128:n * 256 + 256],
            PS[3].bitcast(BF16)[0:16, 128:256], reads=[b_ps[3]], writes=[b_uhl])
        act(f32(o_curve, H * nchunk)[:, n * H:(n + 1) * H], mslice, AF.Exp,
            reads=[b_m], writes=[b_curve])
        dma("sync", f32(o_m127, H * nchunk, 0, 1)[:, n * H:(n + 1) * H],
            mslice[127:128, :], reads=[b_m], writes=[b_m127], sem="ca")
        mm(PS[2][:, 16:32], f32(o_onesf, 128, 0, 1),
           f32(o_m127, H * nchunk, 0, 1)[:, n * H:(n + 1) * H],
           True, True, reads=[b_const, b_m127], writes=[b_ps[2]])
        act(f32(o_acol, H * nchunk)[:, n * H:(n + 1) * H],
            PS[2][:, 16:32], AF.Exp, reads=[b_ps[2]], writes=[b_acol])

    if dbg:
        dma("sync", dbg_d["dbg_m"], f32(o_m_all, H * nchunk),
            reads=[b_m], writes=[buf("dbgst")], sem="out")
    if stop_after == "A":
        return finish_early()

    b_wslot = [buf(f"wslot{i}") for i in range(4)]
    for slot, wn in ((0, "wq"), (1, "wk")):
        for fi in range(8):
            dma("sync", wt_slot(slot, fi), w_bf[wn][:, fi * 128:(fi + 1) * 128],
                reads=[b_wbf[wn]], writes=[b_wslot[slot], b_WS0], transpose=True,
                sem="wl")

    b_qfm = buf("qfm"); b_kst = buf("kst"); b_kst2 = buf("kst2")
    b_kfmh = buf("kfmh"); b_kfmh2 = buf("kfmh2")
    B_KFMH = [b_kfmh, b_kfmh2]
    groups = [(g * 512, min(512, TPC - g * 512)) for g in range(ceil_div(TPC, 512))]
    ng = len(groups)
    assert ng <= 4
    for wi, (slot, bcol_off) in enumerate(((0, o_bqcol), (1, o_bkcol))):
        for f in range(8):
            for fi in range(8):
                lhsT = wt_slot(slot, fi)[:, f * 128:(f + 1) * 128]
                for g, (gs, gw) in enumerate(groups):
                    mm(PS[g][:, 0:gw], lhsT,
                       bf16(o_x1fm, w_fm)[:, fi * TPC + gs: fi * TPC + gs + gw],
                       start=(fi == 0), stop=(fi == 7),
                       reads=[b_wslot[slot], b_x1fm], writes=[b_ps[g]])
            for g, (gs, gw) in enumerate(groups):
                bcol = f32(bcol_off, 8)[:, f:f + 1]
                if wi == 0:
                    act(bf16(o_qfm, w_fm)[:, f * TPC + gs: f * TPC + gs + gw],
                        PS[g][:, 0:gw], AF.Identity, bias=bcol, scale=0.125,
                        reads=[b_ps[g], b_bcol], writes=[b_qfm])
                else:
                    kp = (f * ng + g) % 2
                    o_ks = o_kst if kp == 0 else o_yt
                    bks = b_kst if kp == 0 else b_kst2
                    extra = [b_x1bf] if f == 0 and g == 0 else (
                        [buf("yt")] if f == 0 and g == 1 else [])
                    act(bf16(o_ks, 512)[:, 0:gw],
                        PS[g][:, 0:gw], AF.Identity, bias=bcol, scale=0.125,
                        reads=[b_ps[g], b_bcol], writes=[bks] + extra)
                    dma("sync", kfm_h[f * 128:(f + 1) * 128, gs:gs + gw],
                        bf16(o_ks, 512)[:, 0:gw], reads=[bks],
                        writes=[B_KFMH[kp]], sem="sa" if kp == 0 else "sb")

    for slot, wn in ((2, "wv"), (3, "wb")):
        for fi in range(8):
            dma("sync", wt_slot(slot, fi), w_bf[wn][:, fi * 128:(fi + 1) * 128],
                reads=[b_wbf[wn]],
                writes=[b_wslot[slot]] + ([b_wsl3] if slot == 3 else []),
                transpose=True, sem="wl")

    if stop_after == "B0":
        return finish_early()
    b_state = buf("state"); b_statebf = buf("statebf")
    vop("memset", f32(o_state, 512), 0.0, writes=[b_state])
    vop("memset", bf16(o_statebf, 256), 0.0, writes=[b_statebf])

    b_vt = buf("vt"); b_bt = buf("bt"); b_vb = buf("vb"); b_vB = buf("vB")
    b_kfmch = buf("kfmch"); b_ktm = buf("ktm")
    b_dse = buf("dse"); b_et = buf("et"); b_smt = buf("smt"); b_yt = buf("yt")
    b_yh = buf("y_h")
    bm16 = f32(o_scr + 48, 16)

    nsc = nchunk // scs
    for sc in range(nsc):
        for n4 in range(scs):
            ch = sc * scs + n4
            for fi in range(8):
                lhsT = bf16(o_x1fm, w_fm)[:, fi * TPC + ch * C: fi * TPC + (ch + 1) * C]
                for hf in range(2):
                    mm(PS[4 + hf][:, 0:512], lhsT,
                       wt_slot(2, fi)[:, hf * 512:(hf + 1) * 512],
                       start=(fi == 0), stop=False,
                       reads=[b_x1fm, b_wslot[2]], writes=[b_ps[4 + hf]])
                    mm(PS[6 + hf][:, 0:512], lhsT,
                       wt_slot(3, fi)[:, hf * 512:(hf + 1) * 512],
                       start=(fi == 0), stop=False,
                       reads=[b_x1fm, b_wslot[3]], writes=[b_ps[6 + hf]])
            ones_row = bf16(o_ones32, 64)[0:1, 0:128]
            for hf in range(2):
                mm(PS[4 + hf][:, 0:512], ones_row,
                   bf16(o_bvrow_bf, 512, 0, 1)[0:1, hf * 512:(hf + 1) * 512],
                   start=False, stop=True, reads=[b_const, b_rowsbf],
                   writes=[b_ps[4 + hf]])
                mm(PS[6 + hf][:, 0:512], ones_row,
                   bf16(o_bbrow_bf, 512, 0, 1)[0:1, hf * 512:(hf + 1) * 512],
                   start=False, stop=True, reads=[b_const, b_rowsbf],
                   writes=[b_ps[6 + hf]])
            for hf in range(2):
                sl = slice(hf * 512, (hf + 1) * 512)
                vop("tensor_copy", f32(o_vt, D)[:, sl], PS[4 + hf][:, 0:512],
                    reads=[b_ps[4 + hf]], writes=[b_vt])
                act(f32(o_rl, D)[:, sl], PS[6 + hf][:, 0:512], AF.Relu,
                    reads=[b_ps[6 + hf]], writes=[b_rl])
                act(f32(o_az, D)[:, sl], PS[6 + hf][:, 0:512], AF.Abs,
                    reads=[b_ps[6 + hf]], writes=[b_az])
                act(f32(o_az, D)[:, sl], f32(o_az, D)[:, sl], AF.Exp, scale=-1.0,
                    reads=[b_az], writes=[b_az])
                act(f32(o_az, D)[:, sl], f32(o_az, D)[:, sl], AF.Ln, bias=cstc(0),
                    reads=[b_az, b_const], writes=[b_az])
            vop("tensor_tensor", f32(o_bt, D), f32(o_rl, D), f32(o_az, D),
                op=ALU.add, reads=[b_rl, b_az], writes=[b_bt])
            vop("tensor_tensor", bf16(o_vb + n4 * 512, 512), f32(o_vt, D),
                f32(o_bt, D), op=ALU.mult, reads=[b_vt, b_bt], writes=[b_vb])
            vop("reduce_sum", bm16.rearrange("p (h o) -> p h o", o=1),
                f32(o_bt, D).rearrange("p (h d) -> p h d", h=H), axis=AX.X,
                reads=[b_bt], writes=[b_scr])
            vop("tensor_scalar", bm16, bm16, 1.0 / 64, None, op0=ALU.mult,
                reads=[b_scr], writes=[b_scr])
            for h in range(H):
                vop("tensor_scalar",
                    bf16(o_vB + n4 * 512, 512)[:, h * 64:(h + 1) * 64],
                    f32(o_vt, D)[:, h * 64:(h + 1) * 64], bm16[:, h:h + 1], None,
                    op0=ALU.mult, reads=[b_vt, b_scr], writes=[b_vB])

        for n4 in range(scs):
            ch = sc * scs + n4
            csem = "ca" if ch % 2 == 0 else "cb"
            dma("sync", bf16(o_kfmch, 512).rearrange("p (a j) -> p a j", a=8),
                kfm_h.rearrange("(a p) t -> p a t", p=128)[:, :, ch * C:(ch + 1) * C],
                reads=B_KFMH, writes=[b_kfmch], sem=csem)
            dma("sync", bf16(o_ktm, 512), kfm_h[:, ch * C:(ch + 1) * C],
                reads=B_KFMH, writes=[b_ktm], transpose=True, sem=csem)
            uhl = bf16(o_uhl, nchunk * C, 0, H)
            dma("sync", bf16(o_xt, 1024, 0, 1), uhl[0:8, ch * 256:(ch + 1) * 256],
                reads=[b_uhl], writes=[b_xt], sem=csem)
            dma("sync", bf16(o_xt, 1024, 64, 1), uhl[8:16, ch * 256:(ch + 1) * 256],
                reads=[b_uhl], writes=[b_xt], sem=csem)
            for h in range(H):
                base = (h % 2) * 64
                f = h // 2
                kfm_sl = bf16(o_kfmch, 512, base, 64)[:, f * 128:(f + 1) * 128]
                qfm_sl = bf16(o_qfm, w_fm, base, 64)[:, f * TPC + ch * C: f * TPC + (ch + 1) * C]
                pd = PS[6] if base == 0 else PS[7]
                bpd = b_ps[6] if base == 0 else b_ps[7]
                mm(pd[base:base + 64, f * 64:(f + 1) * 64],
                   bf16(o_ktm, 512)[:, h * 64:(h + 1) * 64],
                   bf16(o_vB + n4 * 512, 512)[:, h * 64:(h + 1) * 64],
                   True, True, reads=[b_ktm, b_vB], writes=[bpd])
                half = 0 if h < 8 else 64
                hb = h % 8
                ones_h = bf16(o_ones32, 64)[half:half + 1, 0:128]
                u_row = bf16(o_xt, 1024, half, 1)
                mm(PS[1][:, 0:128], ones_h, u_row[:, hb * 256:hb * 256 + 128],
                   True, False, reads=[b_const, b_xt], writes=[b_ps[1]], pbase=half)
                mm(PS[1][:, 0:128], ones_h, u_row[:, hb * 256 + 128:hb * 256 + 256],
                   False, True, reads=[b_const, b_xt], writes=[b_ps[1]], pbase=half)
                mm(PS[0][:, 0:128], kfm_sl, qfm_sl, True, True,
                   reads=[b_kfmch, b_qfm], writes=[b_ps[0]], pbase=base)
                pm = PS[4] if h < 8 else PS[5]
                bpm = b_ps[4] if h < 8 else b_ps[5]
                mm(pm[:, (h % 8) * 64:(h % 8) * 64 + 64], qfm_sl,
                   bf16(o_statebf, 256, base, 64)[:, f * 64:(f + 1) * 64],
                   True, True, reads=[b_qfm, b_statebf], writes=[bpm], pbase=base)
                vop("scalar_tensor_tensor", f32(o_dse, 128), PS[1][:, 0:128],
                    f32(o_m_all, H * nchunk)[:, ch * H + h:ch * H + h + 1],
                    f32(o_maskbig, 128), op0=ALU.subtract, op1=ALU.add,
                    reads=[b_ps[1], b_m, b_const], writes=[b_dse])
                act(f32(o_et, 128), f32(o_dse, 128), AF.Exp,
                    reads=[b_dse], writes=[b_et])
                vop("tensor_tensor", bf16(o_smt, 64), PS[0][:, 0:128],
                    f32(o_et, 128), op=ALU.mult,
                    reads=[b_ps[0], b_et], writes=[b_smt])
                py = PS[2] if h < 8 else PS[3]
                bpy = b_ps[2] if h < 8 else b_ps[3]
                mm(py[:, (h % 8) * 64:(h % 8) * 64 + 64], bf16(o_smt, 64),
                   bf16(o_vb + n4 * 512, 512)[:, h * 64:(h + 1) * 64],
                   True, True, reads=[b_smt, b_vb], writes=[bpy])
            for h in range(H):
                pm = PS[4] if h < 8 else PS[5]
                bpm = b_ps[4] if h < 8 else b_ps[5]
                sl = slice((h % 8) * 64, (h % 8) * 64 + 64)
                vop("tensor_scalar", f32(o_yt, D)[:, h * 64:(h + 1) * 64],
                    pm[:, sl],
                    f32(o_curve, H * nchunk)[:, ch * H + h:ch * H + h + 1], None,
                    op0=ALU.mult, reads=[bpm, b_curve], writes=[b_yt])
            for h in range(H):
                py = PS[2] if h < 8 else PS[3]
                bpy = b_ps[2] if h < 8 else b_ps[3]
                sl = slice((h % 8) * 64, (h % 8) * 64 + 64)
                ysl = f32(o_yt, D)[:, h * 64:(h + 1) * 64]
                vop("tensor_tensor", ysl, py[:, sl], ysl, op=ALU.add,
                    reads=[bpy], writes=[b_yt])
            dma("sync", y_h[ch * C:(ch + 1) * C, :], f32(o_yt, D),
                reads=[b_yt], writes=[b_yh], sem="sa")
            for h in range(H):
                base = (h % 2) * 64
                f = h // 2
                pd = PS[6] if base == 0 else PS[7]
                bpd = b_ps[6] if base == 0 else b_ps[7]
                sl = slice(f * 64, (f + 1) * 64)
                st_sl = f32(o_state, 512, base, 64)[:, sl]
                vop("scalar_tensor_tensor", st_sl, st_sl,
                    f32(o_acol, H * nchunk, base, 64)[:, ch * H + h:ch * H + h + 1],
                    pd[base:base + 64, sl], op0=ALU.mult, op1=ALU.add,
                    reads=[b_state, b_acol, bpd], writes=[b_state])
            vop("tensor_copy", bf16(o_statebf, 256), f32(o_state, 512),
                reads=[b_state], writes=[b_statebf])

    b_w1qT = buf("w1qT")
    for fi in range(8):
        dma("sync", bf16(o_WS + fi * 2048, 2048), t1_f[:, fi * 128:(fi + 1) * 128],
            reads=[b_t1f],
            writes=[b_w1qT, b_wsl3, b_WS0] + b_wslot, transpose=True, sem="wl")
    if stop_after == "B":
        return finish_early()

    if dbg:
        dma("sync", dbg_d["dbg_state"], f32(o_state, 512),
            reads=[b_state], writes=[buf("dbgst")], sem="out")
    b_logg = buf("logg")
    lg = f32(o_logg, H * nchunk, 0, 1)
    m127r = f32(o_m127, H * nchunk, 0, 1)
    vop("memset", lg[:, 0:H], 0.0, writes=[b_logg])
    for n in range(1, nchunk):
        vop("tensor_tensor", lg[:, n * H:(n + 1) * H],
            lg[:, (n - 1) * H:n * H], m127r[:, (n - 1) * H:n * H], op=ALU.add,
            reads=[b_logg, b_m127], writes=[b_logg])
    b_laa = buf("laa")
    vop("tensor_tensor", f32(o_scr + 90, H, 0, 1), lg[:, (nchunk - 1) * H:],
        m127r[:, (nchunk - 1) * H:], op=ALU.add, reads=[b_logg, b_m127],
        writes=[b_laa])
    dma("gpsimd", laa_in, f32(o_scr + 90, H, 0, 1), reads=[b_laa], writes=[buf("laad")])
    lag4 = lag_in.rearrange("(f h2) (p d) -> h2 p f d", h2=2, p=64)
    for h2 in range(2):
        dma("gpsimd", lag4[h2],
            f32(o_state, 512, h2 * 64, 64).rearrange("p (f d) -> p f d", d=64),
            reads=[b_state], writes=[buf("lagd")])
    coll("AllGather", ALU.bypass, [laa_in], [laa_out],
         reads=[buf("laad")], writes=[buf("laao")])
    coll("AllGather", ALU.bypass, [lag_in], [lag_out],
         reads=[buf("lagd")], writes=[buf("lago")])
    if stop_after == "C1":
        return finish_early()
    b_lgat = buf("lgat")
    lag3 = lag_out.rearrange("(b h) d -> h b d", h=H)
    for h in range(H):
        dma("sync", bf16(o_lgat, 2048, h * 8, 8), lag3[h],
            reads=[buf("lago")],
            writes=[b_lgat] + ([b_wcl, b_rl] if h == 0 else []), sem="ca")
    b_laat = buf("laat")
    laat = f32(o_scr + 108, 8, 0, H)
    dma("sync", laat, laa_out.rearrange("b h -> h b"),
        reads=[buf("laao")], writes=[b_laat], sem="ca")
    if stop_after == "C2":
        return finish_early()
    wlog = f32(o_scr + 64, 8, 0, H)
    vop("tensor_tensor", laat, laat, f32(o_masklt, 8, 0, H), op=ALU.mult,
        reads=[b_laat, b_const], writes=[b_laat])
    vop("memset", wlog[:, 7:8], 0.0, writes=[b_scr])
    for b in range(6, -1, -1):
        vop("tensor_tensor", wlog[:, b:b + 1], wlog[:, b + 1:b + 2],
            laat[:, b + 1:b + 2], op=ALU.add, reads=[b_scr, b_laat], writes=[b_scr])
    act(wlog, wlog, AF.Exp, reads=[b_scr], writes=[b_scr])
    vop("tensor_tensor", wlog, wlog, f32(o_masklt, 8, 0, H), op=ALU.mult,
        reads=[b_scr, b_const], writes=[b_scr])
    b_wcol = buf("wcol")
    dma("sync", f32(o_scr + 120, 1), wlog, reads=[b_scr], writes=[b_wcol], sem="ca")
    vop("tensor_tensor", bf16(o_scr + 122, 8), f32(o_idh, 16),
        f32(o_scr + 120, 1)[:, 0:1].broadcast_to([128, H]), op=ALU.mult,
        reads=[b_wcol, b_const], writes=[b_wcol])
    b_sinh = buf("sinh")
    for o in range(8):
        mm(PS[0][0:16, 0:512], bf16(o_scr + 122, 8),
           bf16(o_lgat, 2048)[:, o * 512:(o + 1) * 512],
           True, True, reads=[b_wcol, b_lgat], writes=[b_ps[0]])
        vop("tensor_copy", bf16(o_sinh, 2048, 0, H)[:, o * 512:(o + 1) * 512],
            PS[0][0:16, 0:512], reads=[b_ps[0]],
            writes=[b_sinh] + ([b_az, b_vt] if o == 0 else []))
    dma("gpsimd", sin_d, bf16(o_sinh, 2048, 0, H), reads=[b_sinh], writes=[buf("sind")])
    b_sinst = buf("sinst")
    sin4 = sin_d.rearrange("(f h2) (p d) -> h2 p f d", h2=2, p=64)
    for h2 in range(2):
        dma("sync",
            bf16(o_sinst, 256, h2 * 64, 64).rearrange("p (f d) -> p f d", d=64),
            sin4[h2], reads=[buf("sind")],
            writes=[b_sinst] + ([b_bt] if h2 == 0 else []), sem="ca")
    if dbg:
        b_dbgsin = buf("dbgsin")
        vop("tensor_copy", f32(o_yt, 512), bf16(o_sinst, 256),
            reads=[b_sinst], writes=[b_dbgsin, b_yt])
        dma("sync", dbg_d["dbg_sin"], f32(o_yt, 512),
            reads=[b_dbgsin, b_yt], writes=[buf("dbgst")], sem="out")
    if stop_after == "C3":
        return finish_early()

    b_cexp = buf("cexp"); b_x2 = buf("x2h")
    mu2 = f32(o_scr + 20, 2)
    for ch in range(nchunk):
        par = ch % 2
        if par == 0:
            o_y, yb = o_yt, [b_yt]
            o_d, o_e, eb = o_dse, o_et, [b_dse, b_et]
            o_x, xb_ = o_xt, [b_xt]
            muP, bntP, rsP, nbP, b_s = mu2, bnt, rs_col, f32(o_scr + 37, 1), b_scr
            b_ce = b_cexp
        else:
            o_y, yb = o_vb, [b_vb, b_vB]
            o_d, o_e, eb = o_kst, o_kst + 128, [b_kst]
            o_x, xb_ = o_kfmch, [b_kfmch, b_ktm]
            muP, bntP, rsP, nbP, b_s = mu_b, bnt_b, rs_b, f32(o_scr + 151, 1), b_scr2
            b_ce = buf("cexp2")
        mm(PS[1][:, 0:16], f32(o_onesf, 128, 0, 1), lg[:, ch * H:(ch + 1) * H],
           True, True, reads=[b_const, b_logg], writes=[b_ps[1]])
        vop("tensor_tensor", f32(o_d, 128)[:, 0:16], PS[1][:, 0:16],
            f32(o_m_all, H * nchunk)[:, ch * H:(ch + 1) * H], op=ALU.add,
            reads=[b_ps[1], b_m], writes=eb)
        act(f32(o_e, 128)[:, 0:16], f32(o_d, 128)[:, 0:16], AF.Exp,
            reads=eb, writes=[b_ce] + eb)
        dma("sync", f32(o_y, D), y_h[ch * C:(ch + 1) * C, :],
            reads=[b_yh], writes=yb, sem="ca" if par == 0 else "cb")
        for h in range(H):
            base = (h % 2) * 64
            f = h // 2
            qfm_sl = bf16(o_qfm, w_fm, base, 64)[:, f * TPC + ch * C: f * TPC + (ch + 1) * C]
            pc = PS[2] if h < 8 else PS[3]
            bpc = b_ps[2] if h < 8 else b_ps[3]
            mm(pc[:, (h % 8) * 64:(h % 8) * 64 + 64], qfm_sl,
               bf16(o_sinst, 256, base, 64)[:, f * 64:(f + 1) * 64],
               True, True, reads=[b_qfm, b_sinst], writes=[bpc], pbase=base)
        if stop_after == "C6" and ch == 0:
            return finish_early()
        for h in range(H):
            pc = PS[2] if h < 8 else PS[3]
            bpc = b_ps[2] if h < 8 else b_ps[3]
            ysl = f32(o_y, D)[:, h * 64:(h + 1) * 64]
            vop("scalar_tensor_tensor", ysl, pc[:, (h % 8) * 64:(h % 8) * 64 + 64],
                f32(o_e, 128)[:, h:h + 1], ysl, op0=ALU.mult, op1=ALU.add,
                reads=[bpc, b_ce] + yb[1:], writes=[yb[0]])
        if dbg:
            dma("sync", dbg_d["dbg_y"][ch * C:(ch + 1) * C, :], f32(o_y, D),
                reads=yb, writes=[buf("dbgst")], sem="out")
        vop("bn_stats", bntP[:, 0:6], f32(o_y, D)[:, 0:512], reads=yb, writes=[b_s])
        vop("bn_stats", bntP[:, 6:12], f32(o_y, D)[:, 512:1024], reads=yb, writes=[b_s])
        vop("bn_aggr", muP, bntP, reads=[b_s], writes=[b_s])
        act(rsP, muP[:, 1:2], AF.Sqrt, bias=cstc(3), scale=1.0,
            reads=[b_s, b_const], writes=[b_s])
        vop("reciprocal", rsP, rsP, reads=[b_s], writes=[b_s])
        vop("tensor_scalar", nbP, muP[:, 0:1], rsP, -1.0, op0=ALU.mult,
            op1=ALU.mult, reads=[b_s], writes=[b_s])
        act(f32(o_y, D), f32(o_y, D), AF.Identity, bias=nbP, scale=rsP,
            reads=yb + [b_s], writes=[yb[0]])
        dma("sync", f32(o_x, D), x_in[ch * C:(ch + 1) * C, :],
            writes=xb_, sem="ca" if par == 0 else "cb")
        vop("tensor_tensor", f32(o_x, D), f32(o_x, D), f32(o_y, D), op=ALU.add,
            reads=yb, writes=[xb_[0]])
        dma("sync", x2_h[ch * C:(ch + 1) * C, :], f32(o_x, D),
            reads=xb_, writes=[b_x2], sem="sa")
        if dbg:
            dma("sync", dbg_d["dbg_x2"][ch * C:(ch + 1) * C, :], f32(o_x, D),
                reads=xb_, writes=[buf("dbgst")], sem="out")

    if stop_after == "C":
        return finish_early()

    b_w2qT = buf("w2qT")
    for fi in range(32):
        dma("sync", bf16(o_x1fm + fi * 512, 512), t2_f[:, fi * 128:(fi + 1) * 128],
            reads=[b_t2f], writes=[b_w2qT, b_x1fm, b_qfm], transpose=True, sem="wl")

    b_b1t = buf("b1t"); b_b2t = buf("b2t")
    ones_row = bf16(o_ones32, 64)[0:1, 0:128]
    b1t_regions = [] if B1T_DED else [b_uhl]
    b2t_regions = [] if B1T_DED else [b_m127, b_logg]
    for o in range(8):
        mm(PS[4][:, 0:512], ones_row, bf16(o_b1row_bf, 2048, 0, 1)[0:1, o * 512:(o + 1) * 512],
           True, True, reads=[b_const, b_rowsbf], writes=[b_ps[4]])
        vop("tensor_copy", bf16(o_b1t, 2048)[:, o * 512:(o + 1) * 512], PS[4][:, 0:512],
            reads=[b_ps[4]], writes=[b_b1t] + (b1t_regions if o == 0 else []))
    for o in range(2):
        mm(PS[4][:, 0:512], ones_row, bf16(o_b2row_bf, 512, 0, 1)[0:1, o * 512:(o + 1) * 512],
           True, True, reads=[b_const, b_rowsbf], writes=[b_ps[4]])
        vop("tensor_copy", bf16(o_b2t, 512)[:, o * 512:(o + 1) * 512], PS[4][:, 0:512],
            reads=[b_ps[4]], writes=[b_b2t] + (b2t_regions if o == 0 else []))

    b_fx2 = buf("fx2"); b_fxr = buf("fxr"); b_fh = buf("fh")
    b_fge = buf("fge"); b_fxq1 = buf("fxq1"); b_fxq1fm = buf("fxq1fm")
    b_ffout = buf("ffout")
    b_outst = buf("outst")
    mu3 = f32(o_scr + 20, 2)
    bnt8 = f32(o_scr + 24, 48)
    for ch in range(nchunk):
        fx2 = f32(o_fx2, D)
        dma("sync", fx2, x2_h[ch * C:(ch + 1) * C, :],
            reads=[b_x2], writes=[b_fx2] + ([b_xt] if ch == 0 else []),
            sem="ca" if ch % 2 == 0 else "cb")
        vop("bn_stats", bnt[:, 0:6], fx2[:, 0:512], reads=[b_fx2], writes=[b_scr])
        vop("bn_stats", bnt[:, 6:12], fx2[:, 512:1024], reads=[b_fx2], writes=[b_scr])
        vop("bn_aggr", mu3, bnt, reads=[b_scr], writes=[b_scr])
        act(rs_col, mu3[:, 1:2], AF.Sqrt, bias=cstc(3), scale=1.0,
            reads=[b_scr, b_const], writes=[b_scr])
        vop("reciprocal", rs_col, rs_col, reads=[b_scr], writes=[b_scr])
        pe = f32(o_scr + 38, 1)
        vop("tensor_scalar", pe, mu3[:, 1:2], EPS, None, op0=ALU.add,
            reads=[b_scr], writes=[b_scr])
        pet = f32(o_scr + 52, 1)
        vop("reciprocal", pet, pe, reads=[b_scr], writes=[b_scr])
        vop("tensor_tensor", pe, mu3[:, 1:2], pet, op=ALU.mult,
            reads=[b_scr], writes=[b_scr])
        rho = f32(o_scr + 39, 1)
        act(rho, pe, AF.Sqrt, bias=cstc(3), scale=1.0, reads=[b_scr, b_const],
            writes=[b_scr])
        vop("reciprocal", rho, rho, reads=[b_scr], writes=[b_scr])
        rr = f32(o_scr + 40, 1)
        vop("tensor_tensor", rr, rs_col, rho, op=ALU.mult, reads=[b_scr], writes=[b_scr])
        fxr = f32(o_fxr, D)
        vop("tensor_scalar", fxr, fx2, mu3[:, 0:1], rr, op0=ALU.subtract,
            op1=ALU.mult, reads=[b_fx2, b_scr],
            writes=[b_fxr] + ([b_wcl, b_lgat] if ch == 0 else []))
        mx = f32(o_scr + 41, 1)
        vop("reduce_max", mx, fxr, axis=AX.X, apply_absolute_value=True,
            reads=[b_fxr], writes=[b_scr])
        vop("tensor_scalar", mx, mx, 1e-5, None, op0=ALU.max, reads=[b_scr], writes=[b_scr])
        scl = f32(o_scr + 42, 1)
        vop("reciprocal", scl, mx, reads=[b_scr], writes=[b_scr])
        vop("tensor_scalar", scl, scl, 127.0, None, op0=ALU.mult,
            reads=[b_scr], writes=[b_scr])
        stok = f32(o_scr + 43, 1)
        vop("tensor_scalar", stok, mx, 1.0 / 127, None, op0=ALU.mult,
            reads=[b_scr], writes=[b_scr])
        fI = f32(o_fge, D)
        vop("tensor_scalar", i32v(o_ffout, D), fxr, scl, None, op0=ALU.mult,
            reads=[b_fxr, b_scr],
            writes=[b_ffout] + ([b_yt] if ch == 0 else []))
        vop("tensor_copy", fI, i32v(o_ffout, D), reads=[b_ffout],
            writes=[b_fge] + ([b_vb, b_vB, b_kst, b_x1bf, b_kfmch] if ch == 0 else []))
        vop("scalar_tensor_tensor", f32(o_ffout, D), fxr, scl, fI,
            op0=ALU.mult, op1=ALU.subtract,
            reads=[b_fxr, b_scr, b_fge], writes=[b_ffout])
        vop("scalar_tensor_tensor", fI, f32(o_ffout, D), 0.5, fI,
            op0=ALU.is_gt, op1=ALU.add, reads=[b_ffout], writes=[b_fge])
        vop("tensor_scalar", f32(o_ffout, D), f32(o_ffout, D), -0.5, None,
            op0=ALU.is_lt, reads=[b_ffout], writes=[b_ffout])
        vop("tensor_tensor", fI, fI, f32(o_ffout, D), op=ALU.subtract,
            reads=[b_ffout], writes=[b_fge])
        vop("tensor_copy", bf16(o_fxq1, 512), fI,
            reads=[b_fge], writes=[b_fxq1] + ([b_ktm] if ch == 0 else []))
        if dbg:
            vop("tensor_scalar", f32(o_ffout, D), bf16(o_fxq1, 512), stok, None,
                op0=ALU.mult, reads=[b_fxq1, b_scr], writes=[b_ffout])
            dma("sync", dbg_d["dbg_xq1"][ch * C:(ch + 1) * C, :], f32(o_ffout, D),
                reads=[b_ffout], writes=[buf("dbgst")], sem="out")
        for fi in range(8):
            transp(PS[fi].bitcast(BF16)[:, 0:128],
                   bf16(o_fxq1, 512)[:, fi * 128:(fi + 1) * 128],
                   reads=[b_fxq1, b_const], writes=[b_ps[fi]])
        for fi in range(8):
            vop("tensor_copy", bf16(o_fxq1fm + fi * 64, 64),
                PS[fi].bitcast(BF16)[:, 0:128], reads=[b_ps[fi]],
                writes=[b_fxq1fm] + ([b_dse, b_et, b_smt] if ch == 0 and fi == 0 else []))
        for fi in range(8):
            lhsT = bf16(o_fxq1fm + fi * 64, 64)
            for b in range(8):
                mm(PS[b][:, 0:512], lhsT,
                   bf16(o_WS + fi * 2048, 2048)[:, b * 512:(b + 1) * 512],
                   start=(fi == 0), stop=(fi == 7),
                   reads=[b_fxq1fm, b_w1qT], writes=[b_ps[b]])
        sc1 = f32(o_scr + 44, 1)
        vop("tensor_tensor", sc1, stok, m1col, op=ALU.mult,
            reads=[b_scr, b_mcol], writes=[b_scr])
        fh = f32(o_fh, FF)
        for b in range(8):
            vop("scalar_tensor_tensor", fh[:, b * 512:(b + 1) * 512], PS[b][:, 0:512],
                sc1, bf16(o_b1t, 2048)[:, b * 512:(b + 1) * 512],
                op0=ALU.mult, op1=ALU.add, reads=[b_ps[b], b_scr, b_b1t],
                writes=[b_fh] + ([b_rl, b_az, b_vt, b_bt, b_lgat, b_sinh, b_sinst]
                                 if ch == 0 and b == 0 else []))
        for qq in range(4):
            act(fh[:, qq * 1024:(qq + 1) * 1024], fh[:, qq * 1024:(qq + 1) * 1024],
                AF.Gelu, reads=[b_fh], writes=[b_fh])
        if dbg and ch == 0:
            dma("sync", dbg_d["dbg_h"], fh, reads=[b_fh], writes=[buf("dbgst")], sem="out")
        for q8 in range(8):
            vop("bn_stats", bnt8[:, q8 * 6:q8 * 6 + 6], fh[:, q8 * 512:(q8 + 1) * 512],
                reads=[b_fh], writes=[b_scr])
        vop("bn_aggr", mu3, bnt8, reads=[b_scr], writes=[b_scr])
        msq = f32(o_scr + 45, 1)
        vop("tensor_scalar", msq, mu3[:, 0:1], mu3[:, 0:1], None, op0=ALU.mult,
            reads=[b_scr], writes=[b_scr])
        vop("tensor_tensor", msq, msq, mu3[:, 1:2], op=ALU.add,
            reads=[b_scr], writes=[b_scr])
        rhoh = f32(o_scr + 46, 1)
        act(rhoh, msq, AF.Sqrt, bias=cstc(3), scale=1.0, reads=[b_scr, b_const],
            writes=[b_scr])
        vop("reciprocal", rhoh, rhoh, reads=[b_scr], writes=[b_scr])
        mxh = f32(o_scr + 47, 1)
        vop("reduce_max", mxh, fh, axis=AX.X, apply_absolute_value=True,
            reads=[b_fh], writes=[b_scr])
        mxu = f32(o_scr + 41, 1)
        vop("tensor_tensor", mxu, mxh, rhoh, op=ALU.mult, reads=[b_scr], writes=[b_scr])
        vop("tensor_scalar", mxu, mxu, 1e-5, None, op0=ALU.max, reads=[b_scr], writes=[b_scr])
        sclh = f32(o_scr + 42, 1)
        vop("reciprocal", sclh, mxu, reads=[b_scr], writes=[b_scr])
        vop("tensor_scalar", sclh, sclh, 127.0, None, op0=ALU.mult,
            reads=[b_scr], writes=[b_scr])
        vop("tensor_tensor", sclh, sclh, rhoh, op=ALU.mult, reads=[b_scr], writes=[b_scr])
        stok2 = f32(o_scr + 43, 1)
        vop("tensor_scalar", stok2, mxu, 1.0 / 127, None, op0=ALU.mult,
            reads=[b_scr], writes=[b_scr])
        SCR2 = [b_fxq1, b_fxq1fm, b_ffout]
        for hh in range(2):
            fhH = fh[:, hh * 2048:(hh + 1) * 2048]
            dF = f32(o_fge, 2048)
            fS = f32(o_fxq1, 2048)
            vop("tensor_scalar", i32v(o_fge, 2048), fhH, sclh, None, op0=ALU.mult,
                reads=[b_fh, b_scr], writes=[b_fge])
            vop("tensor_copy", fS, i32v(o_fge, 2048), reads=[b_fge], writes=SCR2)
            vop("scalar_tensor_tensor", dF, fhH, sclh, fS,
                op0=ALU.mult, op1=ALU.subtract,
                reads=[b_fh, b_scr] + SCR2, writes=[b_fge])
            vop("scalar_tensor_tensor", fS, dF, 0.5, fS, op0=ALU.is_gt,
                op1=ALU.add, reads=[b_fge], writes=SCR2)
            vop("tensor_scalar", dF, dF, -0.5, None, op0=ALU.is_lt,
                reads=[b_fge], writes=[b_fge])
            vop("tensor_tensor", fS, fS, dF, op=ALU.subtract,
                reads=[b_fge], writes=SCR2)
            vop("tensor_copy", bf16(o_fh + hh * 1024, 1024), fS,
                reads=SCR2, writes=[b_fh])
        q2bf = bf16(o_fh, 2048)
        for g in range(4):
            for j in range(8):
                fi = g * 8 + j
                transp(PS[j].bitcast(BF16)[:, 0:128],
                       q2bf[:, fi * 128:(fi + 1) * 128],
                       reads=[b_fh, b_const], writes=[b_ps[j]])
            for j in range(8):
                fi = g * 8 + j
                vop("tensor_copy", bf16(o_fge + fi * 64, 64),
                    PS[j].bitcast(BF16)[:, 0:128], reads=[b_ps[j]],
                    writes=[b_fge])
        for fi in range(32):
            lhsT = bf16(o_fge + fi * 64, 64)
            mm(PS[0][:, 0:512], lhsT, bf16(o_x1fm + fi * 512, 512)[:, 0:512],
               start=(fi == 0), stop=(fi == 31), reads=[b_fge, b_w2qT],
               writes=[b_ps[0]])
            mm(PS[1][:, 0:512], lhsT, bf16(o_x1fm + fi * 512, 512)[:, 512:1024],
               start=(fi == 0), stop=(fi == 31), reads=[b_fge, b_w2qT],
               writes=[b_ps[1]])
        sc2 = f32(o_scr + 44, 1)
        vop("tensor_tensor", sc2, stok2, m2col, op=ALU.mult,
            reads=[b_scr, b_mcol], writes=[b_scr])
        ffo = f32(o_ffout, D)
        for b in range(2):
            vop("scalar_tensor_tensor", ffo[:, b * 512:(b + 1) * 512], PS[b][:, 0:512],
                sc2, bf16(o_b2t, 512)[:, b * 512:(b + 1) * 512],
                op0=ALU.mult, op1=ALU.add, reads=[b_ps[b], b_scr, b_b2t],
                writes=[b_ffout])
        vop("bn_stats", bnt[:, 0:6], ffo[:, 0:512], reads=[b_ffout], writes=[b_scr])
        vop("bn_stats", bnt[:, 6:12], ffo[:, 512:1024], reads=[b_ffout], writes=[b_scr])
        vop("bn_aggr", mu3, bnt, reads=[b_scr], writes=[b_scr])
        act(rs_col, mu3[:, 1:2], AF.Sqrt, bias=cstc(3), scale=1.0,
            reads=[b_scr, b_const], writes=[b_scr])
        vop("reciprocal", rs_col, rs_col, reads=[b_scr], writes=[b_scr])
        nb = f32(o_scr + 37, 1)
        vop("tensor_scalar", nb, mu3[:, 0:1], rs_col, -1.0, op0=ALU.mult,
            op1=ALU.mult, reads=[b_scr], writes=[b_scr])
        act(ffo, ffo, AF.Identity, bias=nb, scale=rs_col,
            reads=[b_scr], writes=[b_ffout])
        vop("tensor_tensor", fx2, fx2, ffo, op=ALU.add,
            reads=[b_ffout], writes=[b_fx2])
        dma("sync", out_d[ch * C:(ch + 1) * C, :], fx2, reads=[b_fx2],
            writes=[b_outst], sem="out")

    st_sem, st_val = b_outst.w
    if dbg:
        d_sem, d_val = buf("dbgst").w
        st_val = max(st_val, d_val)
    P.raw_wait("gpsimd", st_sem, st_val)
    P.raw_wait("scalar", st_sem, st_val)

    block = E(nc.Block())
    P.build(sems, block)
    ctx.close()
    return nc


def make_consts(core_id):
    triu = np.triu(np.ones((C, C))).astype(ml_dtypes.bfloat16)
    maskbig = np.where(np.triu(np.ones((C, C))) > 0, 0.0, -30000.0).astype(np.float32)
    idt = np.eye(C, dtype=ml_dtypes.bfloat16)
    ones32 = np.ones((C, C), dtype=ml_dtypes.bfloat16)
    onesf = np.ones((1, C), dtype=np.float32)
    idh = np.zeros((C, H), dtype=np.float32)
    for h in range(H):
        for b in range(NCORES):
            idh[h * 8 + b, h] = 1.0
    masklt = np.zeros((H, NCORES), dtype=np.float32)
    masklt[:, :core_id] = 1.0
    cst = np.zeros((C, 8), dtype=np.float32)
    cst[:, 0] = 1.0
    cst[:, 1] = 127.0
    cst[:, 2] = 1.0 / 127.0
    cst[:, 3] = EPS
    cst[:, 4] = 0.125
    cst[:, 5] = 1.0 / 64.0
    return dict(triu=triu, maskbig=maskbig, idt=idt, ones32=ones32, onesf=onesf,
                idh=idh, masklt=masklt, cst=cst)


_NC_CACHE = {}


def kernel(L0_attn_x, L0_attn_Wq, L0_attn_bq, L0_attn_Wk, L0_attn_bk,
           L0_attn_Wv, L0_attn_bv, L0_attn_Wbeta, L0_attn_bbeta,
           L0_attn_Walpha, L0_attn_balpha, L0_ffn_w1, L0_ffn_b1,
           L0_ffn_w2, L0_ffn_b2):
    x = np.ascontiguousarray(np.asarray(L0_attn_x, dtype=np.float32)[0])
    T = x.shape[0]
    nchunk = T // (NCORES * C)
    key = nchunk
    if key not in _NC_CACHE:
        _NC_CACHE[key] = build_program(nchunk=nchunk)
    nc = _NC_CACHE[key]
    TPC = nchunk * C
    com = dict(
        wq=np.asarray(L0_attn_Wq, np.float32), bq=np.asarray(L0_attn_bq, np.float32),
        wk=np.asarray(L0_attn_Wk, np.float32), bk=np.asarray(L0_attn_bk, np.float32),
        wv=np.asarray(L0_attn_Wv, np.float32), bv=np.asarray(L0_attn_bv, np.float32),
        wb=np.asarray(L0_attn_Wbeta, np.float32), bb=np.asarray(L0_attn_bbeta, np.float32),
        wa=np.asarray(L0_attn_Walpha, np.float32), ba=np.asarray(L0_attn_balpha, np.float32),
        b1=np.asarray(L0_ffn_b1, np.float32), b2=np.asarray(L0_ffn_b2, np.float32),
    )
    w1 = np.asarray(L0_ffn_w1, np.float32)
    w2 = np.asarray(L0_ffn_w2, np.float32)
    in_maps = []
    for cid in range(NCORES):
        m = dict(com)
        m["x"] = np.ascontiguousarray(x[cid * TPC:(cid + 1) * TPC])
        m["w1s"] = np.ascontiguousarray(w1[cid * (FF // NCORES):(cid + 1) * (FF // NCORES)])
        m["w2s"] = np.ascontiguousarray(w2[cid * (D // NCORES):(cid + 1) * (D // NCORES)])
        m.update(make_consts(cid))
        in_maps.append(m)
    res = run_bass_kernel_spmd(nc, in_maps, core_ids=list(range(NCORES)))
    out = np.concatenate([np.asarray(r["out"], np.float32) for r in res.results], axis=0)
    return out.reshape(1, T, D)


# version 15 (final)
# speedup vs baseline: 1.0188x; 1.0160x over previous
import numpy as np
import ml_dtypes
from contextlib import ExitStack
import concourse.bass as bass
import concourse.mybir as mybir
from concourse.bass_utils import run_bass_kernel_spmd

F32 = mybir.dt.float32
BF16 = mybir.dt.bfloat16
I32 = mybir.dt.int32
AF = mybir.ActivationFunctionType
ALU = mybir.AluOpType
AX = mybir.AxisListType

NCORES = 8
D = 1024
H = 16
HD = 64
C = 128
FF = 4096
EPS = 1e-5
ENGS = ("sync", "gpsimd", "tensor", "vector", "scalar")
ESEM = {"sync": ("ld", 16), "gpsimd": ("gq", 16), "tensor": ("te", 1),
        "vector": ("ve", 1), "scalar": ("ae", 1)}
ASYNC = ("ld", "xa", "xb", "wl", "ca", "cb", "sa", "sb", "out", "gq", "cc")
ALLSEMS = ("ld", "xa", "xb", "wl", "ca", "cb", "sa", "sb", "out", "gq", "cc", "te", "ve", "ae")


class Buf:
    __slots__ = ("w", "r", "name")

    def __init__(self, name=""):
        self.name = name
        self.w = None
        self.r = []


class Prog:
    def __init__(self, nc):
        self.nc = nc
        self.ops = {e: [] for e in ENGS}
        self.cnt = {s: 0 for s in ALLSEMS}
        self.wm = {e: {} for e in ENGS}
        self.bar = {s: 0 for s in ASYNC}
        self.handles = {}

    def _wait(self, eng, sem, val):
        if val <= 0:
            return
        if self.wm[eng].get(sem, 0) >= val:
            return
        self.wm[eng][sem] = val
        self.ops[eng].append(("wait", sem, val))

    def op(self, eng, fn, reads=(), writes=(), sem=None, step=None,
           no_self_wait=False, pbase=None):
        if sem is None:
            sem, step = ESEM[eng]
        if pbase is not None:
            if getattr(self, "_last_pbase", None) not in (None, pbase):
                self._wait(eng, "te", self.cnt["te"])
            self._last_pbase = pbase
        waits = {}
        for b in reads:
            if b.w is not None:
                s, v = b.w
                if waits.get(s, 0) < v:
                    waits[s] = v
        for b in writes:
            if b.w is not None:
                s, v = b.w
                if waits.get(s, 0) < v:
                    waits[s] = v
            for s, v in b.r:
                if waits.get(s, 0) < v:
                    waits[s] = v
        own = ESEM[eng][0]
        for s, v in sorted(waits.items()):
            if no_self_wait and s == own:
                continue
            if s in self.bar:
                v2 = self.cnt[s]
                if s != sem:
                    self.bar[s] = max(self.bar[s], v2)
                self._wait(eng, s, v2)
            else:
                self._wait(eng, s, v)
        if sem in self.bar and self.bar[sem] > 0:
            self._wait(eng, sem, self.bar[sem])
        self.cnt[sem] += step
        val = self.cnt[sem]
        self.ops[eng].append(("op", fn, sem, step))
        for b in writes:
            b.w = (sem, val)
            b.r = []
        for b in reads:
            b.r.append((sem, val))
        return (sem, val)

    def raw_wait(self, eng, sem, val):
        self._wait(eng, sem, val)

    def build(self, sems, block):
        self.handles = sems

        def body(e, eng_name):
            for item in self.ops[eng_name]:
                if item[0] == "wait":
                    e.wait_ge(self.handles[item[1]], item[2])
                else:
                    _, fn, sem, step = item
                    ins = fn(e)
                    ins.then_inc(self.handles[sem], step)

        block.sync(lambda e: body(e, "sync"))
        block.gpsimd(lambda e: body(e, "gpsimd"))
        block.tensor(lambda e: body(e, "tensor"))
        block.vector(lambda e: body(e, "vector"))
        block.scalar(lambda e: body(e, "scalar"))


def ceil_div(a, b):
    return (a + b - 1) // b


def build_program(nchunk=16, dbg=False, stop_after=None):
    TPC = nchunk * C
    nc = bass.Bass("TRN2", target_bir_lowering=False)
    P = Prog(nc)

    def din(name, shape, dt=F32):
        return nc.dram_tensor(name, shape, dt, kind="ExternalInput").ap()

    x_in = din("x", [TPC, D])
    Wq_in = din("wq", [D, D]); bq_in = din("bq", [D])
    Wk_in = din("wk", [D, D]); bk_in = din("bk", [D])
    Wv_in = din("wv", [D, D]); bv_in = din("bv", [D])
    Wb_in = din("wb", [D, D]); bb_in = din("bb", [D])
    Wa_in = din("wa", [D, D]); ba_in = din("ba", [D])
    w1s_in = din("w1s", [FF // NCORES, D])
    b1_in = din("b1", [FF])
    w2s_in = din("w2s", [D // NCORES, FF])
    b2_in = din("b2", [D])
    triu_in = din("triu", [C, C], BF16)
    maskbig_in = din("maskbig", [C, C])
    idt_in = din("idt", [C, C], BF16)
    ones32_in = din("ones32", [C, C], BF16)
    onesf_in = din("onesf", [1, C])
    idh_in = din("idh", [C, H])
    masklt_in = din("masklt", [H, NCORES])
    cst_in = din("cst", [C, 8])

    out_d = nc.dram_tensor("out", [TPC, D], F32, kind="ExternalOutput").ap()
    dbg_d = {}
    if dbg:
        dbg_d["dbg_m"] = nc.dram_tensor("dbg_m", [C, H * nchunk], F32, kind="ExternalOutput").ap()
        dbg_d["dbg_y"] = nc.dram_tensor("dbg_y", [TPC, D], F32, kind="ExternalOutput").ap()
        dbg_d["dbg_x2"] = nc.dram_tensor("dbg_x2", [TPC, D], F32, kind="ExternalOutput").ap()
        dbg_d["dbg_h"] = nc.dram_tensor("dbg_h", [C, FF], F32, kind="ExternalOutput").ap()
        dbg_d["dbg_xq1"] = nc.dram_tensor("dbg_xq1", [TPC, D], F32, kind="ExternalOutput").ap()
        dbg_d["dbg_state"] = nc.dram_tensor("dbg_state", [C, 512], F32, kind="ExternalOutput").ap()
        dbg_d["dbg_sin"] = nc.dram_tensor("dbg_sin", [C, 512], F32, kind="ExternalOutput").ap()

    def dint(name, shape, dt=F32):
        return nc.dram_tensor(name, shape, dt).ap()

    w_bf = {n: dint(f"{n}_bf", [D, D], BF16) for n in ("wq", "wk", "wv", "wb", "wa")}
    x1_h = dint("x1_h", [TPC, D], BF16)
    kfm_h = dint("kfm_h", [D, TPC], BF16)
    y_h = dint("y_h", [TPC, D], F32)
    x2_h = dint("x2_h", [TPC, D], F32)
    t1_sl = dint("t1_sl", [FF // NCORES, D], BF16)
    t1_f = dint("t1_f", [FF, D], BF16)
    t2_sl = dint("t2_sl", [D // NCORES, FF], BF16)
    t2_f = dint("t2_f", [D, FF], BF16)
    ws_in_d = dint("ws_in_d", [1, 2])
    ws_out_d = dint("ws_out_d", [1, 2])
    lag_in = dint("lag_in", [H, HD * HD], BF16)
    lag_out = dint("lag_out", [NCORES * H, HD * HD], BF16)
    laa_in = dint("laa_in", [1, H])
    laa_out = dint("laa_out", [NCORES, H])
    sin_d = dint("sin_d", [H, HD * HD], BF16)

    ctx = ExitStack()
    E = ctx.enter_context

    AR_W = 53100
    AR = E(nc.sbuf_tensor("AR", [128, AR_W], F32))

    off = [0]

    def alloc(words):
        o = off[0]
        off[0] += words
        assert off[0] <= AR_W, f"arena overflow {off[0]}"
        return o

    def f32(o, w, p0=0, pn=128):
        return AR[p0:p0 + pn, o:o + w]

    def bf16(o, w, p0=0, pn=128):
        return AR.bitcast(BF16)[p0:p0 + pn, 2 * o:2 * o + 2 * w]

    def i32v(o, w, p0=0, pn=128):
        return AR.bitcast(I32)[p0:p0 + pn, o:o + w]

    o_triu = alloc(64)
    o_maskbig = alloc(128)
    o_idt = alloc(64)
    o_ones32 = alloc(64)
    o_onesf = alloc(128)
    o_idh = alloc(16)
    o_masklt = alloc(8)
    o_cst = alloc(8)
    o_bvrow_bf = alloc(512)
    o_bbrow_bf = alloc(512)
    o_barow_bf = alloc(512)
    o_b1row_bf = alloc(2048)
    o_b2row_bf = alloc(512)
    o_bqcol = alloc(8)
    o_bkcol = alloc(8)
    o_m_all = alloc(H * nchunk)
    o_curve = alloc(H * nchunk)
    o_uhl = alloc(nchunk * C)
    o_m127 = alloc(H * nchunk)
    o_logg = alloc(H * nchunk)
    o_acol = alloc(H * nchunk)
    o_scr = alloc(160)
    o_state = alloc(512)
    o_statebf = alloc(256)

    R3_W = 10240
    o_R3 = alloc(R3_W)
    o_WS = alloc(16384)
    w_fm = 8 * TPC // 2
    o_x1fm = alloc(w_fm)
    o_qfm = alloc(w_fm)
    if 2 * w_fm < 16384:
        alloc(16384 - 2 * w_fm)
    if nchunk * C >= 2048:
        o_b1t = o_uhl
        o_b2t = o_uhl + 2048
        B1T_DED = False
    else:
        o_b1t = alloc(2048)
        o_b2t = alloc(512)
        B1T_DED = True
    assert off[0] <= AR_W, f"arena {off[0]}"

    scs = 1
    assert nchunk % scs == 0
    r = o_R3
    o_xt = r; r += 1024
    o_wcl = r; r += 1024
    o_rl = r; r += 1024
    o_az = r; r += 1024
    o_vt = r; r += 1024
    o_bt = r; r += 1024
    o_vb = r; r += 512 * scs
    o_vB = r; r += 512 * scs
    o_kst = r; r += 512
    o_kfmch = r; r += 512
    o_ktm = r; r += 512
    o_dse = r; r += 128
    o_et = r; r += 128
    o_smt = r; r += 64
    o_yt = r; r += 1024
    assert r <= o_R3 + R3_W, f"attn R3 overflow {r - o_R3}"
    o_x1bf = o_kst
    o_lgat = o_wcl
    o_sinh = o_az
    o_sinst = o_bt
    o_wslice = o_xt
    o_wge = o_WS

    o_fx2 = o_R3
    o_fxr = o_fx2 + 1024
    o_fh = o_fxr + 1024
    o_fge = o_fh + 4096
    o_fxq1 = o_fge + 2048
    o_fxq1fm = o_fxq1 + 512
    o_ffout = o_fxq1fm + 512
    assert o_ffout + 1024 <= o_R3 + R3_W

    PS = [E(nc.psum_tensor(f"ps{i}", [128, 512], F32)) for i in range(8)]

    sems = {}
    for s in ALLSEMS:
        sems[s] = E(nc.semaphore(s))

    B = {}

    def buf(name):
        if name not in B:
            B[name] = Buf(name)
        return B[name]

    def dma(eng, out, in_, reads=(), writes=(), transpose=False, sem="ld"):
        if eng == "gpsimd":
            sem = "gq"
        if transpose:
            return P.op(eng, lambda e, o=out, i=in_: e.dma_start(out=o, in_=i, transpose=True),
                        reads=reads, writes=writes, sem=sem, step=16)
        return P.op(eng, lambda e, o=out, i=in_: e.dma_start(
                        out=o, in_=i, allow_slow_non_contiguous=True),
                    reads=reads, writes=writes, sem=sem, step=16)

    def mm(out, lhsT, rhs, start, stop, reads=(), writes=(), pbase=0):
        return P.op("tensor",
                    lambda e, o=out, l=lhsT, r=rhs, s0=start, s1=stop:
                    e.matmul(o, l, r, start=s0, stop=s1),
                    reads=reads, writes=writes, no_self_wait=True, pbase=pbase)

    def transp(out, in_, reads=(), writes=()):
        ident = bf16(o_idt, 64)
        return P.op("tensor",
                    lambda e, o=out, i=in_, d=ident: e.transpose(o, i, d),
                    reads=reads, writes=writes, no_self_wait=True, pbase=0)

    def act(out, in_, func, bias=0.0, scale=1.0, reads=(), writes=()):
        return P.op("scalar",
                    lambda e, o=out, i=in_, f=func, b=bias, s=scale:
                    e.activation(o, i, f, bias=b, scale=s),
                    reads=reads, writes=writes)

    def vop(name, *args, reads=(), writes=(), **kw):
        return P.op("vector",
                    lambda e, a=args, k=kw, n=name: getattr(e, n)(*a, **k),
                    reads=reads, writes=writes)

    def coll(kind, alu, ins, outs, reads=(), writes=()):
        return P.op("gpsimd", lambda e, k=kind, a=alu, i=ins, o=outs:
                    e.collective_compute(k, a, replica_groups=[list(range(NCORES))],
                                         ins=i, outs=o),
                    reads=reads, writes=writes, sem="cc", step=1)

    def cstc(i, pn=128, p0=0):
        return f32(o_cst, 8, p0, pn)[:, i:i + 1]

    def finish_early():
        b_o = buf("outst")
        for ch_ in range(nchunk):
            dma("sync", out_d[ch_ * C:(ch_ + 1) * C, :], f32(o_xt, D),
                reads=[buf("xt")], writes=[b_o], sem="out")
        s_, v_ = b_o.w
        P.raw_wait("gpsimd", s_, v_)
        P.raw_wait("scalar", s_, v_)
        blk = E(nc.Block())
        P.build(sems, blk)
        ctx.close()
        return nc

    b_const = buf("const")
    dma("sync", bf16(o_triu, 64), triu_in, writes=[b_const])
    dma("sync", f32(o_maskbig, 128), maskbig_in, writes=[b_const])
    dma("sync", bf16(o_idt, 64), idt_in, writes=[b_const])
    dma("sync", bf16(o_ones32, 64), ones32_in, writes=[b_const])
    dma("sync", f32(o_onesf, 128, 0, 1), onesf_in, writes=[b_const])
    dma("sync", f32(o_idh, 16), idh_in, writes=[b_const])
    dma("sync", f32(o_masklt, 8, 0, H), masklt_in, writes=[b_const])
    dma("sync", f32(o_cst, 8), cst_in, writes=[b_const])
    b_rowsbf = buf("rowsbf")
    dma("gpsimd", bf16(o_bvrow_bf, 512, 0, 1), bv_in[None, :], writes=[b_rowsbf])
    dma("gpsimd", bf16(o_bbrow_bf, 512, 0, 1), bb_in[None, :], writes=[b_rowsbf])
    dma("gpsimd", bf16(o_barow_bf, 512, 0, 1), ba_in[None, :], writes=[b_rowsbf])
    dma("gpsimd", bf16(o_b1row_bf, 2048, 0, 1), b1_in[None, :], writes=[b_rowsbf])
    dma("gpsimd", bf16(o_b2row_bf, 512, 0, 1), b2_in[None, :], writes=[b_rowsbf])
    b_bcol = buf("bcol")
    dma("sync", f32(o_bqcol, 8), bq_in.rearrange("(a p) -> p a", p=128), writes=[b_bcol])
    dma("sync", f32(o_bkcol, 8), bk_in.rearrange("(a p) -> p a", p=128), writes=[b_bcol])
    vop("tensor_scalar", f32(o_bqcol, 8), f32(o_bqcol, 8), 0.125, None,
        op0=ALU.mult, reads=[b_bcol], writes=[b_bcol])
    vop("tensor_scalar", f32(o_bkcol, 8), f32(o_bkcol, 8), 0.125, None,
        op0=ALU.mult, reads=[b_bcol], writes=[b_bcol])

    b_wbf = {n: buf(f"wbf_{n}") for n in w_bf}
    for n, w_in in (("wa", Wa_in), ("wq", Wq_in), ("wk", Wk_in), ("wv", Wv_in), ("wb", Wb_in)):
        dma("gpsimd", w_bf[n], w_in, writes=[b_wbf[n]])

    b_xt = buf("xt"); b_x1bf = buf("x1bf"); b_wcl = buf("wcl")
    b_rl = buf("rl"); b_az = buf("az")
    RWsl = [b_xt, b_wcl, b_rl, b_az]
    b_WS0 = buf("WS0")
    b_scr = buf("scr")
    b_wsum = buf("wsum")
    b_wsrow = buf("wsrow")
    dma("sync", f32(o_wslice, FF, 0, 128).rearrange("p (a q) -> p a q", a=4),
        w1s_in.rearrange("(a p) q -> p a q", p=128), writes=RWsl)
    vop("reduce_sum", f32(o_scr, 1), f32(o_wslice, FF), axis=AX.X,
        apply_absolute_value=True, reads=RWsl, writes=[b_wsum])
    dma("sync", f32(o_scr + 8, 128, 0, 1), f32(o_scr, 1), reads=[b_wsum], writes=[b_wsrow])
    vop("reduce_sum", f32(o_scr + 4, 1, 0, 1), f32(o_scr + 8, 128, 0, 1), axis=AX.X,
        reads=[b_wsrow], writes=[b_wsrow])
    vop("tensor_copy", f32(o_scr + 6, 1, 0, 1), f32(o_scr + 4, 1, 0, 1),
        reads=[b_wsrow], writes=[b_wsum])
    b_wsl2 = buf("wslice2")
    dma("sync", f32(o_wslice, FF, 0, 128), w2s_in, writes=RWsl + [b_wsl2])
    vop("reduce_sum", f32(o_scr, 1), f32(o_wslice, FF), axis=AX.X,
        apply_absolute_value=True, reads=RWsl, writes=[b_wsum])
    dma("sync", f32(o_scr + 8, 128, 0, 1), f32(o_scr, 1), reads=[b_wsum], writes=[b_wsrow])
    vop("reduce_sum", f32(o_scr + 7, 1, 0, 1), f32(o_scr + 8, 128, 0, 1), axis=AX.X,
        reads=[b_wsrow], writes=[b_wsum])
    dma("gpsimd", ws_in_d, f32(o_scr + 6, 2, 0, 1), reads=[b_wsum], writes=[buf("wsd")])
    coll("AllReduce", ALU.add, [ws_in_d], [ws_out_d],
         reads=[buf("wsd")], writes=[buf("wsd_o")])
    dma("gpsimd", f32(o_scr + 10, 2, 0, 1), ws_out_d, reads=[buf("wsd_o")], writes=[b_wsum])
    b_mcol = buf("mcol")
    vop("tensor_scalar", f32(o_scr + 12, 2, 0, 1), f32(o_scr + 10, 2, 0, 1),
        1.0 / (FF * D), 1e-5, op0=ALU.mult, op1=ALU.max, reads=[b_wsum], writes=[b_wsum])
    b_ps = [buf(f"ps{i}") for i in range(8)]
    mm(PS[0][:, 0:2], f32(o_onesf, 128, 0, 1), f32(o_scr + 12, 2, 0, 1),
       True, True, reads=[b_const, b_wsum], writes=[b_ps[0]])
    vop("tensor_copy", f32(o_scr + 14, 2), PS[0][:, 0:2], reads=[b_ps[0]], writes=[b_mcol])
    m1col = f32(o_scr + 14, 2)[:, 0:1]
    m2col = f32(o_scr + 14, 2)[:, 1:2]
    vop("reciprocal", f32(o_scr + 16, 2), f32(o_scr + 14, 2),
        reads=[b_mcol], writes=[b_mcol])
    s1wcol = f32(o_scr + 16, 2)[:, 0:1]
    s2wcol = f32(o_scr + 16, 2)[:, 1:2]

    def quant_slice(scol, src_bufs, dst_dram, t_buf, src_view=None):
        vop("tensor_scalar", f32(o_wslice, FF), f32(o_wslice, FF), scol, None,
            op0=ALU.mult, reads=src_bufs + [b_mcol], writes=src_bufs)
        vop("tensor_copy", i32v(o_wge, FF), f32(o_wslice, FF),
            reads=src_bufs, writes=[b_WS0])
        vop("tensor_copy", f32(o_wge + FF, FF), i32v(o_wge, FF),
            reads=[b_WS0], writes=[b_WS0])
        vop("tensor_tensor", f32(o_wge, FF), f32(o_wslice, FF), f32(o_wge + FF, FF),
            op=ALU.subtract, reads=[b_WS0] + src_bufs, writes=[b_WS0])
        vop("scalar_tensor_tensor", f32(o_wge + FF, FF), f32(o_wge, FF), 0.5,
            f32(o_wge + FF, FF), op0=ALU.is_gt, op1=ALU.add,
            reads=[b_WS0], writes=[b_WS0])
        vop("tensor_scalar", f32(o_wge, FF), f32(o_wge, FF), -0.5, None,
            op0=ALU.is_lt, reads=[b_WS0], writes=[b_WS0])
        vop("tensor_tensor", f32(o_wge + FF, FF), f32(o_wge + FF, FF),
            f32(o_wge, FF), op=ALU.subtract, reads=[b_WS0], writes=[b_WS0])
        vop("tensor_scalar", bf16(o_wslice, FF // 2), f32(o_wge + FF, FF), 1.0, -1.0,
            op0=ALU.min, op1=ALU.max, reads=[b_WS0], writes=src_bufs)
        sv = bf16(o_wslice, FF // 2) if src_view is None else src_view
        dma("gpsimd", dst_dram, sv, reads=src_bufs, writes=[t_buf])

    b_t1sl = buf("t1sl"); b_t2sl = buf("t2sl")
    quant_slice(s2wcol, RWsl, t2_sl, b_t2sl)
    dma("sync", f32(o_wslice, FF, 0, 128).rearrange("p (a q) -> p a q", a=4),
        w1s_in.rearrange("(a p) q -> p a q", p=128), writes=RWsl)
    quant_slice(s1wcol, RWsl, t1_sl.rearrange("(a p) q -> p a q", p=128), b_t1sl,
                src_view=bf16(o_wslice, FF // 2).rearrange("p (a q) -> p a q", a=4))
    b_t1f = buf("t1f"); b_t2f = buf("t2f")
    coll("AllGather", ALU.bypass, [t2_sl], [t2_f], reads=[b_t2sl], writes=[b_t2f])
    coll("AllGather", ALU.bypass, [t1_sl], [t1_f], reads=[b_t1sl], writes=[b_t1f])
    if stop_after == "setup":
        return finish_early()

    b_x1h = buf("x1h")
    mu = f32(o_scr + 20, 2)
    bnt = f32(o_scr + 24, 12)
    rs_col = f32(o_scr + 36, 1)

    b_scr2 = buf("scr2")
    mu_b = f32(o_scr + 136, 2)
    bnt_b = f32(o_scr + 138, 12)
    rs_b = f32(o_scr + 150, 1)
    for n in range(nchunk):
        par = n % 2
        o_x = o_xt if par == 0 else o_wcl
        b_x = b_xt if par == 0 else b_wcl
        o_xb = o_x1bf if par == 0 else o_rl
        b_xb = b_x1bf if par == 0 else b_rl
        muP, bntP, rsP = (mu, bnt, rs_col) if par == 0 else (mu_b, bnt_b, rs_b)
        b_s = b_scr if par == 0 else b_scr2
        xt = f32(o_x, D)
        dma("sync", xt, x_in[n * C:(n + 1) * C, :], writes=[b_x],
            sem="xa" if par == 0 else "xb")
        vop("bn_stats", bntP[:, 0:6], xt[:, 0:512], reads=[b_x], writes=[b_s])
        vop("bn_stats", bntP[:, 6:12], xt[:, 512:1024], reads=[b_x], writes=[b_s])
        vop("bn_aggr", muP, bntP, reads=[b_s], writes=[b_s])
        act(rsP, muP[:, 1:2], AF.Sqrt, bias=cstc(3), scale=1.0,
            reads=[b_s, b_const], writes=[b_s])
        vop("reciprocal", rsP, rsP, reads=[b_s], writes=[b_s])
        vop("tensor_scalar", bf16(o_xb, 512), xt, muP[:, 0:1], rsP,
            op0=ALU.subtract, op1=ALU.mult, reads=[b_x, b_s], writes=[b_xb])
        dma("sync", x1_h[n * C:(n + 1) * C, :], bf16(o_xb, 512),
            reads=[b_xb], writes=[b_x1h], sem="sa")

    b_x1fm = buf("x1fm")
    for fi in range(8):
        dma("sync", bf16(o_x1fm + fi * (TPC // 2), TPC // 2),
            x1_h[:, fi * 128:(fi + 1) * 128], reads=[b_x1h], writes=[b_x1fm],
            transpose=True, sem="xa")

    def wt_slot(slot, fi):
        return bf16(o_WS + slot * 4096 + fi * 512, 512)

    b_wsl3 = buf("wslot3")
    for fi in range(8):
        dma("sync", wt_slot(3, fi), w_bf["wa"][:, fi * 128:(fi + 1) * 128],
            reads=[b_wbf["wa"]], writes=[b_wsl3, b_WS0], transpose=True, sem="wl")

    b_m = buf("m_all"); b_uhl = buf("uhl"); b_m127 = buf("m127")
    b_curve = buf("curve"); b_acol = buf("acol")
    la_bf = bf16(o_scr + 40, 8)
    la_f = f32(o_scr + 48, 16)
    uh_f = f32(o_scr + 64, 16)
    uh_i = i32v(o_scr + 80, 16)
    uh_bf = bf16(o_scr + 82, 8)
    ul_bf = bf16(o_scr + 90, 8)

    for n in range(nchunk):
        for fi in range(8):
            lhsT = bf16(o_x1fm, w_fm)[:, fi * TPC + n * C: fi * TPC + (n + 1) * C]
            for hf in range(2):
                mm(PS[hf][:, 0:512], lhsT,
                   wt_slot(3, fi)[:, hf * 512:(hf + 1) * 512],
                   start=(fi == 0), stop=False,
                   reads=[b_x1fm, b_wsl3], writes=[b_ps[hf]])
        ones_row = bf16(o_ones32, 64)[0:1, 0:128]
        for hf in range(2):
            mm(PS[hf][:, 0:512], ones_row,
               bf16(o_barow_bf, 512, 0, 1)[0:1, hf * 512:(hf + 1) * 512],
               start=False, stop=True,
               reads=[b_const, b_rowsbf], writes=[b_ps[hf]])
        for hf in range(2):
            sl = slice(hf * 512, (hf + 1) * 512)
            vop("tensor_scalar", f32(o_wcl, D)[:, sl], PS[hf][:, 0:512], 10.0, -10.0,
                op0=ALU.min, op1=ALU.max, reads=[b_ps[hf]], writes=[b_wcl])
            act(f32(o_rl, D)[:, sl], f32(o_wcl, D)[:, sl], AF.Relu, scale=-1.0,
                reads=[b_wcl], writes=[b_rl])
            act(f32(o_az, D)[:, sl], f32(o_wcl, D)[:, sl], AF.Abs,
                reads=[b_wcl], writes=[b_az])
            act(f32(o_az, D)[:, sl], f32(o_az, D)[:, sl], AF.Exp, scale=-1.0,
                reads=[b_az], writes=[b_az])
            act(f32(o_az, D)[:, sl], f32(o_az, D)[:, sl], AF.Ln, bias=cstc(0),
                reads=[b_az, b_const], writes=[b_az])
        vop("tensor_tensor", f32(o_rl, D), f32(o_rl, D), f32(o_az, D), op=ALU.add,
            reads=[b_rl, b_az], writes=[b_rl])
        vop("reduce_sum", la_f.rearrange("p (h o) -> p h o", o=1),
            f32(o_rl, D).rearrange("p (h d) -> p h d", h=H), axis=AX.X,
            reads=[b_rl], writes=[b_scr])
        vop("tensor_scalar", la_bf, la_f, -1.0 / 64, None, op0=ALU.mult,
            reads=[b_scr], writes=[b_scr])
        mm(PS[2][:, 0:16], bf16(o_triu, 64), la_bf, True, True,
           reads=[b_const, b_scr], writes=[b_ps[2]])
        mslice = f32(o_m_all, H * nchunk)[:, n * H:(n + 1) * H]
        vop("tensor_copy", mslice, PS[2][:, 0:16], reads=[b_ps[2]], writes=[b_m])
        vop("tensor_scalar", uh_i, mslice, 0.125, None, op0=ALU.mult,
            reads=[b_m], writes=[b_scr])
        vop("tensor_scalar", uh_f, uh_i, 8.0, None, op0=ALU.mult,
            reads=[b_scr], writes=[b_scr])
        vop("tensor_copy", uh_bf, uh_f, reads=[b_scr], writes=[b_scr])
        vop("tensor_tensor", ul_bf, mslice, uh_f, op=ALU.subtract,
            reads=[b_m, b_scr], writes=[b_scr])
        uhl = bf16(o_uhl, nchunk * C, 0, H)
        transp(PS[3].bitcast(BF16)[0:16, 0:128], uh_bf,
               reads=[b_scr, b_const], writes=[b_ps[3]])
        vop("tensor_copy", uhl[:, n * 256:n * 256 + 128],
            PS[3].bitcast(BF16)[0:16, 0:128], reads=[b_ps[3]], writes=[b_uhl])
        transp(PS[3].bitcast(BF16)[0:16, 128:256], ul_bf,
               reads=[b_scr, b_const], writes=[b_ps[3]])
        vop("tensor_copy", uhl[:, n * 256 + 128:n * 256 + 256],
            PS[3].bitcast(BF16)[0:16, 128:256], reads=[b_ps[3]], writes=[b_uhl])
        act(f32(o_curve, H * nchunk)[:, n * H:(n + 1) * H], mslice, AF.Exp,
            reads=[b_m], writes=[b_curve])
        dma("sync", f32(o_m127, H * nchunk, 0, 1)[:, n * H:(n + 1) * H],
            mslice[127:128, :], reads=[b_m], writes=[b_m127], sem="ca")
        mm(PS[2][:, 16:32], f32(o_onesf, 128, 0, 1),
           f32(o_m127, H * nchunk, 0, 1)[:, n * H:(n + 1) * H],
           True, True, reads=[b_const, b_m127], writes=[b_ps[2]])
        act(f32(o_acol, H * nchunk)[:, n * H:(n + 1) * H],
            PS[2][:, 16:32], AF.Exp, reads=[b_ps[2]], writes=[b_acol])

    if dbg:
        dma("sync", dbg_d["dbg_m"], f32(o_m_all, H * nchunk),
            reads=[b_m], writes=[buf("dbgst")], sem="out")
    if stop_after == "A":
        return finish_early()

    b_wslot = [buf(f"wslot{i}") for i in range(4)]
    for slot, wn in ((0, "wq"), (1, "wk")):
        for fi in range(8):
            dma("sync", wt_slot(slot, fi), w_bf[wn][:, fi * 128:(fi + 1) * 128],
                reads=[b_wbf[wn]], writes=[b_wslot[slot], b_WS0], transpose=True,
                sem="wl")

    b_qfm = buf("qfm"); b_kst = buf("kst"); b_kst2 = buf("kst2")
    b_kfmh = buf("kfmh"); b_kfmh2 = buf("kfmh2")
    B_KFMH = [b_kfmh, b_kfmh2]
    groups = [(g * 512, min(512, TPC - g * 512)) for g in range(ceil_div(TPC, 512))]
    ng = len(groups)
    assert ng <= 4
    for wi, (slot, bcol_off) in enumerate(((0, o_bqcol), (1, o_bkcol))):
        for f in range(8):
            for fi in range(8):
                lhsT = wt_slot(slot, fi)[:, f * 128:(f + 1) * 128]
                for g, (gs, gw) in enumerate(groups):
                    mm(PS[g][:, 0:gw], lhsT,
                       bf16(o_x1fm, w_fm)[:, fi * TPC + gs: fi * TPC + gs + gw],
                       start=(fi == 0), stop=(fi == 7),
                       reads=[b_wslot[slot], b_x1fm], writes=[b_ps[g]])
            for g, (gs, gw) in enumerate(groups):
                bcol = f32(bcol_off, 8)[:, f:f + 1]
                if wi == 0:
                    act(bf16(o_qfm, w_fm)[:, f * TPC + gs: f * TPC + gs + gw],
                        PS[g][:, 0:gw], AF.Identity, bias=bcol, scale=0.125,
                        reads=[b_ps[g], b_bcol], writes=[b_qfm])
                else:
                    kp = (f * ng + g) % 2
                    o_ks = o_kst if kp == 0 else o_yt
                    bks = b_kst if kp == 0 else b_kst2
                    extra = [b_x1bf] if f == 0 and g == 0 else (
                        [buf("yt")] if f == 0 and g == 1 else [])
                    act(bf16(o_ks, 512)[:, 0:gw],
                        PS[g][:, 0:gw], AF.Identity, bias=bcol, scale=0.125,
                        reads=[b_ps[g], b_bcol], writes=[bks] + extra)
                    dma("sync", kfm_h[f * 128:(f + 1) * 128, gs:gs + gw],
                        bf16(o_ks, 512)[:, 0:gw], reads=[bks],
                        writes=[B_KFMH[kp]], sem="sa" if kp == 0 else "sb")

    for slot, wn in ((2, "wv"), (3, "wb")):
        for fi in range(8):
            dma("sync", wt_slot(slot, fi), w_bf[wn][:, fi * 128:(fi + 1) * 128],
                reads=[b_wbf[wn]],
                writes=[b_wslot[slot]] + ([b_wsl3] if slot == 3 else []),
                transpose=True, sem="wl")

    if stop_after == "B0":
        return finish_early()
    b_state = buf("state"); b_statebf = buf("statebf")
    vop("memset", f32(o_state, 512), 0.0, writes=[b_state])
    vop("memset", bf16(o_statebf, 256), 0.0, writes=[b_statebf])

    b_vt = buf("vt"); b_bt = buf("bt"); b_vb = buf("vb"); b_vB = buf("vB")
    b_kfmch = buf("kfmch"); b_ktm = buf("ktm")
    b_dse = buf("dse"); b_et = buf("et"); b_smt = buf("smt"); b_yt = buf("yt")
    b_yh = buf("y_h")
    bm16 = f32(o_scr + 48, 16)

    nsc = nchunk // scs
    for sc in range(nsc):
        for n4 in range(scs):
            ch = sc * scs + n4
            for fi in range(8):
                lhsT = bf16(o_x1fm, w_fm)[:, fi * TPC + ch * C: fi * TPC + (ch + 1) * C]
                for hf in range(2):
                    mm(PS[4 + hf][:, 0:512], lhsT,
                       wt_slot(2, fi)[:, hf * 512:(hf + 1) * 512],
                       start=(fi == 0), stop=False,
                       reads=[b_x1fm, b_wslot[2]], writes=[b_ps[4 + hf]])
                    mm(PS[6 + hf][:, 0:512], lhsT,
                       wt_slot(3, fi)[:, hf * 512:(hf + 1) * 512],
                       start=(fi == 0), stop=False,
                       reads=[b_x1fm, b_wslot[3]], writes=[b_ps[6 + hf]])
            ones_row = bf16(o_ones32, 64)[0:1, 0:128]
            for hf in range(2):
                mm(PS[4 + hf][:, 0:512], ones_row,
                   bf16(o_bvrow_bf, 512, 0, 1)[0:1, hf * 512:(hf + 1) * 512],
                   start=False, stop=True, reads=[b_const, b_rowsbf],
                   writes=[b_ps[4 + hf]])
                mm(PS[6 + hf][:, 0:512], ones_row,
                   bf16(o_bbrow_bf, 512, 0, 1)[0:1, hf * 512:(hf + 1) * 512],
                   start=False, stop=True, reads=[b_const, b_rowsbf],
                   writes=[b_ps[6 + hf]])
            for hf in range(2):
                sl = slice(hf * 512, (hf + 1) * 512)
                vop("tensor_copy", f32(o_vt, D)[:, sl], PS[4 + hf][:, 0:512],
                    reads=[b_ps[4 + hf]], writes=[b_vt])
                act(f32(o_rl, D)[:, sl], PS[6 + hf][:, 0:512], AF.Relu,
                    reads=[b_ps[6 + hf]], writes=[b_rl])
                act(f32(o_az, D)[:, sl], PS[6 + hf][:, 0:512], AF.Abs,
                    reads=[b_ps[6 + hf]], writes=[b_az])
                act(f32(o_az, D)[:, sl], f32(o_az, D)[:, sl], AF.Exp, scale=-1.0,
                    reads=[b_az], writes=[b_az])
                act(f32(o_az, D)[:, sl], f32(o_az, D)[:, sl], AF.Ln, bias=cstc(0),
                    reads=[b_az, b_const], writes=[b_az])
            vop("tensor_tensor", f32(o_bt, D), f32(o_rl, D), f32(o_az, D),
                op=ALU.add, reads=[b_rl, b_az], writes=[b_bt])
            vop("tensor_tensor", bf16(o_vb + n4 * 512, 512), f32(o_vt, D),
                f32(o_bt, D), op=ALU.mult, reads=[b_vt, b_bt], writes=[b_vb])
            vop("reduce_sum", bm16.rearrange("p (h o) -> p h o", o=1),
                f32(o_bt, D).rearrange("p (h d) -> p h d", h=H), axis=AX.X,
                reads=[b_bt], writes=[b_scr])
            vop("tensor_scalar", bm16, bm16, 1.0 / 64, None, op0=ALU.mult,
                reads=[b_scr], writes=[b_scr])
            for h in range(H):
                vop("tensor_scalar",
                    bf16(o_vB + n4 * 512, 512)[:, h * 64:(h + 1) * 64],
                    f32(o_vt, D)[:, h * 64:(h + 1) * 64], bm16[:, h:h + 1], None,
                    op0=ALU.mult, reads=[b_vt, b_scr], writes=[b_vB])

        for n4 in range(scs):
            ch = sc * scs + n4
            csem = "ca" if ch % 2 == 0 else "cb"
            dma("sync", bf16(o_kfmch, 512).rearrange("p (a j) -> p a j", a=8),
                kfm_h.rearrange("(a p) t -> p a t", p=128)[:, :, ch * C:(ch + 1) * C],
                reads=B_KFMH, writes=[b_kfmch], sem=csem)
            dma("sync", bf16(o_ktm, 512), kfm_h[:, ch * C:(ch + 1) * C],
                reads=B_KFMH, writes=[b_ktm], transpose=True, sem=csem)
            uhl = bf16(o_uhl, nchunk * C, 0, H)
            dma("sync", bf16(o_xt, 1024, 0, 1), uhl[0:8, ch * 256:(ch + 1) * 256],
                reads=[b_uhl], writes=[b_xt], sem=csem)
            dma("sync", bf16(o_xt, 1024, 64, 1), uhl[8:16, ch * 256:(ch + 1) * 256],
                reads=[b_uhl], writes=[b_xt], sem=csem)
            for h in range(H):
                base = (h % 2) * 64
                f = h // 2
                kfm_sl = bf16(o_kfmch, 512, base, 64)[:, f * 128:(f + 1) * 128]
                qfm_sl = bf16(o_qfm, w_fm, base, 64)[:, f * TPC + ch * C: f * TPC + (ch + 1) * C]
                pd = PS[6] if base == 0 else PS[7]
                bpd = b_ps[6] if base == 0 else b_ps[7]
                mm(pd[base:base + 64, f * 64:(f + 1) * 64],
                   bf16(o_ktm, 512)[:, h * 64:(h + 1) * 64],
                   bf16(o_vB + n4 * 512, 512)[:, h * 64:(h + 1) * 64],
                   True, True, reads=[b_ktm, b_vB], writes=[bpd])
                half = 0 if h < 8 else 64
                hb = h % 8
                ones_h = bf16(o_ones32, 64)[half:half + 1, 0:128]
                u_row = bf16(o_xt, 1024, half, 1)
                mm(PS[1][:, 0:128], ones_h, u_row[:, hb * 256:hb * 256 + 128],
                   True, False, reads=[b_const, b_xt], writes=[b_ps[1]], pbase=half)
                mm(PS[1][:, 0:128], ones_h, u_row[:, hb * 256 + 128:hb * 256 + 256],
                   False, True, reads=[b_const, b_xt], writes=[b_ps[1]], pbase=half)
                mm(PS[0][:, 0:128], kfm_sl, qfm_sl, True, True,
                   reads=[b_kfmch, b_qfm], writes=[b_ps[0]], pbase=base)
                pm = PS[4] if h < 8 else PS[5]
                bpm = b_ps[4] if h < 8 else b_ps[5]
                mm(pm[:, (h % 8) * 64:(h % 8) * 64 + 64], qfm_sl,
                   bf16(o_statebf, 256, base, 64)[:, f * 64:(f + 1) * 64],
                   True, True, reads=[b_qfm, b_statebf], writes=[bpm], pbase=base)
                vop("scalar_tensor_tensor", f32(o_dse, 128), PS[1][:, 0:128],
                    f32(o_m_all, H * nchunk)[:, ch * H + h:ch * H + h + 1],
                    f32(o_maskbig, 128), op0=ALU.subtract, op1=ALU.add,
                    reads=[b_ps[1], b_m, b_const], writes=[b_dse])
                act(f32(o_et, 128), f32(o_dse, 128), AF.Exp,
                    reads=[b_dse], writes=[b_et])
                vop("tensor_tensor", bf16(o_smt, 64), PS[0][:, 0:128],
                    f32(o_et, 128), op=ALU.mult,
                    reads=[b_ps[0], b_et], writes=[b_smt])
                py = PS[2] if h < 8 else PS[3]
                bpy = b_ps[2] if h < 8 else b_ps[3]
                mm(py[:, (h % 8) * 64:(h % 8) * 64 + 64], bf16(o_smt, 64),
                   bf16(o_vb + n4 * 512, 512)[:, h * 64:(h + 1) * 64],
                   True, True, reads=[b_smt, b_vb], writes=[bpy])
            vop("tensor_copy", f32(o_yt, D)[:, 0:512], PS[2][:, 0:512],
                reads=[b_ps[2]], writes=[b_yt])
            vop("tensor_copy", f32(o_yt, D)[:, 512:1024], PS[3][:, 0:512],
                reads=[b_ps[3]], writes=[b_yt])
            for h in range(H):
                pm = PS[4] if h < 8 else PS[5]
                bpm = b_ps[4] if h < 8 else b_ps[5]
                sl = slice((h % 8) * 64, (h % 8) * 64 + 64)
                ysl = f32(o_yt, D)[:, h * 64:(h + 1) * 64]
                vop("scalar_tensor_tensor", ysl, pm[:, sl],
                    f32(o_curve, H * nchunk)[:, ch * H + h:ch * H + h + 1],
                    ysl, op0=ALU.mult, op1=ALU.add,
                    reads=[bpm, b_curve], writes=[b_yt])
            dma("sync", y_h[ch * C:(ch + 1) * C, :], f32(o_yt, D),
                reads=[b_yt], writes=[b_yh], sem="sa")
            for h in range(H):
                base = (h % 2) * 64
                f = h // 2
                pd = PS[6] if base == 0 else PS[7]
                bpd = b_ps[6] if base == 0 else b_ps[7]
                sl = slice(f * 64, (f + 1) * 64)
                st_sl = f32(o_state, 512, base, 64)[:, sl]
                vop("scalar_tensor_tensor", st_sl, st_sl,
                    f32(o_acol, H * nchunk, base, 64)[:, ch * H + h:ch * H + h + 1],
                    pd[base:base + 64, sl], op0=ALU.mult, op1=ALU.add,
                    reads=[b_state, b_acol, bpd], writes=[b_state])
            vop("tensor_copy", bf16(o_statebf, 256), f32(o_state, 512),
                reads=[b_state], writes=[b_statebf])

    b_w1qT = buf("w1qT")
    for fi in range(8):
        dma("sync", bf16(o_WS + fi * 2048, 2048), t1_f[:, fi * 128:(fi + 1) * 128],
            reads=[b_t1f],
            writes=[b_w1qT, b_wsl3, b_WS0] + b_wslot, transpose=True, sem="wl")
    if stop_after == "B":
        return finish_early()

    if dbg:
        dma("sync", dbg_d["dbg_state"], f32(o_state, 512),
            reads=[b_state], writes=[buf("dbgst")], sem="out")
    b_logg = buf("logg")
    lg = f32(o_logg, H * nchunk, 0, 1)
    m127r = f32(o_m127, H * nchunk, 0, 1)
    vop("memset", lg[:, 0:H], 0.0, writes=[b_logg])
    for n in range(1, nchunk):
        vop("tensor_tensor", lg[:, n * H:(n + 1) * H],
            lg[:, (n - 1) * H:n * H], m127r[:, (n - 1) * H:n * H], op=ALU.add,
            reads=[b_logg, b_m127], writes=[b_logg])
    b_laa = buf("laa")
    vop("tensor_tensor", f32(o_scr + 90, H, 0, 1), lg[:, (nchunk - 1) * H:],
        m127r[:, (nchunk - 1) * H:], op=ALU.add, reads=[b_logg, b_m127],
        writes=[b_laa])
    dma("gpsimd", laa_in, f32(o_scr + 90, H, 0, 1), reads=[b_laa], writes=[buf("laad")])
    lag4 = lag_in.rearrange("(f h2) (p d) -> h2 p f d", h2=2, p=64)
    for h2 in range(2):
        dma("gpsimd", lag4[h2],
            f32(o_state, 512, h2 * 64, 64).rearrange("p (f d) -> p f d", d=64),
            reads=[b_state], writes=[buf("lagd")])
    coll("AllGather", ALU.bypass, [laa_in], [laa_out],
         reads=[buf("laad")], writes=[buf("laao")])
    coll("AllGather", ALU.bypass, [lag_in], [lag_out],
         reads=[buf("lagd")], writes=[buf("lago")])
    if stop_after == "C1":
        return finish_early()
    b_lgat = buf("lgat")
    lag3 = lag_out.rearrange("(b h) d -> h b d", h=H)
    for h in range(H):
        dma("sync", bf16(o_lgat, 2048, h * 8, 8), lag3[h],
            reads=[buf("lago")],
            writes=[b_lgat] + ([b_wcl, b_rl] if h == 0 else []), sem="ca")
    b_laat = buf("laat")
    laat = f32(o_scr + 108, 8, 0, H)
    dma("sync", laat, laa_out.rearrange("b h -> h b"),
        reads=[buf("laao")], writes=[b_laat], sem="ca")
    if stop_after == "C2":
        return finish_early()
    wlog = f32(o_scr + 64, 8, 0, H)
    vop("tensor_tensor", laat, laat, f32(o_masklt, 8, 0, H), op=ALU.mult,
        reads=[b_laat, b_const], writes=[b_laat])
    vop("memset", wlog[:, 7:8], 0.0, writes=[b_scr])
    for b in range(6, -1, -1):
        vop("tensor_tensor", wlog[:, b:b + 1], wlog[:, b + 1:b + 2],
            laat[:, b + 1:b + 2], op=ALU.add, reads=[b_scr, b_laat], writes=[b_scr])
    act(wlog, wlog, AF.Exp, reads=[b_scr], writes=[b_scr])
    vop("tensor_tensor", wlog, wlog, f32(o_masklt, 8, 0, H), op=ALU.mult,
        reads=[b_scr, b_const], writes=[b_scr])
    b_wcol = buf("wcol")
    dma("sync", f32(o_scr + 120, 1), wlog, reads=[b_scr], writes=[b_wcol], sem="ca")
    vop("tensor_tensor", bf16(o_scr + 122, 8), f32(o_idh, 16),
        f32(o_scr + 120, 1)[:, 0:1].broadcast_to([128, H]), op=ALU.mult,
        reads=[b_wcol, b_const], writes=[b_wcol])
    b_sinh = buf("sinh")
    for o in range(8):
        mm(PS[0][0:16, 0:512], bf16(o_scr + 122, 8),
           bf16(o_lgat, 2048)[:, o * 512:(o + 1) * 512],
           True, True, reads=[b_wcol, b_lgat], writes=[b_ps[0]])
        vop("tensor_copy", bf16(o_sinh, 2048, 0, H)[:, o * 512:(o + 1) * 512],
            PS[0][0:16, 0:512], reads=[b_ps[0]],
            writes=[b_sinh] + ([b_az, b_vt] if o == 0 else []))
    dma("gpsimd", sin_d, bf16(o_sinh, 2048, 0, H), reads=[b_sinh], writes=[buf("sind")])
    b_sinst = buf("sinst")
    sin4 = sin_d.rearrange("(f h2) (p d) -> h2 p f d", h2=2, p=64)
    for h2 in range(2):
        dma("sync",
            bf16(o_sinst, 256, h2 * 64, 64).rearrange("p (f d) -> p f d", d=64),
            sin4[h2], reads=[buf("sind")],
            writes=[b_sinst] + ([b_bt] if h2 == 0 else []), sem="ca")
    if dbg:
        b_dbgsin = buf("dbgsin")
        vop("tensor_copy", f32(o_yt, 512), bf16(o_sinst, 256),
            reads=[b_sinst], writes=[b_dbgsin, b_yt])
        dma("sync", dbg_d["dbg_sin"], f32(o_yt, 512),
            reads=[b_dbgsin, b_yt], writes=[buf("dbgst")], sem="out")
    if stop_after == "C3":
        return finish_early()

    b_cexp = buf("cexp"); b_x2 = buf("x2h")
    mu2 = f32(o_scr + 20, 2)
    for ch in range(nchunk):
        par = ch % 2
        if par == 0:
            o_y, yb = o_yt, [b_yt]
            o_d, o_e, eb = o_dse, o_et, [b_dse, b_et]
            o_x, xb_ = o_xt, [b_xt]
            muP, bntP, rsP, nbP, b_s = mu2, bnt, rs_col, f32(o_scr + 37, 1), b_scr
            b_ce = b_cexp
        else:
            o_y, yb = o_vb, [b_vb, b_vB]
            o_d, o_e, eb = o_kst, o_kst + 128, [b_kst]
            o_x, xb_ = o_kfmch, [b_kfmch, b_ktm]
            muP, bntP, rsP, nbP, b_s = mu_b, bnt_b, rs_b, f32(o_scr + 151, 1), b_scr2
            b_ce = buf("cexp2")
        mm(PS[1][:, 0:16], f32(o_onesf, 128, 0, 1), lg[:, ch * H:(ch + 1) * H],
           True, True, reads=[b_const, b_logg], writes=[b_ps[1]])
        vop("tensor_tensor", f32(o_d, 128)[:, 0:16], PS[1][:, 0:16],
            f32(o_m_all, H * nchunk)[:, ch * H:(ch + 1) * H], op=ALU.add,
            reads=[b_ps[1], b_m], writes=eb)
        act(f32(o_e, 128)[:, 0:16], f32(o_d, 128)[:, 0:16], AF.Exp,
            reads=eb, writes=[b_ce] + eb)
        dma("sync", f32(o_y, D), y_h[ch * C:(ch + 1) * C, :],
            reads=[b_yh], writes=yb, sem="ca" if par == 0 else "cb")
        for h in range(H):
            base = (h % 2) * 64
            f = h // 2
            qfm_sl = bf16(o_qfm, w_fm, base, 64)[:, f * TPC + ch * C: f * TPC + (ch + 1) * C]
            pc = PS[2] if h < 8 else PS[3]
            bpc = b_ps[2] if h < 8 else b_ps[3]
            mm(pc[:, (h % 8) * 64:(h % 8) * 64 + 64], qfm_sl,
               bf16(o_sinst, 256, base, 64)[:, f * 64:(f + 1) * 64],
               True, True, reads=[b_qfm, b_sinst], writes=[bpc], pbase=base)
        if stop_after == "C6" and ch == 0:
            return finish_early()
        for h in range(H):
            pc = PS[2] if h < 8 else PS[3]
            bpc = b_ps[2] if h < 8 else b_ps[3]
            ysl = f32(o_y, D)[:, h * 64:(h + 1) * 64]
            vop("scalar_tensor_tensor", ysl, pc[:, (h % 8) * 64:(h % 8) * 64 + 64],
                f32(o_e, 128)[:, h:h + 1], ysl, op0=ALU.mult, op1=ALU.add,
                reads=[bpc, b_ce] + yb[1:], writes=[yb[0]])
        if dbg:
            dma("sync", dbg_d["dbg_y"][ch * C:(ch + 1) * C, :], f32(o_y, D),
                reads=yb, writes=[buf("dbgst")], sem="out")
        vop("bn_stats", bntP[:, 0:6], f32(o_y, D)[:, 0:512], reads=yb, writes=[b_s])
        vop("bn_stats", bntP[:, 6:12], f32(o_y, D)[:, 512:1024], reads=yb, writes=[b_s])
        vop("bn_aggr", muP, bntP, reads=[b_s], writes=[b_s])
        act(rsP, muP[:, 1:2], AF.Sqrt, bias=cstc(3), scale=1.0,
            reads=[b_s, b_const], writes=[b_s])
        vop("reciprocal", rsP, rsP, reads=[b_s], writes=[b_s])
        vop("tensor_scalar", nbP, muP[:, 0:1], rsP, -1.0, op0=ALU.mult,
            op1=ALU.mult, reads=[b_s], writes=[b_s])
        act(f32(o_y, D), f32(o_y, D), AF.Identity, bias=nbP, scale=rsP,
            reads=yb + [b_s], writes=[yb[0]])
        dma("sync", f32(o_x, D), x_in[ch * C:(ch + 1) * C, :],
            writes=xb_, sem="ca" if par == 0 else "cb")
        vop("tensor_tensor", f32(o_x, D), f32(o_x, D), f32(o_y, D), op=ALU.add,
            reads=yb, writes=[xb_[0]])
        dma("sync", x2_h[ch * C:(ch + 1) * C, :], f32(o_x, D),
            reads=xb_, writes=[b_x2], sem="sa")
        if dbg:
            dma("sync", dbg_d["dbg_x2"][ch * C:(ch + 1) * C, :], f32(o_x, D),
                reads=xb_, writes=[buf("dbgst")], sem="out")

    if stop_after == "C":
        return finish_early()

    b_w2qT = buf("w2qT")
    for fi in range(32):
        dma("sync", bf16(o_x1fm + fi * 512, 512), t2_f[:, fi * 128:(fi + 1) * 128],
            reads=[b_t2f], writes=[b_w2qT, b_x1fm, b_qfm], transpose=True, sem="wl")

    b_b1t = buf("b1t"); b_b2t = buf("b2t")
    ones_row = bf16(o_ones32, 64)[0:1, 0:128]
    b1t_regions = [] if B1T_DED else [b_uhl]
    b2t_regions = [] if B1T_DED else [b_m127, b_logg]
    for o in range(8):
        mm(PS[4][:, 0:512], ones_row, bf16(o_b1row_bf, 2048, 0, 1)[0:1, o * 512:(o + 1) * 512],
           True, True, reads=[b_const, b_rowsbf], writes=[b_ps[4]])
        vop("tensor_copy", bf16(o_b1t, 2048)[:, o * 512:(o + 1) * 512], PS[4][:, 0:512],
            reads=[b_ps[4]], writes=[b_b1t] + (b1t_regions if o == 0 else []))
    for o in range(2):
        mm(PS[4][:, 0:512], ones_row, bf16(o_b2row_bf, 512, 0, 1)[0:1, o * 512:(o + 1) * 512],
           True, True, reads=[b_const, b_rowsbf], writes=[b_ps[4]])
        vop("tensor_copy", bf16(o_b2t, 512)[:, o * 512:(o + 1) * 512], PS[4][:, 0:512],
            reads=[b_ps[4]], writes=[b_b2t] + (b2t_regions if o == 0 else []))

    b_fx2 = buf("fx2"); b_fxr = buf("fxr"); b_fh = buf("fh")
    b_fge = buf("fge"); b_fxq1 = buf("fxq1"); b_fxq1fm = buf("fxq1fm")
    b_ffout = buf("ffout")
    b_outst = buf("outst")
    mu3 = f32(o_scr + 20, 2)
    bnt8 = f32(o_scr + 24, 48)
    for ch in range(nchunk):
        fx2 = f32(o_fx2, D)
        dma("sync", fx2, x2_h[ch * C:(ch + 1) * C, :],
            reads=[b_x2], writes=[b_fx2] + ([b_xt] if ch == 0 else []),
            sem="ca" if ch % 2 == 0 else "cb")
        vop("bn_stats", bnt[:, 0:6], fx2[:, 0:512], reads=[b_fx2], writes=[b_scr])
        vop("bn_stats", bnt[:, 6:12], fx2[:, 512:1024], reads=[b_fx2], writes=[b_scr])
        vop("bn_aggr", mu3, bnt, reads=[b_scr], writes=[b_scr])
        act(rs_col, mu3[:, 1:2], AF.Sqrt, bias=cstc(3), scale=1.0,
            reads=[b_scr, b_const], writes=[b_scr])
        vop("reciprocal", rs_col, rs_col, reads=[b_scr], writes=[b_scr])
        pe = f32(o_scr + 38, 1)
        vop("tensor_scalar", pe, mu3[:, 1:2], EPS, None, op0=ALU.add,
            reads=[b_scr], writes=[b_scr])
        pet = f32(o_scr + 52, 1)
        vop("reciprocal", pet, pe, reads=[b_scr], writes=[b_scr])
        vop("tensor_tensor", pe, mu3[:, 1:2], pet, op=ALU.mult,
            reads=[b_scr], writes=[b_scr])
        rho = f32(o_scr + 39, 1)
        act(rho, pe, AF.Sqrt, bias=cstc(3), scale=1.0, reads=[b_scr, b_const],
            writes=[b_scr])
        vop("reciprocal", rho, rho, reads=[b_scr], writes=[b_scr])
        rr = f32(o_scr + 40, 1)
        vop("tensor_tensor", rr, rs_col, rho, op=ALU.mult, reads=[b_scr], writes=[b_scr])
        fxr = f32(o_fxr, D)
        vop("tensor_scalar", fxr, fx2, mu3[:, 0:1], rr, op0=ALU.subtract,
            op1=ALU.mult, reads=[b_fx2, b_scr],
            writes=[b_fxr] + ([b_wcl, b_lgat] if ch == 0 else []))
        mx = f32(o_scr + 41, 1)
        vop("reduce_max", mx, fxr, axis=AX.X, apply_absolute_value=True,
            reads=[b_fxr], writes=[b_scr])
        vop("tensor_scalar", mx, mx, 1e-5, None, op0=ALU.max, reads=[b_scr], writes=[b_scr])
        scl = f32(o_scr + 42, 1)
        vop("reciprocal", scl, mx, reads=[b_scr], writes=[b_scr])
        vop("tensor_scalar", scl, scl, 127.0, None, op0=ALU.mult,
            reads=[b_scr], writes=[b_scr])
        stok = f32(o_scr + 43, 1)
        vop("tensor_scalar", stok, mx, 1.0 / 127, None, op0=ALU.mult,
            reads=[b_scr], writes=[b_scr])
        fI = f32(o_fge, D)
        vop("tensor_scalar", i32v(o_ffout, D), fxr, scl, None, op0=ALU.mult,
            reads=[b_fxr, b_scr],
            writes=[b_ffout] + ([b_yt] if ch == 0 else []))
        vop("tensor_copy", fI, i32v(o_ffout, D), reads=[b_ffout],
            writes=[b_fge] + ([b_vb, b_vB, b_kst, b_x1bf, b_kfmch] if ch == 0 else []))
        vop("scalar_tensor_tensor", f32(o_ffout, D), fxr, scl, fI,
            op0=ALU.mult, op1=ALU.subtract,
            reads=[b_fxr, b_scr, b_fge], writes=[b_ffout])
        vop("scalar_tensor_tensor", fI, f32(o_ffout, D), 0.5, fI,
            op0=ALU.is_gt, op1=ALU.add, reads=[b_ffout], writes=[b_fge])
        vop("tensor_scalar", f32(o_ffout, D), f32(o_ffout, D), -0.5, None,
            op0=ALU.is_lt, reads=[b_ffout], writes=[b_ffout])
        vop("tensor_tensor", fI, fI, f32(o_ffout, D), op=ALU.subtract,
            reads=[b_ffout], writes=[b_fge])
        vop("tensor_copy", bf16(o_fxq1, 512), fI,
            reads=[b_fge], writes=[b_fxq1] + ([b_ktm] if ch == 0 else []))
        if dbg:
            vop("tensor_scalar", f32(o_ffout, D), bf16(o_fxq1, 512), stok, None,
                op0=ALU.mult, reads=[b_fxq1, b_scr], writes=[b_ffout])
            dma("sync", dbg_d["dbg_xq1"][ch * C:(ch + 1) * C, :], f32(o_ffout, D),
                reads=[b_ffout], writes=[buf("dbgst")], sem="out")
        for fi in range(8):
            transp(PS[fi].bitcast(BF16)[:, 0:128],
                   bf16(o_fxq1, 512)[:, fi * 128:(fi + 1) * 128],
                   reads=[b_fxq1, b_const], writes=[b_ps[fi]])
        for fi in range(8):
            vop("tensor_copy", bf16(o_fxq1fm + fi * 64, 64),
                PS[fi].bitcast(BF16)[:, 0:128], reads=[b_ps[fi]],
                writes=[b_fxq1fm] + ([b_dse, b_et, b_smt] if ch == 0 and fi == 0 else []))
        for fi in range(8):
            lhsT = bf16(o_fxq1fm + fi * 64, 64)
            for b in range(8):
                mm(PS[b][:, 0:512], lhsT,
                   bf16(o_WS + fi * 2048, 2048)[:, b * 512:(b + 1) * 512],
                   start=(fi == 0), stop=(fi == 7),
                   reads=[b_fxq1fm, b_w1qT], writes=[b_ps[b]])
        sc1 = f32(o_scr + 44, 1)
        vop("tensor_tensor", sc1, stok, m1col, op=ALU.mult,
            reads=[b_scr, b_mcol], writes=[b_scr])
        fh = f32(o_fh, FF)
        for b in range(8):
            vop("scalar_tensor_tensor", fh[:, b * 512:(b + 1) * 512], PS[b][:, 0:512],
                sc1, bf16(o_b1t, 2048)[:, b * 512:(b + 1) * 512],
                op0=ALU.mult, op1=ALU.add, reads=[b_ps[b], b_scr, b_b1t],
                writes=[b_fh] + ([b_rl, b_az, b_vt, b_bt, b_lgat, b_sinh, b_sinst]
                                 if ch == 0 and b == 0 else []))
        for qq in range(4):
            act(fh[:, qq * 1024:(qq + 1) * 1024], fh[:, qq * 1024:(qq + 1) * 1024],
                AF.Gelu, reads=[b_fh], writes=[b_fh])
        if dbg and ch == 0:
            dma("sync", dbg_d["dbg_h"], fh, reads=[b_fh], writes=[buf("dbgst")], sem="out")
        for q8 in range(8):
            vop("bn_stats", bnt8[:, q8 * 6:q8 * 6 + 6], fh[:, q8 * 512:(q8 + 1) * 512],
                reads=[b_fh], writes=[b_scr])
        vop("bn_aggr", mu3, bnt8, reads=[b_scr], writes=[b_scr])
        msq = f32(o_scr + 45, 1)
        vop("tensor_scalar", msq, mu3[:, 0:1], mu3[:, 0:1], None, op0=ALU.mult,
            reads=[b_scr], writes=[b_scr])
        vop("tensor_tensor", msq, msq, mu3[:, 1:2], op=ALU.add,
            reads=[b_scr], writes=[b_scr])
        rhoh = f32(o_scr + 46, 1)
        act(rhoh, msq, AF.Sqrt, bias=cstc(3), scale=1.0, reads=[b_scr, b_const],
            writes=[b_scr])
        vop("reciprocal", rhoh, rhoh, reads=[b_scr], writes=[b_scr])
        mxh = f32(o_scr + 47, 1)
        vop("reduce_max", mxh, fh, axis=AX.X, apply_absolute_value=True,
            reads=[b_fh], writes=[b_scr])
        mxu = f32(o_scr + 41, 1)
        vop("tensor_tensor", mxu, mxh, rhoh, op=ALU.mult, reads=[b_scr], writes=[b_scr])
        vop("tensor_scalar", mxu, mxu, 1e-5, None, op0=ALU.max, reads=[b_scr], writes=[b_scr])
        sclh = f32(o_scr + 42, 1)
        vop("reciprocal", sclh, mxu, reads=[b_scr], writes=[b_scr])
        vop("tensor_scalar", sclh, sclh, 127.0, None, op0=ALU.mult,
            reads=[b_scr], writes=[b_scr])
        vop("tensor_tensor", sclh, sclh, rhoh, op=ALU.mult, reads=[b_scr], writes=[b_scr])
        stok2 = f32(o_scr + 43, 1)
        vop("tensor_scalar", stok2, mxu, 1.0 / 127, None, op0=ALU.mult,
            reads=[b_scr], writes=[b_scr])
        SCR2 = [b_fxq1, b_fxq1fm, b_ffout]
        for hh in range(2):
            fhH = fh[:, hh * 2048:(hh + 1) * 2048]
            dF = f32(o_fge, 2048)
            fS = f32(o_fxq1, 2048)
            vop("tensor_scalar", i32v(o_fge, 2048), fhH, sclh, None, op0=ALU.mult,
                reads=[b_fh, b_scr], writes=[b_fge])
            vop("tensor_copy", fS, i32v(o_fge, 2048), reads=[b_fge], writes=SCR2)
            vop("scalar_tensor_tensor", dF, fhH, sclh, fS,
                op0=ALU.mult, op1=ALU.subtract,
                reads=[b_fh, b_scr] + SCR2, writes=[b_fge])
            vop("scalar_tensor_tensor", fS, dF, 0.5, fS, op0=ALU.is_gt,
                op1=ALU.add, reads=[b_fge], writes=SCR2)
            vop("tensor_scalar", dF, dF, -0.5, None, op0=ALU.is_lt,
                reads=[b_fge], writes=[b_fge])
            vop("tensor_tensor", fS, fS, dF, op=ALU.subtract,
                reads=[b_fge], writes=SCR2)
            vop("tensor_copy", bf16(o_fh + hh * 1024, 1024), fS,
                reads=SCR2, writes=[b_fh])
        q2bf = bf16(o_fh, 2048)
        for g in range(4):
            for j in range(8):
                fi = g * 8 + j
                transp(PS[j].bitcast(BF16)[:, 0:128],
                       q2bf[:, fi * 128:(fi + 1) * 128],
                       reads=[b_fh, b_const], writes=[b_ps[j]])
            for j in range(8):
                fi = g * 8 + j
                vop("tensor_copy", bf16(o_fge + fi * 64, 64),
                    PS[j].bitcast(BF16)[:, 0:128], reads=[b_ps[j]],
                    writes=[b_fge])
        for fi in range(32):
            lhsT = bf16(o_fge + fi * 64, 64)
            mm(PS[0][:, 0:512], lhsT, bf16(o_x1fm + fi * 512, 512)[:, 0:512],
               start=(fi == 0), stop=(fi == 31), reads=[b_fge, b_w2qT],
               writes=[b_ps[0]])
            mm(PS[1][:, 0:512], lhsT, bf16(o_x1fm + fi * 512, 512)[:, 512:1024],
               start=(fi == 0), stop=(fi == 31), reads=[b_fge, b_w2qT],
               writes=[b_ps[1]])
        sc2 = f32(o_scr + 44, 1)
        vop("tensor_tensor", sc2, stok2, m2col, op=ALU.mult,
            reads=[b_scr, b_mcol], writes=[b_scr])
        ffo = f32(o_ffout, D)
        for b in range(2):
            vop("scalar_tensor_tensor", ffo[:, b * 512:(b + 1) * 512], PS[b][:, 0:512],
                sc2, bf16(o_b2t, 512)[:, b * 512:(b + 1) * 512],
                op0=ALU.mult, op1=ALU.add, reads=[b_ps[b], b_scr, b_b2t],
                writes=[b_ffout])
        vop("bn_stats", bnt[:, 0:6], ffo[:, 0:512], reads=[b_ffout], writes=[b_scr])
        vop("bn_stats", bnt[:, 6:12], ffo[:, 512:1024], reads=[b_ffout], writes=[b_scr])
        vop("bn_aggr", mu3, bnt, reads=[b_scr], writes=[b_scr])
        act(rs_col, mu3[:, 1:2], AF.Sqrt, bias=cstc(3), scale=1.0,
            reads=[b_scr, b_const], writes=[b_scr])
        vop("reciprocal", rs_col, rs_col, reads=[b_scr], writes=[b_scr])
        nb = f32(o_scr + 37, 1)
        vop("tensor_scalar", nb, mu3[:, 0:1], rs_col, -1.0, op0=ALU.mult,
            op1=ALU.mult, reads=[b_scr], writes=[b_scr])
        act(ffo, ffo, AF.Identity, bias=nb, scale=rs_col,
            reads=[b_scr], writes=[b_ffout])
        vop("tensor_tensor", fx2, fx2, ffo, op=ALU.add,
            reads=[b_ffout], writes=[b_fx2])
        dma("sync", out_d[ch * C:(ch + 1) * C, :], fx2, reads=[b_fx2],
            writes=[b_outst], sem="out")

    st_sem, st_val = b_outst.w
    if dbg:
        d_sem, d_val = buf("dbgst").w
        st_val = max(st_val, d_val)
    P.raw_wait("gpsimd", st_sem, st_val)
    P.raw_wait("scalar", st_sem, st_val)

    block = E(nc.Block())
    P.build(sems, block)
    ctx.close()
    return nc


def make_consts(core_id):
    triu = np.triu(np.ones((C, C))).astype(ml_dtypes.bfloat16)
    maskbig = np.where(np.triu(np.ones((C, C))) > 0, 0.0, -30000.0).astype(np.float32)
    idt = np.eye(C, dtype=ml_dtypes.bfloat16)
    ones32 = np.ones((C, C), dtype=ml_dtypes.bfloat16)
    onesf = np.ones((1, C), dtype=np.float32)
    idh = np.zeros((C, H), dtype=np.float32)
    for h in range(H):
        for b in range(NCORES):
            idh[h * 8 + b, h] = 1.0
    masklt = np.zeros((H, NCORES), dtype=np.float32)
    masklt[:, :core_id] = 1.0
    cst = np.zeros((C, 8), dtype=np.float32)
    cst[:, 0] = 1.0
    cst[:, 1] = 127.0
    cst[:, 2] = 1.0 / 127.0
    cst[:, 3] = EPS
    cst[:, 4] = 0.125
    cst[:, 5] = 1.0 / 64.0
    return dict(triu=triu, maskbig=maskbig, idt=idt, ones32=ones32, onesf=onesf,
                idh=idh, masklt=masklt, cst=cst)


_NC_CACHE = {}


def kernel(L0_attn_x, L0_attn_Wq, L0_attn_bq, L0_attn_Wk, L0_attn_bk,
           L0_attn_Wv, L0_attn_bv, L0_attn_Wbeta, L0_attn_bbeta,
           L0_attn_Walpha, L0_attn_balpha, L0_ffn_w1, L0_ffn_b1,
           L0_ffn_w2, L0_ffn_b2):
    x = np.ascontiguousarray(np.asarray(L0_attn_x, dtype=np.float32)[0])
    T = x.shape[0]
    nchunk = T // (NCORES * C)
    key = nchunk
    if key not in _NC_CACHE:
        _NC_CACHE[key] = build_program(nchunk=nchunk)
    nc = _NC_CACHE[key]
    TPC = nchunk * C
    com = dict(
        wq=np.asarray(L0_attn_Wq, np.float32), bq=np.asarray(L0_attn_bq, np.float32),
        wk=np.asarray(L0_attn_Wk, np.float32), bk=np.asarray(L0_attn_bk, np.float32),
        wv=np.asarray(L0_attn_Wv, np.float32), bv=np.asarray(L0_attn_bv, np.float32),
        wb=np.asarray(L0_attn_Wbeta, np.float32), bb=np.asarray(L0_attn_bbeta, np.float32),
        wa=np.asarray(L0_attn_Walpha, np.float32), ba=np.asarray(L0_attn_balpha, np.float32),
        b1=np.asarray(L0_ffn_b1, np.float32), b2=np.asarray(L0_ffn_b2, np.float32),
    )
    w1 = np.asarray(L0_ffn_w1, np.float32)
    w2 = np.asarray(L0_ffn_w2, np.float32)
    in_maps = []
    for cid in range(NCORES):
        m = dict(com)
        m["x"] = np.ascontiguousarray(x[cid * TPC:(cid + 1) * TPC])
        m["w1s"] = np.ascontiguousarray(w1[cid * (FF // NCORES):(cid + 1) * (FF // NCORES)])
        m["w2s"] = np.ascontiguousarray(w2[cid * (D // NCORES):(cid + 1) * (D // NCORES)])
        m.update(make_consts(cid))
        in_maps.append(m)
    res = run_bass_kernel_spmd(nc, in_maps, core_ids=list(range(NCORES)))
    out = np.concatenate([np.asarray(r["out"], np.float32) for r in res.results], axis=0)
    return out.reshape(1, T, D)
